# version 13
# baseline (speedup 1.0000x reference)
from contextlib import ExitStack

import numpy as np
import ml_dtypes
import concourse.bass as bass
import concourse.mybir as mybir
from concourse.bass_utils import run_bass_kernel_spmd

F32 = mybir.dt.float32
BF16 = mybir.dt.bfloat16
AF = mybir.ActivationFunctionType
ALU = mybir.AluOpType
AX = mybir.AxisListType

D = 1024
T = 2048
ND = 8
NT = 16
NH = 16
DH = 64
NP = 8
CH = 512
NC = 4
EPS = 1e-5
ASCALE = float(DH) ** -0.5
NRING = 6


class Ctr:
    def __init__(self, nc, name):
        self.sem = nc.alloc_semaphore(name)
        self.v = 0

    def inc(self, inst, n=1):
        self.v += n
        inst.then_inc(self.sem, n)
        return inst


class Prog:
    def __init__(self, nc):
        self.nc = nc
        self.ctr = {}
        self.dg = nc.alloc_sbuf_tensor("dgap", [1, 8], mybir.dt.float32)
        self._gv = 0
        self._gs = 0

    def c(self, name):
        if name not in self.ctr:
            self.ctr[name] = Ctr(self.nc, name)
        return self.ctr[name]

    def vmark(self, inst):
        return self.c("ve").inc(inst)

    def vjoin(self, v):
        ve = self.c("ve")
        v.wait_ge(ve.sem, ve.v)

    def smark(self, inst):
        return self.c("se").inc(inst)

    def sjoin(self, s):
        se = self.c("se")
        s.wait_ge(se.sem, se.v)


def _mk_consts():
    ident = np.eye(128, dtype=np.float32)
    onesb = np.ones((128, 1), dtype=ml_dtypes.bfloat16)
    onesrow = np.ones((1, 128), dtype=np.float32)
    masks = np.zeros((128, 4 * 1024), dtype=np.float32)
    p = np.arange(128)[:, None]
    f = np.arange(CH)[None, :]
    for j in range(4):
        m = (f >= p + 128 * j)
        masks[:, j * 1024:j * 1024 + CH] = m
        masks[:, j * 1024 + CH:(j + 1) * 1024] = m
    masks = masks.astype(ml_dtypes.bfloat16)
    selh = np.zeros((16, 1024), dtype=np.float32)
    for dt in range(8):
        for pp in range(128):
            selh[2 * dt + pp // 64, dt * 128 + pp] = 1.0
    sel8 = np.zeros((8, 128), dtype=np.float32)
    for q in range(128):
        sel8[q // 16, q] = 1.0
    return dict(IDENT=ident, IDENTB=ident.astype(ml_dtypes.bfloat16),
                ONESB=onesb, ONESROW=onesrow, MASKS=masks,
                SELH=selh, SEL8=sel8)


def _ring_slots(causal):
    slots = {}
    g = 0
    ntk = (lambda c: 4 * c + 4) if causal else (lambda c: NT)
    for p in range(NP):
        for c in range(NC):
            for tk in range(ntk(c)):
                slots[(p, c, tk)] = g % NRING
                g += 1
    return slots



class LN:
    CSB = (2, 3, 6, 7)

    def __init__(self, P, bufs, scr, fused):
        self.P, self.bufs, self.fused = P, bufs, fused
        (self.XSQ, self.t1, self.s2, self.rowa, self.rowb, self.rowc) = scr
        self.lns, self.vcp = P.c("ln_sc"), P.c("ln_vc")
        self.tc, self.sr = P.c("ln_t"), P.c("ln_sr")
        self.vr, self.tb, self.vn = P.c("ln_vr"), P.c("ln_tb"), P.c("ln_vn")
        self.lns0, self.vcp0 = self.lns.v, self.vcp.v
        self.tc0, self.tb0, self.vn0 = self.tc.v, self.tb.v, self.vn.v
        self.s_order = [("mu", 0), ("m2", 0), ("mu", 1), ("m2", 1), ("ln", 0),
                        ("ex", 0), ("mu", 2), ("m2", 2), ("ln", 1), ("ex", 1),
                        ("mu", 3), ("m2", 3), ("ln", 2), ("ex", 2), ("ln", 3),
                        ("ex", 3)]
        self.srv, vv = {}, self.sr.v
        for kc in self.s_order:
            if kc[0] != "ln":
                vv += 1
                self.srv[kc] = vv
        self.vrv = {("var", ch): self.vr.v + ch + 1 for ch in range(NC)}
        self.vrv.update({("murs", ch): self.vr.v + NC + ch + 1
                         for ch in range(NC)})

    def cast_tile(self, s_, dt, src_ap):
        nb = len(self.XSQ)
        if dt >= nb:
            s_.wait_ge(self.tc.sem, self.tc0 + dt - nb + 1)
        if self.fused:
            self.lns.inc(s_.activation(self.bufs["XLN"][dt][:], src_ap[:],
                                       AF.Identity))
        self.lns.inc(s_.activation(self.XSQ[dt % nb][:], src_ap[:],
                                   AF.Square))

    def copy_tile(self, v, dt, src_ap):
        self.vcp.inc(v.tensor_copy(self.bufs["XLN"][dt][:], src_ap[:]))

    def colsums(self, t, dt):
        bufs = self.bufs
        per = 2 if self.fused else 1
        t.wait_ge(self.lns.sem, self.lns0 + per * (dt + 1))
        if not self.fused:
            t.wait_ge(self.vcp.sem, self.vcp0 + dt + 1)
        st, sp = dt == 0, dt == ND - 1
        nb = len(self.XSQ)
        for ch in range(NC):
            cs = slice(ch * CH, (ch + 1) * CH)
            t.matmul(bufs["PSB"][self.CSB[ch]][0:1, :], bufs["ONESB"][:],
                     bufs["XLN"][dt][:, cs], start=st, stop=sp)
        last = None
        for ch in range(NC):
            cs = slice(ch * CH, (ch + 1) * CH)
            last = t.matmul(bufs["PSB"][self.CSB[ch]][32:33, :],
                            bufs["ONESB"][:], self.XSQ[dt % nb][:, cs],
                            start=st, stop=sp)
        self.tc.inc(last)

    def scalar_rows(self, s_):
        P, bufs = self.P, self.bufs
        s_.wait_ge(self.tc.sem, self.tc0 + ND)
        for kind, ch in self.s_order:
            cs = slice(ch * CH, (ch + 1) * CH)
            b = bufs["PSB"][self.CSB[ch]]
            if kind == "mu":
                ins = s_.activation(self.rowa[0:1, cs], b[0:1, :],
                                    AF.Identity, scale=1.0 / D)
            elif kind == "m2":
                ins = s_.activation(self.rowb[0:1, cs], b[32:33, :],
                                    AF.Identity, scale=1.0 / D)
            elif kind == "ln":
                s_.wait_ge(self.vr.sem, self.vrv[("var", ch)])
                P.smark(s_.activation(self.rowb[0:1, cs],
                                      self.rowc[0:1, cs], AF.Ln,
                                      bias=bufs["EPSC"][0:1, 0:1]))
                continue
            else:
                P.sjoin(s_)
                ins = s_.activation(self.rowb[0:1, cs], self.rowb[0:1, cs],
                                    AF.Exp, scale=-0.5)
            self.sr.inc(ins)
            assert self.sr.v == self.srv[(kind, ch)]

    def vector_rows_norm(self, v):
        P, bufs, XLN = self.P, self.bufs, self.bufs["XLN"]
        for ch in range(NC):
            cs = slice(ch * CH, (ch + 1) * CH)
            v.wait_ge(self.sr.sem, self.srv[("m2", ch)])
            P.vmark(v.tensor_mul(self.rowc[0:1, cs], self.rowa[0:1, cs],
                                 self.rowa[0:1, cs]))
            P.vjoin(v)
            self.vr.inc(v.tensor_sub(self.rowc[0:1, cs], self.rowb[0:1, cs],
                                     self.rowc[0:1, cs]))
            assert self.vr.v == self.vrv[("var", ch)]
        for ch in range(NC):
            cs = slice(ch * CH, (ch + 1) * CH)
            v.wait_ge(self.sr.sem, self.srv[("ex", ch)])
            self.vr.inc(v.tensor_mul(self.rowa[0:1, cs], self.rowa[0:1, cs],
                                     self.rowb[0:1, cs]))
            assert self.vr.v == self.vrv[("murs", ch)]
            v.wait_ge(self.tb.sem, self.tb0 + ch + 1)
            if ch > 0:
                P.vjoin(v)
            last = None
            for dt in range(ND):
                sl = (self.t1 if dt < 4 else self.s2)[
                    :, (dt % 4) * CH:(dt % 4 + 1) * CH]
                last = v.tensor_tensor(sl, XLN[dt][:, cs],
                                       bufs["PSB"][ch][:], ALU.mult)
            P.vmark(last)
            P.vjoin(v)
            for dt in range(ND):
                sl = (self.t1 if dt < 4 else self.s2)[
                    :, (dt % 4) * CH:(dt % 4 + 1) * CH]
                last = v.tensor_tensor(XLN[dt][:, cs], sl,
                                       bufs["PSB"][4 + ch][:], ALU.subtract)
            P.vmark(last)
            self.vn.inc(v.engine_nop())

    def tensor_bcasts(self, t, pre=None):
        bufs = self.bufs
        for ch in range(NC):
            cs = slice(ch * CH, (ch + 1) * CH)
            if ch == 0 and pre is not None:
                pre(t)
            t.wait_ge(self.sr.sem, self.srv[("ex", ch)])
            t.wait_ge(self.vr.sem, self.vrv[("murs", ch)])
            t.matmul(bufs["PSB"][ch][:], bufs["ONESROW"][:],
                     self.rowb[0:1, cs], start=True, stop=True)
            self.tb.inc(t.matmul(bufs["PSB"][4 + ch][:], bufs["ONESROW"][:],
                                 self.rowa[0:1, cs], start=True, stop=True))


def emit_ln(P, bufs, src, tag):
    nc = P.nc
    with (nc.sbuf_tensor(f"lnst{tag}0", [128, T], F32) as xst0,
          nc.sbuf_tensor(f"lnst{tag}1", [128, T], F32) as xst1,
          nc.sbuf_tensor(f"lnsq{tag}0", [128, T], BF16) as xsq0,
          nc.sbuf_tensor(f"lnsq{tag}1", [128, T], BF16) as xsq1,
          nc.sbuf_tensor(f"lnt1{tag}", [128, T], F32) as t1,
          nc.sbuf_tensor(f"lnra{tag}", [1, T], F32) as rowa,
          nc.sbuf_tensor(f"lnrb{tag}", [1, T], F32) as rowb,
          nc.sbuf_tensor(f"lnrc{tag}", [1, T], F32) as rowc,
          nc.Block(f"ln_{tag}") as blk):
        XST = [xst0, xst1]
        ln = LN(P, bufs, ([xsq0, xsq1], t1, xst0, rowa, rowb, rowc),
                fused=False)
        lq = [P.c("lnq0"), P.c("lnq1")]
        lq0 = [lq[0].v, lq[1].v]

        @blk.sync
        def _(sync):
            for dt in range(ND):
                if dt >= 2:
                    sync.wait_ge(ln.lns.sem, ln.lns0 + dt - 1)
                    sync.wait_ge(ln.vcp.sem, ln.vcp0 + dt - 1)
                lq[dt % 2].inc(
                    sync.dma_start(XST[dt % 2][:],
                                   src[dt * 128:(dt + 1) * 128, :]), 16)

        @blk.scalar
        def _(s_):
            for dt in range(ND):
                s_.wait_ge(lq[dt % 2].sem, lq0[dt % 2] + 16 * (dt // 2 + 1))
                ln.cast_tile(s_, dt, XST[dt % 2])
            ln.scalar_rows(s_)

        @blk.vector
        def _(v):
            for dt in range(ND):
                v.wait_ge(lq[dt % 2].sem, lq0[dt % 2] + 16 * (dt // 2 + 1))
                ln.copy_tile(v, dt, XST[dt % 2])
            ln.vector_rows_norm(v)

        @blk.tensor
        def _(t):
            for dt in range(ND):
                ln.colsums(t, dt)
            ln.tensor_bcasts(t)


def emit_gate(P, bufs, tag):
    nc = P.nc
    XLN, PSB = bufs["XLN"], bufs["PSB"]
    tg, vg, sg, gq, sq = P.c("t_g"), P.c("v_g"), P.c("s_g"), P.c("gq"), P.c("sq")
    tg0, vg0, sg0, gq0, sq0 = tg.v, vg.v, sg.v, gq.v, sq.v
    R = bufs["GR"]

    with (nc.sbuf_tensor(f"gvtq{tag}", [128, ND * 128], BF16) as vtq,
          nc.sbuf_tensor(f"gvtk{tag}", [128, ND * 128], BF16) as vtk,
          nc.sbuf_tensor(f"gvtv{tag}", [128, ND * 128], BF16) as vtv,
          nc.sbuf_tensor(f"gvto{tag}", [128, ND * 128], BF16) as vto,
          nc.sbuf_tensor(f"gw1{tag}", [128, ND * 256], F32) as gw1,
          ExitStack() as ges,
          nc.Block(f"gate_{tag}") as blk):
        VT = dict(q=vtq, k=vtk, v=vtv, o=vto)
        bufs["SROW"] = {x: ges.enter_context(
            nc.sbuf_tensor(f"srow{tag}_{x}", [1, 128], F32)) for x in "qkvo"}
        bufs["WROW"] = {x: ges.enter_context(
            nc.sbuf_tensor(f"wrow{tag}_{x}", [1, 128], F32)) for x in "qkvo"}

        @blk.gpsimd
        def _(g):
            for x in "qkvo":
                g.dma_start(
                    VT[x][:].rearrange("p (t e) -> p t e", t=ND),
                    bufs[f"fV{x}"][:].rearrange("(t p) e -> p t e", p=128)
                ).then_inc(gq.sem, 16)
                g.dma_start(bufs["UP"][x][:], bufs[f"fU{x}"][:]
                            ).then_inc(gq.sem, 16)
            gq.v += 8 * 16

        @blk.sync
        def _(sync):
            sync.dma_start(gw1[:].rearrange("p (t g) -> p t g", t=ND),
                           bufs["gw1T"][:].rearrange("(t p) g -> p t g", p=128)
                           ).then_inc(sq.sem, 16)
            sync.dma_start(bufs["GW2"][:].rearrange("p (t e) -> p t e", t=2),
                           bufs["gw2T"][:].rearrange("(t p) e -> p t e", p=128)
                           ).then_inc(sq.sem, 16)
            sync.dma_start(bufs["GB1"][:], bufs["gb1c"][:]).then_inc(sq.sem, 16)
            sync.dma_start(bufs["GB2"][:], bufs["gb2r"][:]).then_inc(sq.sem, 16)
            for x in "qkvo":
                sync.dma_start(bufs["SROW"][x][:], bufs[f"fS{x}"][:]
                               ).then_inc(sq.sem, 16)
            sq.v += 8 * 16

        @blk.vector
        def _(v):
            for dt in range(ND):
                vg.inc(v.reduce_sum(bufs["POOL"][:, dt:dt + 1], XLN[dt][:], AX.X))

        @blk.tensor
        def _(t):
            t.wait_ge(vg.sem, vg0 + ND)
            t.wait_ge(sq.sem, sq0 + 8 * 16)
            for i in range(2):
                last = None
                for dt in range(ND):
                    last = t.matmul(PSB[0][:, i:i + 1],
                                    gw1[:, dt * 256 + i * 128:
                                        dt * 256 + (i + 1) * 128],
                                    bufs["POOL"][:, dt:dt + 1],
                                    start=(dt == 0), stop=(dt == ND - 1))
                tg.inc(last)

        @blk.scalar
        def _(s):
            s.wait_ge(sq.sem, sq0 + 8 * 16)
            s.wait_ge(tg.sem, tg0 + 2)
            for i in range(2):
                sg.inc(s.activation(bufs["HSB"][:, i:i + 1], PSB[0][:, i:i + 1],
                                    AF.Relu, bias=bufs["GB1"][:, i:i + 1],
                                    scale=1.0 / T))

        @blk.tensor
        def _(t):
            t.wait_ge(sg.sem, sg0 + 2)
            last = None
            for i in range(2):
                last = t.matmul(PSB[1][0:1, 0:8], bufs["HSB"][:, i:i + 1],
                                bufs["GW2"][:, i * 8:(i + 1) * 8],
                                start=(i == 0), stop=(i == 1))
            tg.inc(last)

        @blk.vector
        def _(v):
            v.wait_ge(tg.sem, tg0 + 3)
            v.wait_ge(sq.sem, sq0 + 8 * 16)
            LG = R[0:1, 0:8]
            P.vmark(v.tensor_add(LG, PSB[1][0:1, 0:8], bufs["GB2"][0:1, :]))
            P.vjoin(v)
            P.vmark(v.reduce_max(R[0:1, 8:9], LG, AX.X))
            P.vjoin(v)
            P.vmark(v.tensor_scalar(R[0:1, 16:24], LG, R[0:1, 8:9], None,
                                    ALU.subtract))
            P.vjoin(v)
            P.vmark(v.tensor_scalar(R[0:1, 24:32], R[0:1, 16:24], 0.0, None,
                                    ALU.is_ge))
            P.vjoin(v)
            P.vmark(v.scalar_tensor_tensor(R[0:1, 32:40], R[0:1, 24:32], -1e30,
                                           LG, ALU.mult, ALU.add))
            P.vjoin(v)
            P.vmark(v.reduce_max(R[0:1, 9:10], R[0:1, 32:40], AX.X))
            P.vjoin(v)
            vg.inc(v.tensor_scalar(R[0:1, 40:48], LG, R[0:1, 9:10], None,
                                   ALU.is_ge))

        @blk.scalar
        def _(s):
            s.wait_ge(vg.sem, vg0 + 9)
            sg.inc(s.activation(R[0:1, 48:56], R[0:1, 16:24], AF.Exp))

        @blk.vector
        def _(v):
            v.wait_ge(sg.sem, sg0 + 3)
            P.vmark(v.tensor_mul(R[0:1, 56:64], R[0:1, 48:56], R[0:1, 40:48]))
            P.vjoin(v)
            P.vmark(v.reduce_sum(R[0:1, 10:11], R[0:1, 56:64], AX.X))
            P.vjoin(v)
            P.vmark(v.reciprocal(R[0:1, 11:12], R[0:1, 10:11]))
            P.vjoin(v)
            vg.inc(v.tensor_scalar(R[0:1, 0:8], R[0:1, 56:64], R[0:1, 11:12],
                                   None, ALU.mult))

        @blk.tensor
        def _(t):
            t.wait_ge(vg.sem, vg0 + 10)
            tg.inc(t.transpose(PSB[2][0:8, 0:1], R[0:1, 0:8],
                               bufs["IDENT"][0:1, 0:1]))

        @blk.vector
        def _(v):
            v.wait_ge(tg.sem, tg0 + 4)
            vg.inc(v.tensor_copy(bufs["GCOL"][:], PSB[2][0:8, 0:1]))

        @blk.tensor
        def _(t):
            t.wait_ge(vg.sem, vg0 + 11)
            tg.inc(t.matmul(PSB[2][0:1, 0:128], bufs["GCOL"][:], bufs["SEL8"][:],
                            start=True, stop=True))

        @blk.vector
        def _(v):
            v.wait_ge(tg.sem, tg0 + 5)
            v.wait_ge(sq.sem, sq0 + 8 * 16)
            for x in "qkvo":
                vg.inc(v.tensor_mul(bufs["WROW"][x][:], PSB[2][0:1, 0:128],
                                    bufs["SROW"][x][:]))

        @blk.tensor
        def _(t):
            for i, x in enumerate("qkvo"):
                t.wait_ge(vg.sem, vg0 + 12 + i)
                if i >= 2:
                    t.wait_ge(vg.sem, vg0 + 16 + (i - 2))
                tg.inc(t.matmul(PSB[3 + (i % 2)][:, 0:128], bufs["ONESROW"][:],
                                bufs["WROW"][x][:], start=True, stop=True))

        @blk.vector
        def _(v):
            v.wait_ge(gq.sem, gq0 + 8 * 16)
            for i, x in enumerate("qkvo"):
                v.wait_ge(tg.sem, tg0 + 6 + i)
                ins = None
                for dt in range(ND):
                    ins = v.tensor_tensor(
                        bufs["VW"][x][:, dt * 128:(dt + 1) * 128],
                        VT[x][:, dt * 128:(dt + 1) * 128],
                        PSB[3 + (i % 2)][:, 0:128], ALU.mult)
                vg.inc(ins)

        @blk.tensor
        def _(t):
            n = 0
            for i, x in enumerate("qkv"):
                t.wait_ge(vg.sem, vg0 + 16 + i)
                for ch in range(NC):
                    n += 1
                    if n > 2:
                        t.wait_ge(vg.sem, vg0 + 19 + n - 2)
                    bank = 5 + (ch % 2)
                    last = None
                    for dt in range(ND):
                        last = t.matmul(PSB[bank][:],
                                        bufs["VW"][x][:, dt * 128:(dt + 1) * 128],
                                        XLN[dt][:, ch * CH:(ch + 1) * CH],
                                        start=(dt == 0), stop=(dt == ND - 1))
                    tg.inc(last)

        @blk.vector
        def _(v):
            n = 0
            for i, x in enumerate("qkv"):
                for ch in range(NC):
                    n += 1
                    v.wait_ge(tg.sem, tg0 + 9 + n)
                    bank = 5 + (ch % 2)
                    vg.inc(v.tensor_copy(bufs["XVS"][x][:, ch * CH:(ch + 1) * CH],
                                         PSB[bank][:]))


def emit_vprime(P, bufs, tag, wname, ffn):
    nc = P.nc
    XLN, VP, PSB = bufs["XLN"], bufs["VP"], bufs["PSB"]
    gq, tvp, vvp, sq = P.c("gq"), P.c("t_vp"), P.c("v_vp"), P.c("sq")
    gq0, tvp0, vvp0, sq0 = gq.v, tvp.v, vvp.v, sq.v

    with (nc.sbuf_tensor(f"wvb{tag}", [128, ND * 512], BF16) as WVB,
          nc.sbuf_tensor(f"vbb{tag}", [128, D], F32) as VBB,
          nc.Block(f"vp_{tag}") as blk):
        vbrow = bufs["DENS"][0:1, 0:D]

        @blk.gpsimd
        def _(g):
            for j in range(2):
                if j == 1:
                    g.wait_ge(tvp.sem, tvp0 + (1 if ffn else 0) + NT)
                g.dma_start(
                    WVB[:].rearrange("p (t o) -> p t o", t=ND),
                    bufs[wname][:, j * 512:(j + 1) * 512].rearrange(
                        "(t p) o -> p t o", p=128)
                ).then_inc(gq.sem, 16)
            gq.v += 32

        if ffn:
            @blk.sync
            def _(sync):
                sync.dma_start(vbrow, bufs["fbv_r"][:]
                               ).then_inc(sq.sem, 16)
                sq.v += 16

            @blk.tensor
            def _(t):
                t.wait_ge(sq.sem, sq0 + 16)
                t.matmul(PSB[6][:], bufs["ONESROW"][:],
                         vbrow[0:1, 0:512], start=True, stop=True)
                tvp.inc(t.matmul(PSB[7][:], bufs["ONESROW"][:],
                                 vbrow[0:1, 512:1024],
                                 start=True, stop=True))

        @blk.vector
        def _(v):
            if ffn:
                v.wait_ge(tvp.sem, tvp0 + 1)
                v.tensor_copy(VBB[:, 0:512], PSB[6][:])
                v.tensor_copy(VBB[:, 512:1024], PSB[7][:])
            for m in range(NT):
                ins = v.memset(
                    VP[m][:].rearrange("p (h c) -> p h c", c=65)[:, :, 64:65],
                    1.0)
            vvp.inc(ins)

        tbase = tvp.v

        @blk.tensor
        def _(t):
            t.wait_ge(vvp.sem, vvp0 + 1)
            n = 0
            for j in range(2):
                t.wait_ge(gq.sem, gq0 + 16 * (j + 1))
                for m in range(NT):
                    n += 1
                    if n > 2:
                        t.wait_ge(vvp.sem, vvp0 + 1 + n - 2)
                    bank = (n - 1) % 2
                    last = None
                    for dt in range(ND):
                        last = t.matmul(
                            PSB[bank][:],
                            XLN[dt][:, m * 128:(m + 1) * 128],
                            WVB[:, dt * 512:(dt + 1) * 512],
                            start=(dt == 0), stop=((not ffn) and dt == ND - 1))
                    if ffn:
                        last = t.matmul(PSB[bank][:],
                                        bufs["XVS"]["v"][:, m * 128:(m + 1) * 128],
                                        bufs["UP"]["v"][:, j * 512:(j + 1) * 512],
                                        start=False, stop=True)
                    tvp.inc(last)

        @blk.vector
        def _(v):
            n = 0
            for j in range(2):
                for m in range(NT):
                    n += 1
                    v.wait_ge(tvp.sem, tbase + n)
                    bank = (n - 1) % 2
                    dst = VP[m][:].rearrange("p (h c) -> p h c", c=65)[
                        :, j * 8:(j + 1) * 8, 0:64]
                    src3 = PSB[bank][:].rearrange("p (h c) -> p h c", c=64)
                    if ffn:
                        vbb3 = VBB[:, j * 512:(j + 1) * 512].rearrange(
                            "p (h c) -> p h c", c=64)
                        vvp.inc(v.tensor_add(dst, src3, vbb3))
                    else:
                        vvp.inc(v.tensor_copy(dst, src3))


def emit_attention(P, bufs, tag, wq_name, wk_name, causal, ffn, ring_slot):
    nc = P.nc
    XLN, VP, PSB, PSD = bufs["XLN"], bufs["VP"], bufs["PSB"], bufs["PSD"]
    RING, QP, KP, QSL, KSL = (bufs["RING"], bufs["QP"], bufs["KP"],
                              bufs["QSL"], bufs["KSL"])
    SQQ, SQK, DENS = bufs["SQQ"], bufs["SQK"], bufs["DENS"]
    ACT2, NEGC, R = bufs["ACT2"], bufs["NEGC"], bufs["GR"]
    DEN = PSB[6]
    softmax_scale = ASCALE if causal else 1.0
    ntk = (lambda c: 4 * c + 4) if causal else (lambda c: NT)

    tqk, vev, snc = P.c("t_qk"), P.c("v_ev"), P.c("s_nc")
    aq = [P.c("aq0"), P.c("aq1")]
    ts, ex, vm, pv = P.c("t_s"), P.c("s_ex"), P.c("v_m"), P.c("t_pv")
    vce, td = P.c("v_ce"), P.c("t_d")
    B = dict(tqk=tqk.v, vev=vev.v, snc=snc.v, ts=ts.v, ex=ex.v,
             vm=vm.v, pv=pv.v, vce=vce.v, td=td.v)
    aq0 = [aq[0].v, aq[1].v]

    ts_done, ex_done, vm_done = {}, {}, {}
    pv_done, exp_ring_wait, scores_bank_wait = {}, {}, {}
    last_scores = {}
    tsv, exv, vmv, pvv = B["ts"], B["ex"], B["vm"], B["pv"]
    slot_last_pv = {}
    hist = []
    for p in range(NP):
        for c in range(NC):
            n = ntk(c)
            for tk in range(n):
                hist.append((p, c, tk))
                g = len(hist) - 1
                scores_bank_wait[(p, c, tk)] = (
                    ex_done[hist[g - 2]] if g >= 2 else B["ex"])
                tsv += 1
                ts_done[(p, c, tk)] = tsv
                slot = ring_slot[(p, c, tk)]
                exp_ring_wait[(p, c, tk)] = slot_last_pv.get(slot, B["pv"])
                exv += 1
                ex_done[(p, c, tk)] = exv
                vmv += 1
                vm_done[(p, c, tk)] = vmv
                if (tk + 1) % 4 == 0 or tk == n - 1:
                    pvv += 1
                slot_last_pv[slot] = pvv
            pv_done[(p, c)] = pvv
        last_scores[p] = tsv

    with nc.Block(f"attn_{tag}") as blk:
        @blk.gpsimd
        def _(g):
            for p in range(NP):
                if p >= 2:
                    g.wait_ge(tqk.sem, B["tqk"] + 11 * (p - 1))
                for wn, bb in ((wq_name, QSL), (wk_name, KSL)):
                    aq[p % 2].inc(g.dma_start(
                        bb[p % 2][:].rearrange("p (t c) -> p t c", t=ND),
                        bufs[wn][:, p * 128:(p + 1) * 128].rearrange(
                            "(t p) c -> p t c", p=128)), 16)

        qk_state = {}

        def t_qk_group(t, p, g):
            if g == 0:
                t.wait_ge(aq[p % 2].sem, aq0[p % 2] + 32 * (p // 2 + 1))
            bank = 7
            if (p, g) != (0, 0):
                t.wait_ge(vev.sem, qk_state["free"])
            if g < 8:
                proj = "q" if g < 4 else "k"
                sl = QSL if g < 4 else KSL
                ch = g % 4
                last = None
                for dt in range(ND):
                    last = t.matmul(
                        PSB[bank][:],
                        sl[p % 2][:, dt * 128:(dt + 1) * 128],
                        XLN[dt][:, ch * CH:(ch + 1) * CH],
                        start=(dt == 0), stop=((not ffn) and dt == ND - 1))
                if ffn:
                    last = t.matmul(
                        PSB[bank][:],
                        bufs["UP"][proj][:, p * 128:(p + 1) * 128],
                        bufs["XVS"][proj][:, ch * CH:(ch + 1) * CH],
                        start=False, stop=True)
            elif g < 10:
                sq_i = SQQ if g == 8 else SQK
                t.wait_ge(vev.sem, B["vev"] + 13 * p + 9 + (g - 8))
                last = None
                for ch in range(NC):
                    last = t.matmul(PSB[bank][0:1, :], bufs["ONESB"][:],
                                    sq_i[:, ch * CH:(ch + 1) * CH],
                                    start=(ch == 0), stop=(ch == NC - 1))
            else:
                t.wait_ge(snc.sem, B["snc"] + p + 1)
                gb = 16 + 24 * (p % 2)
                last = t.matmul(PSB[bank][:, 0:1], bufs["ONESROW"][:],
                                R[0:1, gb + 3:gb + 4], start=True, stop=True)
            tqk.inc(last)
            consumed = {8: 11, 9: 12, 10: 13}.get(g, g + 1)
            qk_state["free"] = B["vev"] + 13 * p + consumed

        cad = 3 if causal else 4

        @blk.tensor
        def _(t):
            def emit_colify(p, c):
                t.wait_ge(vce.sem, B["vce"] + p * NC + c + 1)
                half = (p * NC + c) % 2
                last = None
                for h01 in range(2):
                    h = 2 * p + h01
                    for sub in range(4):
                        col = (h * NC + c) * 4 + sub
                        last = t.matmul(
                            DEN[:, col:col + 1],
                            DENS[0:1, half * 2 * CH + h01 * CH + sub * 128:
                                 half * 2 * CH + h01 * CH + (sub + 1) * 128],
                            bufs["IDENT"][0:1, 0:1], start=True, stop=True)
                td.inc(last)

            for g in range(11):
                t_qk_group(t, 0, g)
            col_pend = []
            TS_MAP = {0: 0, 1: 1, 2: 2, 3: 3, 4: 4, 5: 5, 6: 6, 7: 7,
                      8: 8, 11: 9, 13: 10}
            for p in range(NP):
                qkq = list(range(11)) if p + 1 < NP else []
                ci = [0]
                for c in range(NC):
                    n = ntk(c)
                    hA, hB = 2 * p, 2 * p + 1

                    def emit_pv(tkpv, first):
                        t.wait_ge(ex.sem, ex_done[(p, c, tkpv)])
                        if causal:
                            t.wait_ge(vm.sem, vm_done[(p, c, tkpv)])
                        if first and (p, c) != (0, 0):
                            t.wait_ge(vce.sem, B["vce"] + p * NC + c)
                        slot = ring_slot[(p, c, tkpv)]
                        t.matmul(PSB[4][0:65, :],
                                 VP[tkpv][:, hA * 65:hA * 65 + 65],
                                 RING[slot][:, 0:512], start=(tkpv == 0),
                                 stop=(tkpv == n - 1))
                        last = t.matmul(PSB[5][0:65, :],
                                        VP[tkpv][:, hB * 65:hB * 65 + 65],
                                        RING[slot][:, 512:1024],
                                        start=(tkpv == 0), stop=(tkpv == n - 1))
                        if (tkpv + 1) % 4 == 0 or tkpv == n - 1:
                            pv.inc(last)

                    pend = []
                    for tk in range(n):
                        t.wait_ge(ex.sem, scores_bank_wait[(p, c, tk)])
                        ks = slice(tk * 128, (tk + 1) * 128)
                        cs = slice(c * CH, (c + 1) * CH)
                        pd = PSD[(ts_done[(p, c, tk)] - 1 - B["ts"]) % 2]
                        t.matmul(pd[:, 0:512], KP[p % 2][0:64, ks],
                                 QP[p % 2][0:64, cs], start=True,
                                 stop=True, tile_position=(0, 0))
                        ts.inc(t.matmul(pd[:, 512:1024], KP[p % 2][64:128, ks],
                                        QP[p % 2][64:128, cs], start=True,
                                        stop=True, tile_position=(64, 0)))
                        pend.append(tk)
                        if len(pend) > 2:
                            emit_pv(pend.pop(0), first=(tk == 2))
                        if tk == 0 and col_pend:
                            emit_colify(*col_pend.pop(0))
                        if tk % cad == cad - 1 and p + 1 < NP:
                            g = TS_MAP.get(ci[0])
                            ci[0] += 1
                            if g is not None and g in qkq:
                                qkq.remove(g)
                                t_qk_group(t, p + 1, g)
                    for tkpv in pend:
                        emit_pv(tkpv, first=(tkpv == 0))
                    if c == NC - 1:
                        while qkq:
                            t_qk_group(t, p + 1, qkq.pop(0))
                    col_pend.append((p, c))
            while col_pend:
                emit_colify(*col_pend.pop(0))

        @blk.scalar
        def _(s):
            def negc_math(pp):
                gb = 16 + 24 * (pp % 2)
                s.wait_ge(vev.sem, B["vev"] + 13 * pp + 12)
                if pp >= 2:
                    s.wait_ge(tqk.sem, B["tqk"] + 11 * (pp - 2) + 11)
                P.smark(s.activation(R[0:1, gb + 1:gb + 2], R[0:1, gb:gb + 1],
                                     AF.Ln))
                P.sjoin(s)
                P.smark(s.activation(R[0:1, gb + 2:gb + 3],
                                     R[0:1, gb + 1:gb + 2], AF.Exp, scale=0.5))
                P.sjoin(s)
                snc.inc(s.mul(R[0:1, gb + 3:gb + 4], R[0:1, gb + 2:gb + 3],
                              -softmax_scale))

            negc_math(0)
            for p in range(NP):
                si = [0]
                done_negc = [p + 1 >= NP]
                s.wait_ge(vev.sem, B["vev"] + 13 * (p + 1))
                for c in range(NC):
                    n = ntk(c)
                    for tk in range(n):
                        s.wait_ge(ts.sem, ts_done[(p, c, tk)])
                        if exp_ring_wait[(p, c, tk)] > B["pv"]:
                            s.wait_ge(pv.sem, exp_ring_wait[(p, c, tk)])
                        slot = ring_slot[(p, c, tk)]
                        pd = PSD[(ts_done[(p, c, tk)] - 1 - B["ts"]) % 2]
                        ex.inc(s.activation(RING[slot][:], pd[:], AF.Exp,
                                            bias=NEGC[:, p:p + 1],
                                            scale=softmax_scale))
                        if tk % cad == cad - 1:
                            if si[0] == 12 and not done_negc[0]:
                                negc_math(p + 1)
                                done_negc[0] = True
                            si[0] += 1
                if not done_negc[0]:
                    negc_math(p + 1)

        ev_state = {}

        def v_evgroup(v, p, g):
            gbp = 16 + 24 * (p % 2)
            if g < 8:
                v.wait_ge(tqk.sem, B["tqk"] + 11 * p + g + 1)
                if p >= 2 and g == 0:
                    v.wait_ge(ts.sem, last_scores[p - 2])
                proj = "q" if g < 4 else "k"
                ch = g % 4
                dstb = QP if proj == "q" else KP
                cs = slice(ch * CH, (ch + 1) * CH)
                if ffn and proj == "q":
                    ins = v.tensor_scalar(dstb[p % 2][:, cs], PSB[7][:],
                                          bufs["FBQ"][:, p:p + 1], ASCALE,
                                          ALU.add, ALU.mult)
                elif ffn:
                    ins = v.tensor_scalar(dstb[p % 2][:, cs], PSB[7][:],
                                          bufs["FBK"][:, p:p + 1], None,
                                          ALU.add)
                else:
                    ins = v.tensor_copy(dstb[p % 2][:, cs], PSB[7][:])
                vev.inc(ins)
            elif g == 8:
                if p >= 1:
                    v.wait_ge(tqk.sem, B["tqk"] + 11 * (p - 1) + 10)
                v.wait_ge(vev.sem, vev.v)
                vev.inc(v.tensor_mul(SQQ[:], QP[p % 2][:], QP[p % 2][:]))
            elif g == 9:
                v.wait_ge(tqk.sem, B["tqk"] + 11 * p + 9)
                vev.inc(v.tensor_mul(SQK[:], KP[p % 2][:], KP[p % 2][:]))
            elif g == 10:
                v.wait_ge(tqk.sem, B["tqk"] + 11 * p + 9)
                if p >= 2:
                    v.wait_ge(snc.sem, B["snc"] + p - 1)
                vev.inc(v.reduce_max(R[0:1, gbp + 4:gbp + 5], PSB[7][0:1, :],
                                     AX.X))
            elif g == 11:
                v.wait_ge(tqk.sem, B["tqk"] + 11 * p + 10)
                P.vmark(v.reduce_max(R[0:1, gbp + 5:gbp + 6], PSB[7][0:1, :],
                                     AX.X))
                P.vjoin(v)
                vev.inc(v.tensor_mul(R[0:1, gbp:gbp + 1],
                                     R[0:1, gbp + 4:gbp + 5],
                                     R[0:1, gbp + 5:gbp + 6]))
            elif g == 12:
                v.wait_ge(tqk.sem, B["tqk"] + 11 * p + 11)
                vev.inc(v.tensor_copy(NEGC[:, p:p + 1], PSB[7][:, 0:1]))

        @blk.vector
        def _(v):
            for g in range(13):
                v_evgroup(v, 0, g)
            VS_MAP = {0: 0, 1: 1, 2: 2, 3: 3, 4: 4, 5: 5, 6: 6, 7: 7,
                      8: 8, 9: 9, 10: 10, 11: 11, 14: 12}
            for p in range(NP):
                evq = list(range(13)) if p + 1 < NP else []
                vi = [0]
                for c in range(NC):
                    n = ntk(c)
                    hA, hB = 2 * p, 2 * p + 1
                    for tk in range(n):
                        if causal:
                            v.wait_ge(ex.sem, ex_done[(p, c, tk)])
                            j = tk - 4 * c
                            slot = ring_slot[(p, c, tk)]
                            if 0 <= j <= 3:
                                ms = bufs["MASKS"][:, j * 1024:(j + 1) * 1024]
                                vm.inc(v.tensor_mul(RING[slot][:], RING[slot][:],
                                                    ms))
                            else:
                                vm.inc(v.engine_nop())
                        if tk % cad == cad - 1 and p + 1 < NP:
                            g = VS_MAP.get(vi[0])
                            vi[0] += 1
                            if g is not None and g in evq:
                                evq.remove(g)
                                v_evgroup(v, p + 1, g)
                    v.wait_ge(pv.sem, pv_done[(p, c)])
                    cc = p * NC + c
                    if cc >= 2:
                        v.wait_ge(td.sem, B["td"] + cc - 1)
                    half = cc % 2
                    v.tensor_copy(DENS[0:1, half * 2 * CH:half * 2 * CH + CH],
                                  PSB[4][64:65, :])
                    v.tensor_copy(DENS[0:1, half * 2 * CH + CH:
                                       half * 2 * CH + 2 * CH],
                                  PSB[5][64:65, :])
                    v.tensor_copy(ACT2[hA // 2][(hA % 2) * 64:(hA % 2) * 64 + 64,
                                                c * CH:(c + 1) * CH],
                                  PSB[4][0:64, :])
                    vce.inc(v.tensor_copy(
                        ACT2[hB // 2][(hB % 2) * 64:(hB % 2) * 64 + 64,
                                      c * CH:(c + 1) * CH], PSB[5][0:64, :]))
                    if c == NC - 1:
                        while evq:
                            v_evgroup(v, p + 1, evq.pop(0))



def emit_ctxnorm(P, bufs, tag):
    nc = P.nc
    PSB, ACT2 = bufs["PSB"], bufs["ACT2"]
    tn, vn = P.c("t_nb"), P.c("v_nm")
    tn0, vn0 = tn.v, vn.v

    with nc.Block(f"cn_{tag}") as blk:
        @blk.vector
        def _(v):
            vn.inc(v.reciprocal(bufs["RECIP"][:], PSB[6][:, 0:256]))

        @blk.tensor
        def _(t):
            t.wait_ge(vn.sem, vn0 + 1)
            i = 0
            for h in range(NH):
                for c in range(NC):
                    bank = i % 2
                    if i >= 2:
                        t.wait_ge(vn.sem, vn0 + 1 + i - 1)
                    last = None
                    for sub in range(4):
                        col = (h * NC + c) * 4 + sub
                        last = t.matmul(
                            PSB[bank][0:64, sub * 128:(sub + 1) * 128],
                            bufs["RECIP"][:, col:col + 1].broadcast_to([128, 64]),
                            bufs["IDENT"][:], start=True, stop=True)
                    tn.inc(last)
                    i += 1

        @blk.vector
        def _(v):
            i = 0
            for h in range(NH):
                for c in range(NC):
                    v.wait_ge(tn.sem, tn0 + i + 1)
                    bank = i % 2
                    sl = ACT2[h // 2][(h % 2) * 64:(h % 2) * 64 + 64,
                                      c * CH:(c + 1) * CH]
                    vn.inc(v.tensor_tensor(sl, sl, PSB[bank][0:64, :], ALU.mult))
                    i += 1


def emit_oproj(P, bufs, tag, wo_name, src_res_name, dst_name, ffn,
               do_cn=False, ln_src=None, ln_tag=None):
    nc = P.nc
    ACT2, PSB = bufs["ACT2"], bufs["PSB"]
    src_res = bufs[src_res_name]
    dst = bufs[dst_name]
    to, vo, gq, gqb = P.c("t_o"), P.c("v_o"), P.c("gq"), P.c("gqb")
    oq = [P.c("oq0"), P.c("oq1")]
    so = [P.c("sto0"), P.c("sto1")]
    to0, vo0, gq0, gqb0 = to.v, vo.v, gq.v, gqb.v
    oq0 = [oq[0].v, oq[1].v]
    so0 = [so[0].v, so[1].v]

    with (nc.sbuf_tensor(f"wob{tag}", [128, ND * D], BF16)) as WOB, (
          nc.sbuf_tensor(f"xo{tag}0", [128, T], F32)) as xo0, (
          nc.sbuf_tensor(f"xo{tag}1", [128, T], F32)) as xo1, (
          nc.sbuf_tensor(f"xn{tag}0", [128, T], F32)) as xn0, (
          nc.sbuf_tensor(f"xn{tag}1", [128, T], F32)) as xn1, (
          ExitStack()) as les, (
          nc.Block(f"o_{tag}")) as blk:
        XOLD = [xo0, xo1]
        XNEW = [xn0, xn1]
        ln = None
        if ln_src is not None:
            lt = ln_tag
            ln_scr = (
                [les.enter_context(nc.sbuf_tensor(f"lnsq{lt}{i}", [128, T],
                                                  BF16)) for i in range(4)],
                les.enter_context(nc.sbuf_tensor(f"lnt1{lt}", [128, T], F32)),
                les.enter_context(nc.sbuf_tensor(f"lns2{lt}", [128, T], F32)),
                les.enter_context(nc.sbuf_tensor(f"lnra{lt}", [1, T], F32)),
                les.enter_context(nc.sbuf_tensor(f"lnrb{lt}", [1, T], F32)),
                les.enter_context(nc.sbuf_tensor(f"lnrc{lt}", [1, T], F32)),
            )
            ln = LN(P, bufs, ln_scr, fused=True)

        nxvso_holder = [NC if ffn else 0]
        tn, vn = P.c("t_nb"), P.c("v_nm")
        tn0, vn0 = tn.v, vn.v

        if do_cn:
            @blk.vector
            def _(v):
                P.vmark(v.reciprocal(bufs["RECIP"][:], PSB[6][:, 0:256]))
                P.vjoin(v)
                P.vmark(v.tensor_copy(bufs["RECIPH"][:], bufs["RECIP"][:]))
                P.vjoin(v)
                vn.inc(v.tensor_sub(bufs["RECIPL"][:], bufs["RECIP"][:],
                                    bufs["RECIPH"][:]))

            @blk.tensor
            def _(t):
                t.wait_ge(vn.sem, vn0 + 1)
                i = 0
                for h in range(NH):
                    for c in range(NC):
                        bank = i % 2
                        if i >= 2:
                            t.wait_ge(vn.sem, vn0 + 1 + i - 1)
                        last = None
                        for sub in range(4):
                            col = (h * NC + c) * 4 + sub
                            for rp in (bufs["RECIPH"], bufs["RECIPL"]):
                                last = t.matmul(
                                    PSB[bank][0:64, sub * 128:(sub + 1) * 128],
                                    rp[:, col:col + 1].broadcast_to([128, 64]),
                                    bufs["IDENTB"][:],
                                    start=(rp is bufs["RECIPH"]),
                                    stop=(rp is bufs["RECIPL"]))
                        tn.inc(last)
                        i += 1

            @blk.vector
            def _(v):
                i = 0
                for h in range(NH):
                    for c in range(NC):
                        v.wait_ge(tn.sem, tn0 + i + 1)
                        bank = i % 2
                        sl = ACT2[h // 2][(h % 2) * 64:(h % 2) * 64 + 64,
                                          c * CH:(c + 1) * CH]
                        vn.inc(v.tensor_tensor(sl, sl, PSB[bank][0:64, :],
                                               ALU.mult))
                        i += 1
        ncn = vn.v - vn0

        @blk.gpsimd
        def _(g):
            def wob_dma(j):
                (gq, gqb)[j % 2].inc(g.dma_start(
                    WOB[:].rearrange("p (t o) -> p t o", t=ND)[
                        :, :, j * 128:(j + 1) * 128],
                    bufs[wo_name][:, j * 128:(j + 1) * 128].rearrange(
                        "(t p) o -> p t o", p=128)), 16)

            off = NC if ffn else 0
            wob_dma(0)
            wob_dma(1)
            for dt2 in range(ND):
                if dt2 >= 2:
                    g.wait_ge(vo.sem, vo0 + nxvso_holder[0] + NC * (dt2 - 1))
                oq[dt2 % 2].inc(
                    g.dma_start(XOLD[dt2 % 2][:],
                                src_res[dt2 * 128:(dt2 + 1) * 128, :]), 16)
                j = dt2 + 2
                if j < ND:
                    g.wait_ge(to.sem, to0 + off + (j - 1) * NC)
                    wob_dma(j)

        def _store(sync, dt2):
            sync.wait_ge(vo.sem, vo0 + nxvso_holder[0] + NC * (dt2 + 1))
            if dt2 >= 2:
                sync.wait_ge(so[dt2 % 2].sem, so0[dt2 % 2] + 16 * (dt2 // 2))
            so[dt2 % 2].inc(
                sync.dma_start(dst[dt2 * 128:(dt2 + 1) * 128, :],
                               XNEW[dt2 % 2][:]), 16)

        @blk.sync
        def _(sync):
            sync.wait_ge(so[0].sem, so0[0])
            sync.wait_ge(so[1].sem, so0[1])
            for dt2 in range(ND):
                _store(sync, dt2)
            sync.wait_ge(so[0].sem, so0[0] + 16 * (ND // 2))
            sync.wait_ge(so[1].sem, so0[1] + 16 * (ND // 2))


        nxvso = 0
        if ffn:
            @blk.tensor
            def _(t):
                t.wait_ge(vn.sem, vn0 + ncn)
                for ch in range(NC):
                    bank = 4 + (ch % 2)
                    if ch >= 2:
                        t.wait_ge(vo.sem, vo0 + ch - 1)
                    last = None
                    for dt in range(ND):
                        last = t.matmul(
                            PSB[bank][:],
                            bufs["VW"]["o"][:, dt * 128:(dt + 1) * 128],
                            ACT2[dt][:, ch * CH:(ch + 1) * CH],
                            start=(dt == 0), stop=(dt == ND - 1))
                    to.inc(last)

            @blk.vector
            def _(v):
                for ch in range(NC):
                    v.wait_ge(to.sem, to0 + ch + 1)
                    bank = 4 + (ch % 2)
                    vo.inc(v.tensor_copy(
                        bufs["XVS"]["o"][:, ch * CH:(ch + 1) * CH], PSB[bank][:]))
            nxvso = NC

        @blk.tensor
        def _(t):
            t.wait_ge(vn.sem, vn0 + ncn)
            if ffn:
                t.wait_ge(vo.sem, vo0 + NC)
            ng = 0
            for dt2 in range(ND):
                t.wait_ge((gq, gqb)[dt2 % 2].sem,
                          (gq0, gqb0)[dt2 % 2] + 16 * (dt2 // 2 + 1))
                if ln is not None and dt2 >= 2:
                    ln.colsums(t, dt2 - 2)
                for ch in range(NC):
                    ng += 1
                    if ng > 2:
                        t.wait_ge(vo.sem, vo0 + nxvso + ng - 2)
                    bank = (ng - 1) % 2
                    last = None
                    for dt in range(ND):
                        last = t.matmul(
                            PSB[bank][:],
                            WOB[:, dt * D + dt2 * 128:dt * D + (dt2 + 1) * 128],
                            ACT2[dt][:, ch * CH:(ch + 1) * CH],
                            start=(dt == 0), stop=((not ffn) and dt == ND - 1))
                    if ffn:
                        last = t.matmul(
                            PSB[bank][:],
                            bufs["UP"]["o"][:, dt2 * 128:(dt2 + 1) * 128],
                            bufs["XVS"]["o"][:, ch * CH:(ch + 1) * CH],
                            start=False, stop=True)
                    to.inc(last)
            if ln is not None:
                ln.colsums(t, ND - 2)
                ln.colsums(t, ND - 1)

        @blk.vector
        def _(v):
            ng = 0
            for dt2 in range(ND):
                v.wait_ge(oq[dt2 % 2].sem, oq0[dt2 % 2] + 16 * (dt2 // 2 + 1))
                if dt2 >= 2:
                    v.wait_ge(so[dt2 % 2].sem, so0[dt2 % 2] + 16 * (dt2 // 2))
                    if ln is not None:
                        v.wait_ge(ln.lns.sem, ln.lns0 + 2 * (dt2 - 1))
                for ch in range(NC):
                    ng += 1
                    v.wait_ge(to.sem, to0 + nxvso + ng)
                    bank = (ng - 1) % 2
                    cs = slice(ch * CH, (ch + 1) * CH)
                    if ffn:
                        ins = v.scalar_tensor_tensor(
                            XNEW[dt2 % 2][:, cs], PSB[bank][:],
                            bufs["FBO"][:, dt2:dt2 + 1], XOLD[dt2 % 2][:, cs],
                            ALU.add, ALU.add)
                    else:
                        ins = v.tensor_add(XNEW[dt2 % 2][:, cs], PSB[bank][:],
                                           XOLD[dt2 % 2][:, cs])
                    vo.inc(ins)




        if ln is not None:
            @blk.scalar
            def _(s_):
                for dt in range(ND):
                    s_.wait_ge(vo.sem, vo0 + nxvso_holder[0] + NC * (dt + 1))
                    ln.cast_tile(s_, dt, XNEW[dt % 2])
                ln.scalar_rows(s_)

            @blk.vector
            def _(v):
                ln.vector_rows_norm(v)

            @blk.tensor
            def _(t):
                ln.tensor_bcasts(
                    t, pre=lambda t_: t_.wait_ge(
                        vo.sem, vo0 + nxvso_holder[0] + NC * ND))


def emit_attnb(P, bufs):
    nc = P.nc
    XLN, PSB, ACT2 = bufs["XLN"], bufs["PSB"], bufs["ACT2"]
    gq, gqb, sq = P.c("gq"), P.c("gqb"), P.c("sq")

    with ExitStack() as es:
        QT = [es.enter_context(nc.sbuf_tensor(f"bq{i}", [128, T], BF16))
              for i in range(ND)]
        WB = es.enter_context(nc.sbuf_tensor("bwb", [128, ND * D], BF16))
        LAMC = es.enter_context(nc.sbuf_tensor("lamc", [128, 8], F32))

        tp, vp2, sp = P.c("t_bp"), P.c("v_bp"), P.c("s_bp")
        gq0, sq0, tp0, vp0, sp0 = gq.v, sq.v, tp.v, vp2.v, sp.v
        gq0b = gqb.v
        with (nc.sbuf_tensor("lamr", [1, 16], F32) as LAMR,
              nc.sbuf_tensor("lamt", [16, 1], F32) as LAMT,
              nc.Block("bq") as blk):

            @blk.gpsimd
            def _(g):
                for j in range(ND):
                    if j >= 2:
                        g.wait_ge(tp.sem, tp0 + (j - 1) * NC)
                    (gq, gqb)[j % 2].inc(g.dma_start(
                        WB[:].rearrange("p (t o) -> p t o", t=ND)[
                            :, :, j * 128:(j + 1) * 128],
                        bufs["bWqT"][:, j * 128:(j + 1) * 128].rearrange(
                            "(t p) o -> p t o", p=128)), 16)

            @blk.sync
            def _(sync):
                sync.dma_start(LAMR[:], bufs["bbeta"][:]).then_inc(sq.sem, 16)
                sync.dma_start(bufs["BGAM"][:], bufs["bgam"][:]
                               ).then_inc(sq.sem, 16)
                sync.dma_start(bufs["BBET"][:], bufs["bbet"][:]
                               ).then_inc(sq.sem, 16)
                sq.v += 3 * 16

            @blk.tensor
            def _(t):
                ng = 0
                for dt2 in range(ND):
                    t.wait_ge((gq, gqb)[dt2 % 2].sem,
                              (gq0, gq0b)[dt2 % 2] + 16 * (dt2 // 2 + 1))
                    for ch in range(NC):
                        ng += 1
                        if ng > 3:
                            t.wait_ge(vp2.sem, vp0 + ng - 3)
                        bank = (ng - 1) % 3
                        last = None
                        for dt in range(ND):
                            last = t.matmul(
                                PSB[bank][:],
                                WB[:, dt * D + dt2 * 128:dt * D + (dt2 + 1) * 128],
                                XLN[dt][:, ch * CH:(ch + 1) * CH],
                                start=(dt == 0), stop=(dt == ND - 1))
                        tp.inc(last)

            @blk.vector
            def _(v):
                ng = 0
                for dt2 in range(ND):
                    for ch in range(NC):
                        ng += 1
                        v.wait_ge(tp.sem, tp0 + ng)
                        bank = (ng - 1) % 3
                        vp2.inc(v.tensor_copy(
                            QT[dt2][:, ch * CH:(ch + 1) * CH], PSB[bank][:]))

            @blk.scalar
            def _(s):
                s.wait_ge(sq.sem, sq0 + 3 * 16)
                sp.inc(s.activation(LAMR[:], LAMR[:], AF.Sigmoid))

            @blk.tensor
            def _(t):
                t.wait_ge(sp.sem, sp0 + 1)
                tp.inc(t.transpose(PSB[3][0:16, 0:1], LAMR[:],
                                   bufs["IDENT"][0:1, 0:1]))

            @blk.vector
            def _(v):
                v.wait_ge(tp.sem, tp0 + 33)
                vp2.inc(v.tensor_copy(LAMT[:], PSB[3][0:16, 0:1]))

            @blk.tensor
            def _(t):
                t.wait_ge(vp2.sem, vp0 + 33)
                last = None
                for dt in range(ND):
                    last = t.matmul(PSB[4][:, dt:dt + 1],
                                    bufs["SELH"][:, dt * 128:(dt + 1) * 128],
                                    LAMT[:], start=True, stop=True)
                tp.inc(last)

            @blk.vector
            def _(v):
                v.wait_ge(tp.sem, tp0 + 34)
                vp2.inc(v.tensor_copy(LAMC[:], PSB[4][:, 0:8]))

        tv, vv = P.c("t_bv"), P.c("v_bv")
        gq1, tv0, vv0, gq1b = gq.v, tv.v, vv.v, gqb.v
        with (nc.sbuf_tensor("bvt", [128, T], BF16) as VB,
              nc.sbuf_tensor("bscn", [128, T], F32) as SCN,
              nc.Block("bv") as blk):

            @blk.gpsimd
            def _(g):
                for j in range(ND):
                    if j >= 2:
                        g.wait_ge(tv.sem, tv0 + (j - 1) * NC)
                    (gq, gqb)[j % 2].inc(g.dma_start(
                        WB[:].rearrange("p (t o) -> p t o", t=ND)[
                            :, :, j * 128:(j + 1) * 128],
                        bufs["bWvT"][:, j * 128:(j + 1) * 128].rearrange(
                            "(t p) o -> p t o", p=128)), 16)

            @blk.tensor
            def _(t):
                ng = 0
                for dt2 in range(ND):
                    t.wait_ge((gq, gqb)[dt2 % 2].sem,
                              (gq1, gq1b)[dt2 % 2] + 16 * (dt2 // 2 + 1))
                    for ch in range(NC):
                        ng += 1
                        if ng > 3:
                            t.wait_ge(vv.sem, vv0 + (ng - 3) + (ng - 3) // 4 * 2)
                        bank = (ng - 1) % 3
                        last = None
                        for dt in range(ND):
                            last = t.matmul(
                                PSB[bank][:],
                                WB[:, dt * D + dt2 * 128:dt * D + (dt2 + 1) * 128],
                                XLN[dt][:, ch * CH:(ch + 1) * CH],
                                start=(dt == 0), stop=(dt == ND - 1))
                        tv.inc(last)

            @blk.vector
            def _(v):
                ng = 0
                for dt2 in range(ND):
                    v.wait_ge(vv.sem, vv.v)
                    for ch in range(NC):
                        ng += 1
                        v.wait_ge(tv.sem, tv0 + ng)
                        bank = (ng - 1) % 3
                        vv.inc(v.tensor_copy(VB[:, ch * CH:(ch + 1) * CH],
                                             PSB[bank][:]))
                    v.wait_ge(vv.sem, vv.v)
                    vv.inc(v.tensor_tensor_scan(
                        SCN[:], LAMC[:, dt2:dt2 + 1].broadcast_to([128, T]),
                        VB[:], 0.0, ALU.mult, ALU.add))
                    v.wait_ge(vv.sem, vv.v)
                    vv.inc(v.tensor_tensor(ACT2[dt2][:], QT[dt2][:], SCN[:],
                                           ALU.mult))

        ty, vy, sy = P.c("t_y"), P.c("v_y"), P.c("s_y")
        with (nc.sbuf_tensor("bysq", [128, T], BF16) as YSQ,
              nc.sbuf_tensor("byt10", [128, T], F32) as t1a,
              nc.sbuf_tensor("byt11", [128, T], F32) as t1b,
              nc.sbuf_tensor("byra", [1, T], F32) as rowa,
              nc.sbuf_tensor("byrb", [1, T], F32) as rowb,
              nc.sbuf_tensor("byrc", [1, T], F32) as rowc,
              nc.Block("byln") as blk):
            T1 = [t1a, t1b]
            ty0, vy0, sy0 = ty.v, vy.v, sy.v

            @blk.vector
            def _(v):
                for dt in range(ND):
                    if dt >= 1:
                        v.wait_ge(ty.sem, ty0 + dt)
                    vy.inc(v.tensor_mul(YSQ[:], ACT2[dt][:], ACT2[dt][:]))

            @blk.tensor
            def _(t):
                for dt in range(ND):
                    t.wait_ge(vy.sem, vy0 + dt + 1)
                    st_ = (dt == 0)
                    sp_ = (dt == ND - 1)
                    for ch in range(NC):
                        t.matmul(PSB[ch][0:1, :], bufs["ONESB"][:],
                                 ACT2[dt][:, ch * CH:(ch + 1) * CH],
                                 start=st_, stop=sp_)
                    last = None
                    for ch in range(NC):
                        last = t.matmul(PSB[4 + ch][0:1, :], bufs["ONESB"][:],
                                        YSQ[:, ch * CH:(ch + 1) * CH],
                                        start=st_, stop=sp_)
                    ty.inc(last)

            @blk.vector
            def _(v):
                v.wait_ge(ty.sem, ty0 + ND)
                ins = None
                for ch in range(NC):
                    v.tensor_scalar_mul(rowa[0:1, ch * CH:(ch + 1) * CH],
                                        PSB[ch][0:1, :], 1.0 / D)
                    ins = v.tensor_scalar_mul(rowb[0:1, ch * CH:(ch + 1) * CH],
                                              PSB[4 + ch][0:1, :], 1.0 / D)
                P.vmark(ins)
                P.vjoin(v)
                P.vmark(v.tensor_mul(rowc[0:1, :], rowa[0:1, :], rowa[0:1, :]))
                P.vjoin(v)
                vy.inc(v.tensor_sub(rowc[0:1, :], rowb[0:1, :], rowc[0:1, :]))

            @blk.scalar
            def _(s):
                s.wait_ge(vy.sem, vy0 + ND + 1)
                P.smark(s.activation(rowb[0:1, :], rowc[0:1, :], AF.Ln,
                                     bias=bufs["EPSC"][0:1, 0:1]))
                P.sjoin(s)
                sy.inc(s.activation(rowb[0:1, :], rowb[0:1, :], AF.Exp,
                                    scale=-0.5))

            @blk.vector
            def _(v):
                v.wait_ge(sy.sem, sy0 + 1)
                vy.inc(v.tensor_mul(rowa[0:1, :], rowa[0:1, :], rowb[0:1, :]))

            @blk.tensor
            def _(t):
                t.wait_ge(vy.sem, vy0 + ND + 2)
                last = None
                for ch in range(NC):
                    t.matmul(PSB[ch][:], bufs["ONESROW"][:],
                             rowb[0:1, ch * CH:(ch + 1) * CH], start=True,
                             stop=True)
                    last = t.matmul(PSB[4 + ch][:], bufs["ONESROW"][:],
                                    rowa[0:1, ch * CH:(ch + 1) * CH],
                                    start=True, stop=True)
                ty.inc(last)

            @blk.vector
            def _(v):
                v.wait_ge(ty.sem, ty0 + ND + 1)
                for dt in range(ND):
                    if dt >= 2:
                        v.wait_ge(sy.sem, sy0 + 1 + dt - 1)
                    ins = None
                    for ch in range(NC):
                        cs = slice(ch * CH, (ch + 1) * CH)
                        ins = v.tensor_tensor(T1[dt % 2][:, cs], ACT2[dt][:, cs],
                                              PSB[ch][:], ALU.mult)
                    P.vmark(ins)
                    P.vjoin(v)
                    ins = None
                    for ch in range(NC):
                        cs = slice(ch * CH, (ch + 1) * CH)
                        ins = v.tensor_tensor(T1[dt % 2][:, cs],
                                              T1[dt % 2][:, cs],
                                              PSB[4 + ch][:], ALU.subtract)
                    vy.inc(ins)

            @blk.scalar
            def _(s):
                for dt in range(ND):
                    s.wait_ge(vy.sem, vy0 + ND + 2 + dt + 1)
                    sy.inc(s.activation(ACT2[dt][:], T1[dt % 2][:], AF.Identity,
                                        bias=bufs["BBET"][:, dt:dt + 1],
                                        scale=bufs["BGAM"][:, dt:dt + 1]))

        tg2, vg2, sg2 = P.c("t_bg"), P.c("v_bg"), P.c("s_bg")
        gq2, tg20, vg20, sg20, gq2b = gq.v, tg2.v, vg2.v, sg2.v, gqb.v
        with (nc.sbuf_tensor("bgt0", [128, T], BF16) as gt0,
              nc.sbuf_tensor("bgt1", [128, T], BF16) as gt1,
              nc.Block("bg") as blk):
            GTt = [gt0, gt1]

            @blk.gpsimd
            def _(g):
                for j in range(ND):
                    if j >= 2:
                        g.wait_ge(tg2.sem, tg20 + (j - 1) * NC)
                    (gq, gqb)[j % 2].inc(g.dma_start(
                        WB[:].rearrange("p (t o) -> p t o", t=ND)[
                            :, :, j * 128:(j + 1) * 128],
                        bufs["bWgT"][:, j * 128:(j + 1) * 128].rearrange(
                            "(t p) o -> p t o", p=128)), 16)

            @blk.tensor
            def _(t):
                ng = 0
                for dt2 in range(ND):
                    t.wait_ge((gq, gqb)[dt2 % 2].sem,
                              (gq2, gq2b)[dt2 % 2] + 16 * (dt2 // 2 + 1))
                    for ch in range(NC):
                        ng += 1
                        if ng > 3:
                            t.wait_ge(sg2.sem, sg20 + ng - 3)
                        bank = (ng - 1) % 3
                        last = None
                        for dt in range(ND):
                            last = t.matmul(
                                PSB[bank][:],
                                WB[:, dt * D + dt2 * 128:dt * D + (dt2 + 1) * 128],
                                XLN[dt][:, ch * CH:(ch + 1) * CH],
                                start=(dt == 0), stop=(dt == ND - 1))
                        tg2.inc(last)

            @blk.scalar
            def _(s):
                ng = 0
                for dt2 in range(ND):
                    if dt2 >= 2:
                        s.wait_ge(vg2.sem, vg20 + dt2 - 1)
                    for ch in range(NC):
                        ng += 1
                        s.wait_ge(tg2.sem, tg20 + ng)
                        bank = (ng - 1) % 3
                        sg2.inc(s.activation(GTt[dt2 % 2][:, ch * CH:(ch + 1) * CH],
                                             PSB[bank][:], AF.Silu))

            @blk.vector
            def _(v):
                for dt2 in range(ND):
                    v.wait_ge(sg2.sem, sg20 + NC * (dt2 + 1))
                    vg2.inc(v.tensor_mul(ACT2[dt2][:], ACT2[dt2][:],
                                         GTt[dt2 % 2][:]))



def build_program(nsub=4):
    nc = bass.Bass("TRN2", target_bir_lowering=False, debug=False)
    P = Prog(nc)
    bufs = {}

    def din(name, shape, dtype=F32):
        bufs[name] = nc.dram_tensor(name, shape, dtype, kind="ExternalInput")

    din("xT", [D, T])
    for w in ("aWqT", "aWkT", "aWvT", "aWoT", "bWqT", "bWvT", "bWgT", "bWoT",
              "fWqT", "fWkT", "fWvT", "fWoT"):
        din(w, [D, D])
    for x in "qkvo":
        din(f"fV{x}", [D, 128])
        din(f"fU{x}", [128, D])
        din(f"fS{x}", [1, 128])
    din("fbq_c", [128, 8]); din("fbk_c", [128, 8]); din("fbo_c", [128, 8])
    din("fbv_r", [1, D])
    din("gw1T", [D, 256]); din("gb1c", [128, 2]); din("gw2T", [256, 8])
    din("gb2r", [1, 8])
    din("bbeta", [1, 16]); din("bgam", [128, 8]); din("bbet", [128, 8])
    din("IDENT", [128, 128]); din("IDENTB", [128, 128], BF16)
    din("ONESB", [128, 1], BF16)
    din("ONESROW", [1, 128]); din("MASKS", [128, 4 * 1024], BF16)
    din("SELH", [16, D]); din("SEL8", [8, 128])

    bufs["y"] = nc.dram_tensor("y", [D, T], F32, kind="ExternalOutput")
    bufs["Xs"] = nc.dram_tensor("Xs", [D, T], F32, kind="Internal")

    for cn, shape, dt in (("IDENT", [128, 128], F32),
                          ("IDENTB", [128, 128], BF16),
                          ("ONESB", [128, 1], BF16),
                          ("ONESROW", [1, 128], F32),
                          ("MASKS", [128, 4 * 1024], BF16),
                          ("SELH", [16, D], F32), ("SEL8", [8, 128], F32)):
        sb = nc.alloc_sbuf_tensor("c_" + cn, shape, dt)
        bufs[cn + "_dram"] = bufs[cn]
        bufs[cn] = sb
    bufs["XLN"] = [nc.alloc_sbuf_tensor(f"xln{i}", [128, T], BF16)
                   for i in range(ND)]
    bufs["ACT2"] = [nc.alloc_sbuf_tensor(f"act2{i}", [128, T], BF16)
                    for i in range(ND)]
    bufs["GR"] = nc.alloc_sbuf_tensor("grow", [1, 64], F32)
    bufs["NEGC"] = nc.alloc_sbuf_tensor("negcb", [128, NP], F32)
    bufs["RECIP"] = nc.alloc_sbuf_tensor("recip", [128, 256], F32)
    bufs["RECIPH"] = nc.alloc_sbuf_tensor("reciph", [128, 256], BF16)
    bufs["RECIPL"] = nc.alloc_sbuf_tensor("recipl", [128, 256], BF16)
    psd = [nc.alloc_psum_tensor(f"psd{i}", [128, 1024], F32)
           for i in range(2)]
    bufs["PSD"] = [h[:] for h in psd]
    bufs["PSB"] = [psd[0][:, 0:512], psd[0][:, 512:1024],
                   psd[1][:, 0:512], psd[1][:, 512:1024]] + [
        nc.alloc_psum_tensor(f"psb{i}", [128, 512], F32)[:] for i in range(4, 8)]
    bufs["GW2"] = nc.alloc_sbuf_tensor("gw2", [128, 16], F32)
    bufs["GB1"] = nc.alloc_sbuf_tensor("gb1", [128, 2], F32)
    bufs["GB2"] = nc.alloc_sbuf_tensor("gb2", [1, 8], F32)
    bufs["POOL"] = nc.alloc_sbuf_tensor("pool", [128, 8], F32)
    bufs["HSB"] = nc.alloc_sbuf_tensor("hsb", [128, 2], F32)
    bufs["GCOL"] = nc.alloc_sbuf_tensor("gcol", [8, 1], F32)
    bufs["FBQ"] = nc.alloc_sbuf_tensor("fbq", [128, 8], F32)
    bufs["FBK"] = nc.alloc_sbuf_tensor("fbk", [128, 8], F32)
    bufs["FBO"] = nc.alloc_sbuf_tensor("fbo", [128, 8], F32)
    bufs["BGAM"] = nc.alloc_sbuf_tensor("bgam_sb", [128, 8], F32)
    bufs["BBET"] = nc.alloc_sbuf_tensor("bbet_sb", [128, 8], F32)
    bufs["EPSC"] = nc.alloc_sbuf_tensor("epsc", [1, 1], F32)

    sq = P.c("sq")
    with nc.Block("init") as blk:
        @blk.sync
        def _(sync):
            for cn in ("IDENT", "IDENTB", "ONESB", "ONESROW", "MASKS", "SELH",
                       "SEL8"):
                sync.dma_start(bufs[cn][:], bufs[cn + "_dram"][:]
                               ).then_inc(sq.sem, 16)
            sync.dma_start(bufs["FBQ"][:], bufs["fbq_c"][:]).then_inc(sq.sem, 16)
            sync.dma_start(bufs["FBK"][:], bufs["fbk_c"][:]).then_inc(sq.sem, 16)
            sync.dma_start(bufs["FBO"][:], bufs["fbo_c"][:]).then_inc(sq.sem, 16)
            sq.v += 10 * 16
            sync.wait_ge(sq.sem, sq.v)

        @blk.vector
        def _(v):
            v.memset(bufs["EPSC"][:], EPS)

    rs_c = _ring_slots(True)
    rs_f = _ring_slots(False)

    def attention_scope(es, tag):
        bufs["VP"] = [es.enter_context(
            nc.sbuf_tensor(f"vpr{tag}{i}", [128, 16 * 65], BF16))
            for i in range(NT)]
        bufs["RING"] = [es.enter_context(
            nc.sbuf_tensor(f"ring{tag}{i}", [128, 1024], BF16))
            for i in range(NRING)]
        bufs["QP"] = [es.enter_context(
            nc.sbuf_tensor(f"qpair{tag}{i}", [128, T], BF16)) for i in range(2)]
        bufs["KP"] = [es.enter_context(
            nc.sbuf_tensor(f"kpair{tag}{i}", [128, T], BF16)) for i in range(2)]
        bufs["QSL"] = [es.enter_context(
            nc.sbuf_tensor(f"qsl{tag}{i}", [128, ND * 128], BF16))
            for i in range(2)]
        bufs["KSL"] = [es.enter_context(
            nc.sbuf_tensor(f"ksl{tag}{i}", [128, ND * 128], BF16))
            for i in range(2)]
        bufs["SQQ"] = es.enter_context(
            nc.sbuf_tensor(f"sqq{tag}", [128, T], BF16))
        bufs["SQK"] = bufs["SQQ"]
        bufs["DENS"] = es.enter_context(
            nc.sbuf_tensor(f"dens{tag}", [1, 4 * CH], F32))

    def ffn_scope(es, tag):
        bufs["VW"] = {x: es.enter_context(
            nc.sbuf_tensor(f"vw{tag}_{x}", [128, ND * 128], BF16))
            for x in "qkvo"}
        bufs["UP"] = {"o": es.enter_context(
            nc.sbuf_tensor(f"up{tag}_o", [128, D], BF16))}
        bufs["XVS"] = {"o": es.enter_context(
            nc.sbuf_tensor(f"xvs{tag}_o", [128, T], BF16))}

    def ffn_scope2(es, tag):
        for x in "qkv":
            bufs["UP"][x] = es.enter_context(
                nc.sbuf_tensor(f"up{tag}_{x}", [128, D], BF16))
            bufs["XVS"][x] = es.enter_context(
                nc.sbuf_tensor(f"xvs{tag}_{x}", [128, T], BF16))

    emit_ln(P, bufs, bufs["xT"], "a1")
    with ExitStack() as es:
        attention_scope(es, "a1")
        emit_vprime(P, bufs, "a1", "aWvT", ffn=False)
        emit_attention(P, bufs, "a1", "aWqT", "aWkT", causal=True, ffn=False,
                       ring_slot=rs_c)
    if nsub == 1:
        emit_oproj(P, bufs, "a1", "aWoT", "xT", "y", ffn=False, do_cn=True)
    else:
        emit_oproj(P, bufs, "a1", "aWoT", "xT", "Xs", ffn=False, do_cn=True,
                   ln_src=bufs["Xs"], ln_tag="f1")

    if nsub >= 2:
        with ExitStack() as es:
            ffn_scope(es, "f1")
            with ExitStack() as es2:
                ffn_scope2(es2, "f1")
                emit_gate(P, bufs, "f1")
                attention_scope(es2, "f1")
                emit_vprime(P, bufs, "f1", "fWvT", ffn=True)
                emit_attention(P, bufs, "f1", "fWqT", "fWkT", causal=False,
                               ffn=True, ring_slot=rs_f)
            if nsub == 2:
                emit_oproj(P, bufs, "f1", "fWoT", "Xs", "y", ffn=True,
                           do_cn=True)
            else:
                emit_oproj(P, bufs, "f1", "fWoT", "Xs", "Xs", ffn=True,
                           do_cn=True, ln_src=bufs["Xs"], ln_tag="b1")

    if nsub >= 3:
        emit_attnb(P, bufs)
        if nsub == 3:
            emit_oproj(P, bufs, "b1", "bWoT", "Xs", "y", ffn=False)
        else:
            emit_oproj(P, bufs, "b1", "bWoT", "Xs", "Xs", ffn=False,
                       ln_src=bufs["Xs"], ln_tag="f2")

    if nsub >= 4:
        with ExitStack() as es:
            ffn_scope(es, "f2")
            with ExitStack() as es2:
                ffn_scope2(es2, "f2")
                emit_gate(P, bufs, "f2")
                attention_scope(es2, "f2")
                emit_vprime(P, bufs, "f2", "fWvT", ffn=True)
                emit_attention(P, bufs, "f2", "fWqT", "fWkT", causal=False,
                               ffn=True, ring_slot=rs_f)
            emit_oproj(P, bufs, "f2", "fWoT", "Xs", "y", ffn=True, do_cn=True)

    with nc.Block("fin") as blk:
        @blk.sync
        def _(sync):
            for nm in ("sto0", "sto1"):
                if nm in P.ctr:
                    sync.wait_ge(P.ctr[nm].sem, P.ctr[nm].v)

    return nc


def host_inputs(inp):
    c = np.ascontiguousarray
    d = {k: np.asarray(v) for k, v in _mk_consts().items()}
    d["aWqT"] = c(np.asarray(inp["L0_attn_a_q_proj_weight"]).T)
    d["aWkT"] = c(np.asarray(inp["L0_attn_a_k_proj_weight"]).T)
    d["aWvT"] = c(np.asarray(inp["L0_attn_a_v_proj_weight"]).T)
    d["aWoT"] = c(np.asarray(inp["L0_attn_a_o_proj_weight"]).T)
    d["bWqT"] = c(np.asarray(inp["L0_attn_b_Wq"]).T)
    d["bWvT"] = c(np.asarray(inp["L0_attn_b_Wv"]).T)
    d["bWgT"] = c(np.asarray(inp["L0_attn_b_Wg"]).T)
    d["bWoT"] = c(np.asarray(inp["L0_attn_b_Wo"]).T)
    d["bbeta"] = c(np.asarray(inp["L0_attn_b_beta"]).reshape(1, 16))
    d["bgam"] = c(np.asarray(inp["L0_attn_b_ln_gamma"]).reshape(8, 128).T)
    d["bbet"] = c(np.asarray(inp["L0_attn_b_ln_beta"]).reshape(8, 128).T)
    for x in "qkvo":
        d[f"fW{x}T"] = c(np.asarray(inp[f"L0_ffn_{x}Wm"]).T)
        d[f"fV{x}"] = c(np.asarray(inp[f"L0_ffn_{x}V"]
                                   ).transpose(2, 0, 1).reshape(D, 128))
        d[f"fU{x}"] = c(np.asarray(inp[f"L0_ffn_{x}U"]
                                   ).transpose(0, 2, 1).reshape(128, D))
        d[f"fS{x}"] = c(np.asarray(inp[f"L0_ffn_{x}S"]).reshape(1, 128))
    d["fbq_c"] = c(np.asarray(inp["L0_ffn_qb"]).reshape(8, 128).T)
    d["fbk_c"] = c(np.asarray(inp["L0_ffn_kb"]).reshape(8, 128).T)
    d["fbo_c"] = c(np.asarray(inp["L0_ffn_ob"]).reshape(8, 128).T)
    d["fbv_r"] = c(np.asarray(inp["L0_ffn_vb"]).reshape(1, D))
    d["gw1T"] = c(np.asarray(inp["L0_ffn_gw1"]).T)
    d["gb1c"] = c(np.asarray(inp["L0_ffn_gb1"]).reshape(2, 128).T)
    d["gw2T"] = c(np.asarray(inp["L0_ffn_gw2"]).T)
    d["gb2r"] = c(np.asarray(inp["L0_ffn_gb2"]).reshape(1, 8))
    return d


def kernel(**inp):
    shared = host_inputs(inp)
    x = np.asarray(inp["L0_attn_a_x"])
    nc = build_program(nsub=4)
    in_maps = []
    for b in range(8):
        m = dict(shared)
        m["xT"] = np.ascontiguousarray(x[b].T)
        in_maps.append(m)
    res = run_bass_kernel_spmd(nc, in_maps, core_ids=list(range(8)))
    out = np.stack([np.asarray(r["y"]).T for r in res.results], axis=0)
    return np.ascontiguousarray(out.astype(np.float32))


# version 14
# speedup vs baseline: 1.0345x; 1.0345x over previous
from contextlib import ExitStack

import numpy as np
import ml_dtypes
import concourse.bass as bass
import concourse.mybir as mybir
from concourse.bass_utils import run_bass_kernel_spmd

F32 = mybir.dt.float32
BF16 = mybir.dt.bfloat16
AF = mybir.ActivationFunctionType
ALU = mybir.AluOpType
AX = mybir.AxisListType

D = 1024
T = 2048
ND = 8
NT = 16
NH = 16
DH = 64
NP = 8
CH = 512
NC = 4
EPS = 1e-5
ASCALE = float(DH) ** -0.5
NRING = 6


class Ctr:
    def __init__(self, nc, name):
        self.sem = nc.alloc_semaphore(name)
        self.v = 0

    def inc(self, inst, n=1):
        self.v += n
        inst.then_inc(self.sem, n)
        return inst


class Prog:
    def __init__(self, nc):
        self.nc = nc
        self.ctr = {}
        self.dg = nc.alloc_sbuf_tensor("dgap", [1, 8], mybir.dt.float32)
        self._gv = 0
        self._gs = 0

    def c(self, name):
        if name not in self.ctr:
            self.ctr[name] = Ctr(self.nc, name)
        return self.ctr[name]

    def vmark(self, inst):
        return self.c("ve").inc(inst)

    def vjoin(self, v):
        ve = self.c("ve")
        v.wait_ge(ve.sem, ve.v)

    def smark(self, inst):
        return self.c("se").inc(inst)

    def sjoin(self, s):
        se = self.c("se")
        s.wait_ge(se.sem, se.v)


def _mk_consts():
    ident = np.eye(128, dtype=np.float32)
    onesb = np.ones((128, 1), dtype=ml_dtypes.bfloat16)
    onesrow = np.ones((1, 128), dtype=np.float32)
    masks = np.zeros((128, 4 * 1024), dtype=np.float32)
    p = np.arange(128)[:, None]
    f = np.arange(CH)[None, :]
    for j in range(4):
        m = (f >= p + 128 * j)
        masks[:, j * 1024:j * 1024 + CH] = m
        masks[:, j * 1024 + CH:(j + 1) * 1024] = m
    masks = masks.astype(ml_dtypes.bfloat16)
    selh = np.zeros((16, 1024), dtype=np.float32)
    for dt in range(8):
        for pp in range(128):
            selh[2 * dt + pp // 64, dt * 128 + pp] = 1.0
    sel8 = np.zeros((8, 128), dtype=np.float32)
    for q in range(128):
        sel8[q // 16, q] = 1.0
    return dict(IDENT=ident, IDENTB=ident.astype(ml_dtypes.bfloat16),
                ONESB=onesb, ONESROW=onesrow, MASKS=masks,
                SELH=selh, SEL8=sel8)


def _ring_slots(causal):
    slots = {}
    g = 0
    ntk = (lambda c: 4 * c + 4) if causal else (lambda c: NT)
    for p in range(NP):
        for c in range(NC):
            for tk in range(ntk(c)):
                slots[(p, c, tk)] = g % NRING
                g += 1
    return slots



class LN:
    CSB = (2, 3, 6, 7)

    def __init__(self, P, bufs, scr, fused):
        self.P, self.bufs, self.fused = P, bufs, fused
        (self.XSQ, self.t1, self.s2, self.rowa, self.rowb, self.rowc) = scr
        self.lns, self.vcp = P.c("ln_sc"), P.c("ln_vc")
        self.tc, self.sr = P.c("ln_t"), P.c("ln_sr")
        self.vr, self.tb, self.vn = P.c("ln_vr"), P.c("ln_tb"), P.c("ln_vn")
        self.lns0, self.vcp0 = self.lns.v, self.vcp.v
        self.tc0, self.tb0, self.vn0 = self.tc.v, self.tb.v, self.vn.v
        self.s_order = [("mu", 0), ("m2", 0), ("mu", 1), ("m2", 1), ("ln", 0),
                        ("ex", 0), ("mu", 2), ("m2", 2), ("ln", 1), ("ex", 1),
                        ("mu", 3), ("m2", 3), ("ln", 2), ("ex", 2), ("ln", 3),
                        ("ex", 3)]
        self.srv, vv = {}, self.sr.v
        for kc in self.s_order:
            if kc[0] != "ln":
                vv += 1
                self.srv[kc] = vv
        self.vrv = {("var", ch): self.vr.v + ch + 1 for ch in range(NC)}
        self.vrv.update({("murs", ch): self.vr.v + NC + ch + 1
                         for ch in range(NC)})

    def cast_tile(self, s_, dt, src_ap):
        nb = len(self.XSQ)
        if dt >= nb:
            s_.wait_ge(self.tc.sem, self.tc0 + dt - nb + 1)
        if self.fused:
            self.lns.inc(s_.activation(self.bufs["XLN"][dt][:], src_ap[:],
                                       AF.Identity))
        self.lns.inc(s_.activation(self.XSQ[dt % nb][:], src_ap[:],
                                   AF.Square))

    def copy_tile(self, v, dt, src_ap):
        self.vcp.inc(v.tensor_copy(self.bufs["XLN"][dt][:], src_ap[:]))

    def colsums(self, t, dt):
        bufs = self.bufs
        per = 2 if self.fused else 1
        t.wait_ge(self.lns.sem, self.lns0 + per * (dt + 1))
        if not self.fused:
            t.wait_ge(self.vcp.sem, self.vcp0 + dt + 1)
        st, sp = dt == 0, dt == ND - 1
        nb = len(self.XSQ)
        for ch in range(NC):
            cs = slice(ch * CH, (ch + 1) * CH)
            t.matmul(bufs["PSB"][self.CSB[ch]][0:1, :], bufs["ONESB"][:],
                     bufs["XLN"][dt][:, cs], start=st, stop=sp)
        last = None
        for ch in range(NC):
            cs = slice(ch * CH, (ch + 1) * CH)
            last = t.matmul(bufs["PSB"][self.CSB[ch]][32:33, :],
                            bufs["ONESB"][:], self.XSQ[dt % nb][:, cs],
                            start=st, stop=sp)
        self.tc.inc(last)

    def scalar_rows(self, s_):
        P, bufs = self.P, self.bufs
        s_.wait_ge(self.tc.sem, self.tc0 + ND)
        for kind, ch in self.s_order:
            cs = slice(ch * CH, (ch + 1) * CH)
            b = bufs["PSB"][self.CSB[ch]]
            if kind == "mu":
                ins = s_.activation(self.rowa[0:1, cs], b[0:1, :],
                                    AF.Identity, scale=1.0 / D)
            elif kind == "m2":
                ins = s_.activation(self.rowb[0:1, cs], b[32:33, :],
                                    AF.Identity, scale=1.0 / D)
            elif kind == "ln":
                s_.wait_ge(self.vr.sem, self.vrv[("var", ch)])
                P.smark(s_.activation(self.rowb[0:1, cs],
                                      self.rowc[0:1, cs], AF.Ln,
                                      bias=bufs["EPSC"][0:1, 0:1]))
                continue
            else:
                P.sjoin(s_)
                ins = s_.activation(self.rowb[0:1, cs], self.rowb[0:1, cs],
                                    AF.Exp, scale=-0.5)
            self.sr.inc(ins)
            assert self.sr.v == self.srv[(kind, ch)]

    def vector_rows_norm(self, v):
        P, bufs, XLN = self.P, self.bufs, self.bufs["XLN"]
        for ch in range(NC):
            cs = slice(ch * CH, (ch + 1) * CH)
            v.wait_ge(self.sr.sem, self.srv[("m2", ch)])
            P.vmark(v.tensor_mul(self.rowc[0:1, cs], self.rowa[0:1, cs],
                                 self.rowa[0:1, cs]))
            P.vjoin(v)
            self.vr.inc(v.tensor_sub(self.rowc[0:1, cs], self.rowb[0:1, cs],
                                     self.rowc[0:1, cs]))
            assert self.vr.v == self.vrv[("var", ch)]
        for ch in range(NC):
            cs = slice(ch * CH, (ch + 1) * CH)
            v.wait_ge(self.sr.sem, self.srv[("ex", ch)])
            self.vr.inc(v.tensor_mul(self.rowa[0:1, cs], self.rowa[0:1, cs],
                                     self.rowb[0:1, cs]))
            assert self.vr.v == self.vrv[("murs", ch)]
            v.wait_ge(self.tb.sem, self.tb0 + ch + 1)
            if ch > 0:
                P.vjoin(v)
            last = None
            for dt in range(ND):
                sl = (self.t1 if dt < 4 else self.s2)[
                    :, (dt % 4) * CH:(dt % 4 + 1) * CH]
                last = v.tensor_tensor(sl, XLN[dt][:, cs],
                                       bufs["PSB"][ch][:], ALU.mult)
            P.vmark(last)
            P.vjoin(v)
            for dt in range(ND):
                sl = (self.t1 if dt < 4 else self.s2)[
                    :, (dt % 4) * CH:(dt % 4 + 1) * CH]
                last = v.tensor_tensor(XLN[dt][:, cs], sl,
                                       bufs["PSB"][4 + ch][:], ALU.subtract)
            P.vmark(last)
            self.vn.inc(v.engine_nop())

    def tensor_bcasts(self, t, pre=None):
        bufs = self.bufs
        for ch in range(NC):
            cs = slice(ch * CH, (ch + 1) * CH)
            if ch == 0 and pre is not None:
                pre(t)
            t.wait_ge(self.sr.sem, self.srv[("ex", ch)])
            t.wait_ge(self.vr.sem, self.vrv[("murs", ch)])
            t.matmul(bufs["PSB"][ch][:], bufs["ONESROW"][:],
                     self.rowb[0:1, cs], start=True, stop=True)
            self.tb.inc(t.matmul(bufs["PSB"][4 + ch][:], bufs["ONESROW"][:],
                                 self.rowa[0:1, cs], start=True, stop=True))


def emit_ln(P, bufs, src, tag):
    nc = P.nc
    with (nc.sbuf_tensor(f"lnst{tag}0", [128, T], F32) as xst0,
          nc.sbuf_tensor(f"lnst{tag}1", [128, T], F32) as xst1,
          nc.sbuf_tensor(f"lnsq{tag}0", [128, T], BF16) as xsq0,
          nc.sbuf_tensor(f"lnsq{tag}1", [128, T], BF16) as xsq1,
          nc.sbuf_tensor(f"lnt1{tag}", [128, T], F32) as t1,
          nc.sbuf_tensor(f"lnra{tag}", [1, T], F32) as rowa,
          nc.sbuf_tensor(f"lnrb{tag}", [1, T], F32) as rowb,
          nc.sbuf_tensor(f"lnrc{tag}", [1, T], F32) as rowc,
          nc.Block(f"ln_{tag}") as blk):
        XST = [xst0, xst1]
        ln = LN(P, bufs, ([xsq0, xsq1], t1, xst0, rowa, rowb, rowc),
                fused=False)
        lq = [P.c("lnq0"), P.c("lnq1")]
        lq0 = [lq[0].v, lq[1].v]

        @blk.sync
        def _(sync):
            for dt in range(ND):
                if dt >= 2:
                    sync.wait_ge(ln.lns.sem, ln.lns0 + dt - 1)
                    sync.wait_ge(ln.vcp.sem, ln.vcp0 + dt - 1)
                lq[dt % 2].inc(
                    sync.dma_start(XST[dt % 2][:],
                                   src[dt * 128:(dt + 1) * 128, :]), 16)

        @blk.scalar
        def _(s_):
            for dt in range(ND):
                s_.wait_ge(lq[dt % 2].sem, lq0[dt % 2] + 16 * (dt // 2 + 1))
                ln.cast_tile(s_, dt, XST[dt % 2])
            ln.scalar_rows(s_)

        @blk.vector
        def _(v):
            for dt in range(ND):
                v.wait_ge(lq[dt % 2].sem, lq0[dt % 2] + 16 * (dt // 2 + 1))
                ln.copy_tile(v, dt, XST[dt % 2])
            ln.vector_rows_norm(v)

        @blk.tensor
        def _(t):
            for dt in range(ND):
                ln.colsums(t, dt)
            ln.tensor_bcasts(t)


def emit_gate(P, bufs, tag):
    nc = P.nc
    XLN, PSB = bufs["XLN"], bufs["PSB"]
    tg, vg, sg, gq, sq = P.c("t_g"), P.c("v_g"), P.c("s_g"), P.c("gq"), P.c("sq")
    tg0, vg0, sg0, gq0, sq0 = tg.v, vg.v, sg.v, gq.v, sq.v
    R = bufs["GR"]

    with (nc.sbuf_tensor(f"gvtq{tag}", [128, ND * 128], BF16) as vtq,
          nc.sbuf_tensor(f"gvtk{tag}", [128, ND * 128], BF16) as vtk,
          nc.sbuf_tensor(f"gvtv{tag}", [128, ND * 128], BF16) as vtv,
          nc.sbuf_tensor(f"gvto{tag}", [128, ND * 128], BF16) as vto,
          nc.sbuf_tensor(f"gw1{tag}", [128, ND * 256], F32) as gw1,
          ExitStack() as ges,
          nc.Block(f"gate_{tag}") as blk):
        VT = dict(q=vtq, k=vtk, v=vtv, o=vto)
        bufs["SROW"] = {x: ges.enter_context(
            nc.sbuf_tensor(f"srow{tag}_{x}", [1, 128], F32)) for x in "qkvo"}
        bufs["WROW"] = {x: ges.enter_context(
            nc.sbuf_tensor(f"wrow{tag}_{x}", [1, 128], F32)) for x in "qkvo"}

        @blk.gpsimd
        def _(g):
            for x in "qkvo":
                g.dma_start(
                    VT[x][:].rearrange("p (t e) -> p t e", t=ND),
                    bufs[f"fV{x}"][:].rearrange("(t p) e -> p t e", p=128)
                ).then_inc(gq.sem, 16)
                g.dma_start(bufs["UP"][x][:], bufs[f"fU{x}"][:]
                            ).then_inc(gq.sem, 16)
            gq.v += 8 * 16

        @blk.sync
        def _(sync):
            sync.dma_start(gw1[:].rearrange("p (t g) -> p t g", t=ND),
                           bufs["gw1T"][:].rearrange("(t p) g -> p t g", p=128)
                           ).then_inc(sq.sem, 16)
            sync.dma_start(bufs["GW2"][:].rearrange("p (t e) -> p t e", t=2),
                           bufs["gw2T"][:].rearrange("(t p) e -> p t e", p=128)
                           ).then_inc(sq.sem, 16)
            sync.dma_start(bufs["GB1"][:], bufs["gb1c"][:]).then_inc(sq.sem, 16)
            sync.dma_start(bufs["GB2"][:], bufs["gb2r"][:]).then_inc(sq.sem, 16)
            for x in "qkvo":
                sync.dma_start(bufs["SROW"][x][:], bufs[f"fS{x}"][:]
                               ).then_inc(sq.sem, 16)
            sq.v += 8 * 16

        @blk.vector
        def _(v):
            for dt in range(ND):
                vg.inc(v.reduce_sum(bufs["POOL"][:, dt:dt + 1], XLN[dt][:], AX.X))

        @blk.tensor
        def _(t):
            t.wait_ge(vg.sem, vg0 + ND)
            t.wait_ge(sq.sem, sq0 + 8 * 16)
            for i in range(2):
                last = None
                for dt in range(ND):
                    last = t.matmul(PSB[0][:, i:i + 1],
                                    gw1[:, dt * 256 + i * 128:
                                        dt * 256 + (i + 1) * 128],
                                    bufs["POOL"][:, dt:dt + 1],
                                    start=(dt == 0), stop=(dt == ND - 1))
                tg.inc(last)

        @blk.scalar
        def _(s):
            s.wait_ge(sq.sem, sq0 + 8 * 16)
            s.wait_ge(tg.sem, tg0 + 2)
            for i in range(2):
                sg.inc(s.activation(bufs["HSB"][:, i:i + 1], PSB[0][:, i:i + 1],
                                    AF.Relu, bias=bufs["GB1"][:, i:i + 1],
                                    scale=1.0 / T))

        @blk.tensor
        def _(t):
            t.wait_ge(sg.sem, sg0 + 2)
            last = None
            for i in range(2):
                last = t.matmul(PSB[1][0:1, 0:8], bufs["HSB"][:, i:i + 1],
                                bufs["GW2"][:, i * 8:(i + 1) * 8],
                                start=(i == 0), stop=(i == 1))
            tg.inc(last)

        @blk.vector
        def _(v):
            v.wait_ge(tg.sem, tg0 + 3)
            v.wait_ge(sq.sem, sq0 + 8 * 16)
            LG = R[0:1, 0:8]
            P.vmark(v.tensor_add(LG, PSB[1][0:1, 0:8], bufs["GB2"][0:1, :]))
            P.vjoin(v)
            P.vmark(v.reduce_max(R[0:1, 8:9], LG, AX.X))
            P.vjoin(v)
            P.vmark(v.tensor_scalar(R[0:1, 16:24], LG, R[0:1, 8:9], None,
                                    ALU.subtract))
            P.vjoin(v)
            P.vmark(v.tensor_scalar(R[0:1, 24:32], R[0:1, 16:24], 0.0, None,
                                    ALU.is_ge))
            P.vjoin(v)
            P.vmark(v.scalar_tensor_tensor(R[0:1, 32:40], R[0:1, 24:32], -1e30,
                                           LG, ALU.mult, ALU.add))
            P.vjoin(v)
            P.vmark(v.reduce_max(R[0:1, 9:10], R[0:1, 32:40], AX.X))
            P.vjoin(v)
            vg.inc(v.tensor_scalar(R[0:1, 40:48], LG, R[0:1, 9:10], None,
                                   ALU.is_ge))

        @blk.scalar
        def _(s):
            s.wait_ge(vg.sem, vg0 + 9)
            sg.inc(s.activation(R[0:1, 48:56], R[0:1, 16:24], AF.Exp))

        @blk.vector
        def _(v):
            v.wait_ge(sg.sem, sg0 + 3)
            P.vmark(v.tensor_mul(R[0:1, 56:64], R[0:1, 48:56], R[0:1, 40:48]))
            P.vjoin(v)
            P.vmark(v.reduce_sum(R[0:1, 10:11], R[0:1, 56:64], AX.X))
            P.vjoin(v)
            P.vmark(v.reciprocal(R[0:1, 11:12], R[0:1, 10:11]))
            P.vjoin(v)
            vg.inc(v.tensor_scalar(R[0:1, 0:8], R[0:1, 56:64], R[0:1, 11:12],
                                   None, ALU.mult))

        @blk.tensor
        def _(t):
            t.wait_ge(vg.sem, vg0 + 10)
            tg.inc(t.transpose(PSB[2][0:8, 0:1], R[0:1, 0:8],
                               bufs["IDENT"][0:1, 0:1]))

        @blk.vector
        def _(v):
            v.wait_ge(tg.sem, tg0 + 4)
            vg.inc(v.tensor_copy(bufs["GCOL"][:], PSB[2][0:8, 0:1]))

        @blk.tensor
        def _(t):
            t.wait_ge(vg.sem, vg0 + 11)
            tg.inc(t.matmul(PSB[2][0:1, 0:128], bufs["GCOL"][:], bufs["SEL8"][:],
                            start=True, stop=True))

        @blk.vector
        def _(v):
            v.wait_ge(tg.sem, tg0 + 5)
            v.wait_ge(sq.sem, sq0 + 8 * 16)
            for x in "qkvo":
                vg.inc(v.tensor_mul(bufs["WROW"][x][:], PSB[2][0:1, 0:128],
                                    bufs["SROW"][x][:]))

        @blk.tensor
        def _(t):
            for i, x in enumerate("qkvo"):
                t.wait_ge(vg.sem, vg0 + 12 + i)
                if i >= 2:
                    t.wait_ge(vg.sem, vg0 + 16 + (i - 2))
                tg.inc(t.matmul(PSB[3 + (i % 2)][:, 0:128], bufs["ONESROW"][:],
                                bufs["WROW"][x][:], start=True, stop=True))

        @blk.vector
        def _(v):
            v.wait_ge(gq.sem, gq0 + 8 * 16)
            for i, x in enumerate("qkvo"):
                v.wait_ge(tg.sem, tg0 + 6 + i)
                ins = None
                for dt in range(ND):
                    ins = v.tensor_tensor(
                        bufs["VW"][x][:, dt * 128:(dt + 1) * 128],
                        VT[x][:, dt * 128:(dt + 1) * 128],
                        PSB[3 + (i % 2)][:, 0:128], ALU.mult)
                vg.inc(ins)

        @blk.tensor
        def _(t):
            n = 0
            for i, x in enumerate("qkv"):
                t.wait_ge(vg.sem, vg0 + 16 + i)
                for ch in range(NC):
                    n += 1
                    if n > 2:
                        t.wait_ge(vg.sem, vg0 + 19 + n - 2)
                    bank = 5 + (ch % 2)
                    last = None
                    for dt in range(ND):
                        last = t.matmul(PSB[bank][:],
                                        bufs["VW"][x][:, dt * 128:(dt + 1) * 128],
                                        XLN[dt][:, ch * CH:(ch + 1) * CH],
                                        start=(dt == 0), stop=(dt == ND - 1))
                    tg.inc(last)

        @blk.vector
        def _(v):
            n = 0
            for i, x in enumerate("qkv"):
                for ch in range(NC):
                    n += 1
                    v.wait_ge(tg.sem, tg0 + 9 + n)
                    bank = 5 + (ch % 2)
                    vg.inc(v.tensor_copy(bufs["XVS"][x][:, ch * CH:(ch + 1) * CH],
                                         PSB[bank][:]))


def emit_vprime(P, bufs, tag, wname, ffn):
    nc = P.nc
    XLN, VP, PSB = bufs["XLN"], bufs["VP"], bufs["PSB"]
    gq, tvp, vvp, sq = P.c("gq"), P.c("t_vp"), P.c("v_vp"), P.c("sq")
    gq0, tvp0, vvp0, sq0 = gq.v, tvp.v, vvp.v, sq.v

    with (nc.sbuf_tensor(f"wvb{tag}", [128, ND * 512], BF16) as WVB,
          nc.sbuf_tensor(f"vbb{tag}", [128, D], F32) as VBB,
          nc.Block(f"vp_{tag}") as blk):
        vbrow = bufs["DENS"][0:1, 0:D]

        @blk.gpsimd
        def _(g):
            for j in range(2):
                if j == 1:
                    g.wait_ge(tvp.sem, tvp0 + (1 if ffn else 0) + NT)
                g.dma_start(
                    WVB[:].rearrange("p (t o) -> p t o", t=ND),
                    bufs[wname][:, j * 512:(j + 1) * 512].rearrange(
                        "(t p) o -> p t o", p=128)
                ).then_inc(gq.sem, 16)
            gq.v += 32

        if ffn:
            @blk.sync
            def _(sync):
                sync.dma_start(vbrow, bufs["fbv_r"][:]
                               ).then_inc(sq.sem, 16)
                sq.v += 16

            @blk.tensor
            def _(t):
                t.wait_ge(sq.sem, sq0 + 16)
                t.matmul(PSB[6][:], bufs["ONESROW"][:],
                         vbrow[0:1, 0:512], start=True, stop=True)
                tvp.inc(t.matmul(PSB[7][:], bufs["ONESROW"][:],
                                 vbrow[0:1, 512:1024],
                                 start=True, stop=True))

        @blk.vector
        def _(v):
            if ffn:
                v.wait_ge(tvp.sem, tvp0 + 1)
                v.tensor_copy(VBB[:, 0:512], PSB[6][:])
                v.tensor_copy(VBB[:, 512:1024], PSB[7][:])
            for m in range(NT):
                ins = v.memset(
                    VP[m][:].rearrange("p (h c) -> p h c", c=65)[:, :, 64:65],
                    1.0)
            vvp.inc(ins)

        tbase = tvp.v

        @blk.tensor
        def _(t):
            t.wait_ge(vvp.sem, vvp0 + 1)
            n = 0
            for j in range(2):
                t.wait_ge(gq.sem, gq0 + 16 * (j + 1))
                for m in range(NT):
                    n += 1
                    if n > 2:
                        t.wait_ge(vvp.sem, vvp0 + 1 + n - 2)
                    bank = (n - 1) % 2
                    last = None
                    for dt in range(ND):
                        last = t.matmul(
                            PSB[bank][:],
                            XLN[dt][:, m * 128:(m + 1) * 128],
                            WVB[:, dt * 512:(dt + 1) * 512],
                            start=(dt == 0), stop=((not ffn) and dt == ND - 1))
                    if ffn:
                        last = t.matmul(PSB[bank][:],
                                        bufs["XVS"]["v"][:, m * 128:(m + 1) * 128],
                                        bufs["UP"]["v"][:, j * 512:(j + 1) * 512],
                                        start=False, stop=True)
                    tvp.inc(last)

        @blk.vector
        def _(v):
            n = 0
            for j in range(2):
                for m in range(NT):
                    n += 1
                    v.wait_ge(tvp.sem, tbase + n)
                    bank = (n - 1) % 2
                    dst = VP[m][:].rearrange("p (h c) -> p h c", c=65)[
                        :, j * 8:(j + 1) * 8, 0:64]
                    src3 = PSB[bank][:].rearrange("p (h c) -> p h c", c=64)
                    if ffn:
                        vbb3 = VBB[:, j * 512:(j + 1) * 512].rearrange(
                            "p (h c) -> p h c", c=64)
                        vvp.inc(v.tensor_add(dst, src3, vbb3))
                    else:
                        vvp.inc(v.tensor_copy(dst, src3))


def emit_attention(P, bufs, tag, wq_name, wk_name, causal, ffn, ring_slot):
    nc = P.nc
    XLN, VP, PSB, PSD = bufs["XLN"], bufs["VP"], bufs["PSB"], bufs["PSD"]
    RING, QP, KP, QSL, KSL = (bufs["RING"], bufs["QP"], bufs["KP"],
                              bufs["QSL"], bufs["KSL"])
    SQQ, SQK, DENS = bufs["SQQ"], bufs["SQK"], bufs["DENS"]
    ACT2, NEGC, R = bufs["ACT2"], bufs["NEGC"], bufs["GR"]
    DEN = PSB[6]
    softmax_scale = ASCALE if causal else 1.0
    ntk = (lambda c: 4 * c + 4) if causal else (lambda c: NT)

    tqk, vev, snc = P.c("t_qk"), P.c("v_ev"), P.c("s_nc")
    aq = [P.c("aq0"), P.c("aq1")]
    ts, ex, vm, pv = P.c("t_s"), P.c("s_ex"), P.c("v_m"), P.c("t_pv")
    vce, td = P.c("v_ce"), P.c("t_d")
    B = dict(tqk=tqk.v, vev=vev.v, snc=snc.v, ts=ts.v, ex=ex.v,
             vm=vm.v, pv=pv.v, vce=vce.v, td=td.v)
    aq0 = [aq[0].v, aq[1].v]

    ts_done, ex_done, vm_done = {}, {}, {}
    pv_done, exp_ring_wait, scores_bank_wait = {}, {}, {}
    last_scores = {}
    tsv, exv, vmv, pvv = B["ts"], B["ex"], B["vm"], B["pv"]
    slot_last_pv = {}
    hist = []
    for p in range(NP):
        for c in range(NC):
            n = ntk(c)
            for tk in range(n):
                hist.append((p, c, tk))
                g = len(hist) - 1
                scores_bank_wait[(p, c, tk)] = (
                    ex_done[hist[g - 2]] if g >= 2 else B["ex"])
                tsv += 1
                ts_done[(p, c, tk)] = tsv
                slot = ring_slot[(p, c, tk)]
                exp_ring_wait[(p, c, tk)] = slot_last_pv.get(slot, B["pv"])
                exv += 1
                ex_done[(p, c, tk)] = exv
                vmv += 1
                vm_done[(p, c, tk)] = vmv
                if (tk + 1) % 4 == 0 or tk == n - 1:
                    pvv += 1
                slot_last_pv[slot] = pvv
            pv_done[(p, c)] = pvv
        last_scores[p] = tsv

    with nc.Block(f"attn_{tag}") as blk:
        @blk.gpsimd
        def _(g):
            for p in range(NP):
                if p >= 2:
                    g.wait_ge(tqk.sem, B["tqk"] + 11 * (p - 1))
                for wn, bb in ((wq_name, QSL), (wk_name, KSL)):
                    aq[p % 2].inc(g.dma_start(
                        bb[p % 2][:].rearrange("p (t c) -> p t c", t=ND),
                        bufs[wn][:, p * 128:(p + 1) * 128].rearrange(
                            "(t p) c -> p t c", p=128)), 16)

        qk_state = {}

        def t_qk_group(t, p, g):
            if g == 0:
                t.wait_ge(aq[p % 2].sem, aq0[p % 2] + 32 * (p // 2 + 1))
            bank = 7
            if (p, g) != (0, 0):
                t.wait_ge(vev.sem, qk_state["free"])
            if g < 8:
                proj = "q" if g < 4 else "k"
                sl = QSL if g < 4 else KSL
                ch = g % 4
                last = None
                for dt in range(ND):
                    last = t.matmul(
                        PSB[bank][:],
                        sl[p % 2][:, dt * 128:(dt + 1) * 128],
                        XLN[dt][:, ch * CH:(ch + 1) * CH],
                        start=(dt == 0), stop=((not ffn) and dt == ND - 1))
                if ffn:
                    last = t.matmul(
                        PSB[bank][:],
                        bufs["UP"][proj][:, p * 128:(p + 1) * 128],
                        bufs["XVS"][proj][:, ch * CH:(ch + 1) * CH],
                        start=False, stop=True)
            elif g < 10:
                sq_i = SQQ if g == 8 else SQK
                t.wait_ge(vev.sem, B["vev"] + 13 * p + 9 + (g - 8))
                last = None
                for ch in range(NC):
                    last = t.matmul(PSB[bank][0:1, :], bufs["ONESB"][:],
                                    sq_i[:, ch * CH:(ch + 1) * CH],
                                    start=(ch == 0), stop=(ch == NC - 1))
            else:
                t.wait_ge(snc.sem, B["snc"] + p + 1)
                gb = 16 + 24 * (p % 2)
                last = t.matmul(PSB[bank][:, 0:1], bufs["ONESROW"][:],
                                R[0:1, gb + 3:gb + 4], start=True, stop=True)
            tqk.inc(last)
            consumed = {8: 11, 9: 12, 10: 13}.get(g, g + 1)
            qk_state["free"] = B["vev"] + 13 * p + consumed

        cad = 3 if causal else 4

        @blk.tensor
        def _(t):
            def emit_colify(p, c):
                t.wait_ge(vce.sem, B["vce"] + p * NC + c + 1)
                half = (p * NC + c) % 2
                last = None
                for h01 in range(2):
                    h = 2 * p + h01
                    for sub in range(4):
                        col = (h * NC + c) * 4 + sub
                        last = t.matmul(
                            DEN[:, col:col + 1],
                            DENS[0:1, half * 2 * CH + h01 * CH + sub * 128:
                                 half * 2 * CH + h01 * CH + (sub + 1) * 128],
                            bufs["IDENT"][0:1, 0:1], start=True, stop=True)
                td.inc(last)

            for g in range(11):
                t_qk_group(t, 0, g)
            col_pend = []
            TS_MAP = {0: 0, 1: 1, 2: 2, 3: 3, 4: 4, 5: 5, 6: 6, 7: 7,
                      8: 8, 11: 9, 13: 10}
            for p in range(NP):
                qkq = list(range(11)) if p + 1 < NP else []
                ci = [0]
                for c in range(NC):
                    n = ntk(c)
                    hA, hB = 2 * p, 2 * p + 1

                    def emit_pv(tkpv, first):
                        t.wait_ge(ex.sem, ex_done[(p, c, tkpv)])
                        if causal:
                            t.wait_ge(vm.sem, vm_done[(p, c, tkpv)])
                        if first and (p, c) != (0, 0):
                            t.wait_ge(vce.sem, B["vce"] + p * NC + c)
                        slot = ring_slot[(p, c, tkpv)]
                        t.matmul(PSB[4][0:65, :],
                                 VP[tkpv][:, hA * 65:hA * 65 + 65],
                                 RING[slot][:, 0:512], start=(tkpv == 0),
                                 stop=(tkpv == n - 1))
                        last = t.matmul(PSB[5][0:65, :],
                                        VP[tkpv][:, hB * 65:hB * 65 + 65],
                                        RING[slot][:, 512:1024],
                                        start=(tkpv == 0), stop=(tkpv == n - 1))
                        if (tkpv + 1) % 4 == 0 or tkpv == n - 1:
                            pv.inc(last)

                    pend = []
                    for tk in range(n):
                        t.wait_ge(ex.sem, scores_bank_wait[(p, c, tk)])
                        ks = slice(tk * 128, (tk + 1) * 128)
                        cs = slice(c * CH, (c + 1) * CH)
                        pd = PSD[(ts_done[(p, c, tk)] - 1 - B["ts"]) % 2]
                        t.matmul(pd[:, 0:512], KP[p % 2][0:64, ks],
                                 QP[p % 2][0:64, cs], start=True,
                                 stop=True, tile_position=(0, 0))
                        ts.inc(t.matmul(pd[:, 512:1024], KP[p % 2][64:128, ks],
                                        QP[p % 2][64:128, cs], start=True,
                                        stop=True, tile_position=(64, 0)))
                        pend.append(tk)
                        if len(pend) > 2:
                            emit_pv(pend.pop(0), first=(tk == 2))
                        if tk == 0 and col_pend:
                            emit_colify(*col_pend.pop(0))
                        if tk % cad == cad - 1 and p + 1 < NP:
                            g = TS_MAP.get(ci[0])
                            ci[0] += 1
                            if g is not None and g in qkq:
                                qkq.remove(g)
                                t_qk_group(t, p + 1, g)
                    for tkpv in pend:
                        emit_pv(tkpv, first=(tkpv == 0))
                    if c == NC - 1:
                        while qkq:
                            t_qk_group(t, p + 1, qkq.pop(0))
                    col_pend.append((p, c))
            while col_pend:
                emit_colify(*col_pend.pop(0))

        @blk.scalar
        def _(s):
            def negc_math(pp):
                gb = 16 + 24 * (pp % 2)
                s.wait_ge(vev.sem, B["vev"] + 13 * pp + 12)
                if pp >= 2:
                    s.wait_ge(tqk.sem, B["tqk"] + 11 * (pp - 2) + 11)
                P.smark(s.activation(R[0:1, gb + 1:gb + 2], R[0:1, gb:gb + 1],
                                     AF.Ln))
                P.sjoin(s)
                P.smark(s.activation(R[0:1, gb + 2:gb + 3],
                                     R[0:1, gb + 1:gb + 2], AF.Exp, scale=0.5))
                P.sjoin(s)
                snc.inc(s.mul(R[0:1, gb + 3:gb + 4], R[0:1, gb + 2:gb + 3],
                              -softmax_scale))

            negc_math(0)
            for p in range(NP):
                si = [0]
                done_negc = [p + 1 >= NP]
                s.wait_ge(vev.sem, B["vev"] + 13 * (p + 1))
                for c in range(NC):
                    n = ntk(c)
                    for tk in range(n):
                        s.wait_ge(ts.sem, ts_done[(p, c, tk)])
                        if exp_ring_wait[(p, c, tk)] > B["pv"]:
                            s.wait_ge(pv.sem, exp_ring_wait[(p, c, tk)])
                        slot = ring_slot[(p, c, tk)]
                        pd = PSD[(ts_done[(p, c, tk)] - 1 - B["ts"]) % 2]
                        ex.inc(s.activation(RING[slot][:], pd[:], AF.Exp,
                                            bias=NEGC[:, p:p + 1],
                                            scale=softmax_scale))
                        if tk % cad == cad - 1:
                            if si[0] == 12 and not done_negc[0]:
                                negc_math(p + 1)
                                done_negc[0] = True
                            si[0] += 1
                if not done_negc[0]:
                    negc_math(p + 1)

        ev_state = {}

        def v_evgroup(v, p, g):
            gbp = 16 + 24 * (p % 2)
            if g < 8:
                v.wait_ge(tqk.sem, B["tqk"] + 11 * p + g + 1)
                if p >= 2 and g == 0:
                    v.wait_ge(ts.sem, last_scores[p - 2])
                proj = "q" if g < 4 else "k"
                ch = g % 4
                dstb = QP if proj == "q" else KP
                cs = slice(ch * CH, (ch + 1) * CH)
                if ffn and proj == "q":
                    ins = v.tensor_scalar(dstb[p % 2][:, cs], PSB[7][:],
                                          bufs["FBQ"][:, p:p + 1], ASCALE,
                                          ALU.add, ALU.mult)
                elif ffn:
                    ins = v.tensor_scalar(dstb[p % 2][:, cs], PSB[7][:],
                                          bufs["FBK"][:, p:p + 1], None,
                                          ALU.add)
                else:
                    ins = v.tensor_copy(dstb[p % 2][:, cs], PSB[7][:])
                vev.inc(ins)
            elif g == 8:
                if p >= 1:
                    v.wait_ge(tqk.sem, B["tqk"] + 11 * (p - 1) + 10)
                v.wait_ge(vev.sem, vev.v)
                vev.inc(v.tensor_mul(SQQ[:], QP[p % 2][:], QP[p % 2][:]))
            elif g == 9:
                v.wait_ge(tqk.sem, B["tqk"] + 11 * p + 9)
                vev.inc(v.tensor_mul(SQK[:], KP[p % 2][:], KP[p % 2][:]))
            elif g == 10:
                v.wait_ge(tqk.sem, B["tqk"] + 11 * p + 9)
                if p >= 2:
                    v.wait_ge(snc.sem, B["snc"] + p - 1)
                vev.inc(v.reduce_max(R[0:1, gbp + 4:gbp + 5], PSB[7][0:1, :],
                                     AX.X))
            elif g == 11:
                v.wait_ge(tqk.sem, B["tqk"] + 11 * p + 10)
                P.vmark(v.reduce_max(R[0:1, gbp + 5:gbp + 6], PSB[7][0:1, :],
                                     AX.X))
                P.vjoin(v)
                vev.inc(v.tensor_mul(R[0:1, gbp:gbp + 1],
                                     R[0:1, gbp + 4:gbp + 5],
                                     R[0:1, gbp + 5:gbp + 6]))
            elif g == 12:
                v.wait_ge(tqk.sem, B["tqk"] + 11 * p + 11)
                vev.inc(v.tensor_copy(NEGC[:, p:p + 1], PSB[7][:, 0:1]))

        @blk.vector
        def _(v):
            for g in range(13):
                v_evgroup(v, 0, g)
            VS_MAP = {0: 0, 1: 1, 2: 2, 3: 3, 4: 4, 5: 5, 6: 6, 7: 7,
                      8: 8, 9: 9, 10: 10, 11: 11, 14: 12}
            for p in range(NP):
                evq = list(range(13)) if p + 1 < NP else []
                vi = [0]
                for c in range(NC):
                    n = ntk(c)
                    hA, hB = 2 * p, 2 * p + 1
                    for tk in range(n):
                        if causal:
                            v.wait_ge(ex.sem, ex_done[(p, c, tk)])
                            j = tk - 4 * c
                            slot = ring_slot[(p, c, tk)]
                            if 0 <= j <= 3:
                                ms = bufs["MASKS"][:, j * 1024:(j + 1) * 1024]
                                vm.inc(v.tensor_mul(RING[slot][:], RING[slot][:],
                                                    ms))
                            else:
                                vm.inc(v.engine_nop())
                        if tk % cad == cad - 1 and p + 1 < NP:
                            g = VS_MAP.get(vi[0])
                            vi[0] += 1
                            if g is not None and g in evq:
                                evq.remove(g)
                                v_evgroup(v, p + 1, g)
                    v.wait_ge(pv.sem, pv_done[(p, c)])
                    cc = p * NC + c
                    if cc >= 2:
                        v.wait_ge(td.sem, B["td"] + cc - 1)
                    half = cc % 2
                    v.tensor_copy(DENS[0:1, half * 2 * CH:half * 2 * CH + CH],
                                  PSB[4][64:65, :])
                    v.tensor_copy(DENS[0:1, half * 2 * CH + CH:
                                       half * 2 * CH + 2 * CH],
                                  PSB[5][64:65, :])
                    v.tensor_copy(ACT2[hA // 2][(hA % 2) * 64:(hA % 2) * 64 + 64,
                                                c * CH:(c + 1) * CH],
                                  PSB[4][0:64, :])
                    vce.inc(v.tensor_copy(
                        ACT2[hB // 2][(hB % 2) * 64:(hB % 2) * 64 + 64,
                                      c * CH:(c + 1) * CH], PSB[5][0:64, :]))
                    if c == NC - 1:
                        while evq:
                            v_evgroup(v, p + 1, evq.pop(0))



def emit_ctxnorm(P, bufs, tag):
    nc = P.nc
    PSB, ACT2 = bufs["PSB"], bufs["ACT2"]
    tn, vn = P.c("t_nb"), P.c("v_nm")
    tn0, vn0 = tn.v, vn.v

    with nc.Block(f"cn_{tag}") as blk:
        @blk.vector
        def _(v):
            vn.inc(v.reciprocal(bufs["RECIP"][:], PSB[6][:, 0:256]))

        @blk.tensor
        def _(t):
            t.wait_ge(vn.sem, vn0 + 1)
            i = 0
            for h in range(NH):
                for c in range(NC):
                    bank = i % 2
                    if i >= 2:
                        t.wait_ge(vn.sem, vn0 + 1 + i - 1)
                    last = None
                    for sub in range(4):
                        col = (h * NC + c) * 4 + sub
                        last = t.matmul(
                            PSB[bank][0:64, sub * 128:(sub + 1) * 128],
                            bufs["RECIP"][:, col:col + 1].broadcast_to([128, 64]),
                            bufs["IDENT"][:], start=True, stop=True)
                    tn.inc(last)
                    i += 1

        @blk.vector
        def _(v):
            i = 0
            for h in range(NH):
                for c in range(NC):
                    v.wait_ge(tn.sem, tn0 + i + 1)
                    bank = i % 2
                    sl = ACT2[h // 2][(h % 2) * 64:(h % 2) * 64 + 64,
                                      c * CH:(c + 1) * CH]
                    vn.inc(v.tensor_tensor(sl, sl, PSB[bank][0:64, :], ALU.mult))
                    i += 1


def emit_oproj(P, bufs, tag, wo_name, src_res_name, dst_name, ffn,
               do_cn=False, ln_src=None, ln_tag=None):
    nc = P.nc
    ACT2, PSB = bufs["ACT2"], bufs["PSB"]
    src_res = bufs[src_res_name]
    dst = bufs[dst_name]
    to, vo, gq, gqb = P.c("t_o"), P.c("v_o"), P.c("gq"), P.c("gqb")
    oq = [P.c("oq0"), P.c("oq1")]
    so = [P.c("sto0"), P.c("sto1")]
    to0, vo0, gq0, gqb0 = to.v, vo.v, gq.v, gqb.v
    oq0 = [oq[0].v, oq[1].v]
    so0 = [so[0].v, so[1].v]

    with (nc.sbuf_tensor(f"wob{tag}", [128, ND * D], BF16)) as WOB, (
          nc.sbuf_tensor(f"xo{tag}0", [128, T], F32)) as xo0, (
          nc.sbuf_tensor(f"xo{tag}1", [128, T], F32)) as xo1, (
          nc.sbuf_tensor(f"xn{tag}0", [128, T], F32)) as xn0, (
          nc.sbuf_tensor(f"xn{tag}1", [128, T], F32)) as xn1, (
          ExitStack()) as les, (
          nc.Block(f"o_{tag}")) as blk:
        XOLD = [xo0, xo1]
        XNEW = [xn0, xn1]
        ln = None
        if ln_src is not None:
            lt = ln_tag
            ln_scr = (
                [les.enter_context(nc.sbuf_tensor(f"lnsq{lt}{i}", [128, T],
                                                  BF16)) for i in range(4)],
                les.enter_context(nc.sbuf_tensor(f"lnt1{lt}", [128, T], F32)),
                les.enter_context(nc.sbuf_tensor(f"lns2{lt}", [128, T], F32)),
                les.enter_context(nc.sbuf_tensor(f"lnra{lt}", [1, T], F32)),
                les.enter_context(nc.sbuf_tensor(f"lnrb{lt}", [1, T], F32)),
                les.enter_context(nc.sbuf_tensor(f"lnrc{lt}", [1, T], F32)),
            )
            ln = LN(P, bufs, ln_scr, fused=True)

        nxvso_holder = [NC if ffn else 0]
        tn, vn = P.c("t_nb"), P.c("v_nm")
        tn0, vn0 = tn.v, vn.v

        if do_cn:
            @blk.vector
            def _(v):
                P.vmark(v.reciprocal(bufs["RECIP"][:], PSB[6][:, 0:256]))
                P.vjoin(v)
                P.vmark(v.tensor_copy(bufs["RECIPH"][:], bufs["RECIP"][:]))
                P.vjoin(v)
                vn.inc(v.tensor_sub(bufs["RECIPL"][:], bufs["RECIP"][:],
                                    bufs["RECIPH"][:]))

            @blk.tensor
            def _(t):
                t.wait_ge(vn.sem, vn0 + 1)
                i = 0
                for h in range(NH):
                    for c in range(NC):
                        bank = i % 2
                        if i >= 2:
                            t.wait_ge(vn.sem, vn0 + 1 + i - 1)
                        last = None
                        for sub in range(4):
                            col = (h * NC + c) * 4 + sub
                            for rp in (bufs["RECIPH"], bufs["RECIPL"]):
                                last = t.matmul(
                                    PSB[bank][0:64, sub * 128:(sub + 1) * 128],
                                    rp[:, col:col + 1].broadcast_to([128, 64]),
                                    bufs["IDENTB"][:],
                                    start=(rp is bufs["RECIPH"]),
                                    stop=(rp is bufs["RECIPL"]))
                        tn.inc(last)
                        i += 1

            @blk.vector
            def _(v):
                i = 0
                for h in range(NH):
                    for c in range(NC):
                        v.wait_ge(tn.sem, tn0 + i + 1)
                        bank = i % 2
                        sl = ACT2[h // 2][(h % 2) * 64:(h % 2) * 64 + 64,
                                          c * CH:(c + 1) * CH]
                        vn.inc(v.tensor_tensor(sl, sl, PSB[bank][0:64, :],
                                               ALU.mult))
                        i += 1
        ncn = vn.v - vn0

        @blk.gpsimd
        def _(g):
            def wob_dma(j):
                (gq, gqb)[j % 2].inc(g.dma_start(
                    WOB[:].rearrange("p (t o) -> p t o", t=ND)[
                        :, :, j * 128:(j + 1) * 128],
                    bufs[wo_name][:, j * 128:(j + 1) * 128].rearrange(
                        "(t p) o -> p t o", p=128)), 16)

            off = NC if ffn else 0
            for j in range(ND):
                if j >= 2:
                    g.wait_ge(to.sem, to0 + off + (j - 1) * NC)
                wob_dma(j)

        def _store(sync, dt2):
            sync.wait_ge(vo.sem, vo0 + nxvso_holder[0] + NC * (dt2 + 1))
            if dt2 >= 2:
                sync.wait_ge(so[dt2 % 2].sem, so0[dt2 % 2] + 16 * (dt2 // 2))
            so[dt2 % 2].inc(
                sync.dma_start(dst[dt2 * 128:(dt2 + 1) * 128, :],
                               XNEW[dt2 % 2][:]), 16)

        @blk.sync
        def _(sync):
            sync.wait_ge(so[0].sem, so0[0])
            sync.wait_ge(so[1].sem, so0[1])
            for dt2 in range(ND):
                if dt2 >= 2:
                    sync.wait_ge(vo.sem, vo0 + nxvso_holder[0] +
                                 NC * (dt2 - 1))
                oq[dt2 % 2].inc(
                    sync.dma_start(XOLD[dt2 % 2][:],
                                   src_res[dt2 * 128:(dt2 + 1) * 128, :]), 16)
                if dt2 >= 2:
                    _store(sync, dt2 - 2)
            _store(sync, ND - 2)
            _store(sync, ND - 1)
            sync.wait_ge(so[0].sem, so0[0] + 16 * (ND // 2))
            sync.wait_ge(so[1].sem, so0[1] + 16 * (ND // 2))


        nxvso = 0
        if ffn:
            @blk.tensor
            def _(t):
                t.wait_ge(vn.sem, vn0 + ncn)
                for ch in range(NC):
                    bank = 4 + (ch % 2)
                    if ch >= 2:
                        t.wait_ge(vo.sem, vo0 + ch - 1)
                    last = None
                    for dt in range(ND):
                        last = t.matmul(
                            PSB[bank][:],
                            bufs["VW"]["o"][:, dt * 128:(dt + 1) * 128],
                            ACT2[dt][:, ch * CH:(ch + 1) * CH],
                            start=(dt == 0), stop=(dt == ND - 1))
                    to.inc(last)

            @blk.vector
            def _(v):
                for ch in range(NC):
                    v.wait_ge(to.sem, to0 + ch + 1)
                    bank = 4 + (ch % 2)
                    vo.inc(v.tensor_copy(
                        bufs["XVS"]["o"][:, ch * CH:(ch + 1) * CH], PSB[bank][:]))
            nxvso = NC

        @blk.tensor
        def _(t):
            t.wait_ge(vn.sem, vn0 + ncn)
            if ffn:
                t.wait_ge(vo.sem, vo0 + NC)
            ng = 0
            for dt2 in range(ND):
                t.wait_ge((gq, gqb)[dt2 % 2].sem,
                          (gq0, gqb0)[dt2 % 2] + 16 * (dt2 // 2 + 1))
                if ln is not None and dt2 >= 2:
                    ln.colsums(t, dt2 - 2)
                for ch in range(NC):
                    ng += 1
                    if ng > 2:
                        t.wait_ge(vo.sem, vo0 + nxvso + ng - 2)
                    bank = (ng - 1) % 2
                    last = None
                    for dt in range(ND):
                        last = t.matmul(
                            PSB[bank][:],
                            WOB[:, dt * D + dt2 * 128:dt * D + (dt2 + 1) * 128],
                            ACT2[dt][:, ch * CH:(ch + 1) * CH],
                            start=(dt == 0), stop=((not ffn) and dt == ND - 1))
                    if ffn:
                        last = t.matmul(
                            PSB[bank][:],
                            bufs["UP"]["o"][:, dt2 * 128:(dt2 + 1) * 128],
                            bufs["XVS"]["o"][:, ch * CH:(ch + 1) * CH],
                            start=False, stop=True)
                    to.inc(last)
            if ln is not None:
                ln.colsums(t, ND - 2)
                ln.colsums(t, ND - 1)

        @blk.vector
        def _(v):
            ng = 0
            for dt2 in range(ND):
                v.wait_ge(oq[dt2 % 2].sem, oq0[dt2 % 2] + 16 * (dt2 // 2 + 1))
                if dt2 >= 2:
                    v.wait_ge(so[dt2 % 2].sem, so0[dt2 % 2] + 16 * (dt2 // 2))
                    if ln is not None:
                        v.wait_ge(ln.lns.sem, ln.lns0 + 2 * (dt2 - 1))
                for ch in range(NC):
                    ng += 1
                    v.wait_ge(to.sem, to0 + nxvso + ng)
                    bank = (ng - 1) % 2
                    cs = slice(ch * CH, (ch + 1) * CH)
                    if ffn:
                        ins = v.scalar_tensor_tensor(
                            XNEW[dt2 % 2][:, cs], PSB[bank][:],
                            bufs["FBO"][:, dt2:dt2 + 1], XOLD[dt2 % 2][:, cs],
                            ALU.add, ALU.add)
                    else:
                        ins = v.tensor_add(XNEW[dt2 % 2][:, cs], PSB[bank][:],
                                           XOLD[dt2 % 2][:, cs])
                    vo.inc(ins)




        if ln is not None:
            @blk.scalar
            def _(s_):
                for dt in range(ND):
                    s_.wait_ge(vo.sem, vo0 + nxvso_holder[0] + NC * (dt + 1))
                    ln.cast_tile(s_, dt, XNEW[dt % 2])
                ln.scalar_rows(s_)

            @blk.vector
            def _(v):
                ln.vector_rows_norm(v)

            @blk.tensor
            def _(t):
                ln.tensor_bcasts(
                    t, pre=lambda t_: t_.wait_ge(
                        vo.sem, vo0 + nxvso_holder[0] + NC * ND))


def emit_attnb(P, bufs):
    nc = P.nc
    XLN, PSB, ACT2 = bufs["XLN"], bufs["PSB"], bufs["ACT2"]
    gq, gqb, sq = P.c("gq"), P.c("gqb"), P.c("sq")

    with ExitStack() as es:
        QT = [es.enter_context(nc.sbuf_tensor(f"bq{i}", [128, T], BF16))
              for i in range(ND)]
        WB = es.enter_context(nc.sbuf_tensor("bwb", [128, ND * D], BF16))
        LAMC = es.enter_context(nc.sbuf_tensor("lamc", [128, 8], F32))

        tp, vp2, sp = P.c("t_bp"), P.c("v_bp"), P.c("s_bp")
        gq0, sq0, tp0, vp0, sp0 = gq.v, sq.v, tp.v, vp2.v, sp.v
        gq0b = gqb.v
        with (nc.sbuf_tensor("lamr", [1, 16], F32) as LAMR,
              nc.sbuf_tensor("lamt", [16, 1], F32) as LAMT,
              nc.Block("bq") as blk):

            @blk.gpsimd
            def _(g):
                for j in range(ND):
                    if j >= 2:
                        g.wait_ge(tp.sem, tp0 + (j - 1) * NC)
                    (gq, gqb)[j % 2].inc(g.dma_start(
                        WB[:].rearrange("p (t o) -> p t o", t=ND)[
                            :, :, j * 128:(j + 1) * 128],
                        bufs["bWqT"][:, j * 128:(j + 1) * 128].rearrange(
                            "(t p) o -> p t o", p=128)), 16)

            @blk.sync
            def _(sync):
                sync.dma_start(LAMR[:], bufs["bbeta"][:]).then_inc(sq.sem, 16)
                sync.dma_start(bufs["BGAM"][:], bufs["bgam"][:]
                               ).then_inc(sq.sem, 16)
                sync.dma_start(bufs["BBET"][:], bufs["bbet"][:]
                               ).then_inc(sq.sem, 16)
                sq.v += 3 * 16

            @blk.tensor
            def _(t):
                ng = 0
                for dt2 in range(ND):
                    t.wait_ge((gq, gqb)[dt2 % 2].sem,
                              (gq0, gq0b)[dt2 % 2] + 16 * (dt2 // 2 + 1))
                    for ch in range(NC):
                        ng += 1
                        if ng > 3:
                            t.wait_ge(vp2.sem, vp0 + ng - 3)
                        bank = (ng - 1) % 3
                        last = None
                        for dt in range(ND):
                            last = t.matmul(
                                PSB[bank][:],
                                WB[:, dt * D + dt2 * 128:dt * D + (dt2 + 1) * 128],
                                XLN[dt][:, ch * CH:(ch + 1) * CH],
                                start=(dt == 0), stop=(dt == ND - 1))
                        tp.inc(last)

            @blk.vector
            def _(v):
                ng = 0
                for dt2 in range(ND):
                    for ch in range(NC):
                        ng += 1
                        v.wait_ge(tp.sem, tp0 + ng)
                        bank = (ng - 1) % 3
                        vp2.inc(v.tensor_copy(
                            QT[dt2][:, ch * CH:(ch + 1) * CH], PSB[bank][:]))

            @blk.scalar
            def _(s):
                s.wait_ge(sq.sem, sq0 + 3 * 16)
                sp.inc(s.activation(LAMR[:], LAMR[:], AF.Sigmoid))

            @blk.tensor
            def _(t):
                t.wait_ge(sp.sem, sp0 + 1)
                tp.inc(t.transpose(PSB[3][0:16, 0:1], LAMR[:],
                                   bufs["IDENT"][0:1, 0:1]))

            @blk.vector
            def _(v):
                v.wait_ge(tp.sem, tp0 + 33)
                vp2.inc(v.tensor_copy(LAMT[:], PSB[3][0:16, 0:1]))

            @blk.tensor
            def _(t):
                t.wait_ge(vp2.sem, vp0 + 33)
                last = None
                for dt in range(ND):
                    last = t.matmul(PSB[4][:, dt:dt + 1],
                                    bufs["SELH"][:, dt * 128:(dt + 1) * 128],
                                    LAMT[:], start=True, stop=True)
                tp.inc(last)

            @blk.vector
            def _(v):
                v.wait_ge(tp.sem, tp0 + 34)
                vp2.inc(v.tensor_copy(LAMC[:], PSB[4][:, 0:8]))

        tv, vv = P.c("t_bv"), P.c("v_bv")
        gq1, tv0, vv0, gq1b = gq.v, tv.v, vv.v, gqb.v
        with (nc.sbuf_tensor("bvt", [128, T], BF16) as VB,
              nc.sbuf_tensor("bscn", [128, T], F32) as SCN,
              nc.Block("bv") as blk):

            @blk.gpsimd
            def _(g):
                for j in range(ND):
                    if j >= 2:
                        g.wait_ge(tv.sem, tv0 + (j - 1) * NC)
                    (gq, gqb)[j % 2].inc(g.dma_start(
                        WB[:].rearrange("p (t o) -> p t o", t=ND)[
                            :, :, j * 128:(j + 1) * 128],
                        bufs["bWvT"][:, j * 128:(j + 1) * 128].rearrange(
                            "(t p) o -> p t o", p=128)), 16)

            @blk.tensor
            def _(t):
                ng = 0
                for dt2 in range(ND):
                    t.wait_ge((gq, gqb)[dt2 % 2].sem,
                              (gq1, gq1b)[dt2 % 2] + 16 * (dt2 // 2 + 1))
                    for ch in range(NC):
                        ng += 1
                        if ng > 3:
                            t.wait_ge(vv.sem, vv0 + (ng - 3) + (ng - 3) // 4 * 2)
                        bank = (ng - 1) % 3
                        last = None
                        for dt in range(ND):
                            last = t.matmul(
                                PSB[bank][:],
                                WB[:, dt * D + dt2 * 128:dt * D + (dt2 + 1) * 128],
                                XLN[dt][:, ch * CH:(ch + 1) * CH],
                                start=(dt == 0), stop=(dt == ND - 1))
                        tv.inc(last)

            @blk.vector
            def _(v):
                ng = 0
                for dt2 in range(ND):
                    v.wait_ge(vv.sem, vv.v)
                    for ch in range(NC):
                        ng += 1
                        v.wait_ge(tv.sem, tv0 + ng)
                        bank = (ng - 1) % 3
                        vv.inc(v.tensor_copy(VB[:, ch * CH:(ch + 1) * CH],
                                             PSB[bank][:]))
                    v.wait_ge(vv.sem, vv.v)
                    vv.inc(v.tensor_tensor_scan(
                        SCN[:], LAMC[:, dt2:dt2 + 1].broadcast_to([128, T]),
                        VB[:], 0.0, ALU.mult, ALU.add))
                    v.wait_ge(vv.sem, vv.v)
                    vv.inc(v.tensor_tensor(ACT2[dt2][:], QT[dt2][:], SCN[:],
                                           ALU.mult))

        ty, vy, sy = P.c("t_y"), P.c("v_y"), P.c("s_y")
        with (nc.sbuf_tensor("bysq", [128, T], BF16) as YSQ,
              nc.sbuf_tensor("byt10", [128, T], F32) as t1a,
              nc.sbuf_tensor("byt11", [128, T], F32) as t1b,
              nc.sbuf_tensor("byra", [1, T], F32) as rowa,
              nc.sbuf_tensor("byrb", [1, T], F32) as rowb,
              nc.sbuf_tensor("byrc", [1, T], F32) as rowc,
              nc.Block("byln") as blk):
            T1 = [t1a, t1b]
            ty0, vy0, sy0 = ty.v, vy.v, sy.v

            @blk.vector
            def _(v):
                for dt in range(ND):
                    if dt >= 1:
                        v.wait_ge(ty.sem, ty0 + dt)
                    vy.inc(v.tensor_mul(YSQ[:], ACT2[dt][:], ACT2[dt][:]))

            @blk.tensor
            def _(t):
                for dt in range(ND):
                    t.wait_ge(vy.sem, vy0 + dt + 1)
                    st_ = (dt == 0)
                    sp_ = (dt == ND - 1)
                    for ch in range(NC):
                        t.matmul(PSB[ch][0:1, :], bufs["ONESB"][:],
                                 ACT2[dt][:, ch * CH:(ch + 1) * CH],
                                 start=st_, stop=sp_)
                    last = None
                    for ch in range(NC):
                        last = t.matmul(PSB[4 + ch][0:1, :], bufs["ONESB"][:],
                                        YSQ[:, ch * CH:(ch + 1) * CH],
                                        start=st_, stop=sp_)
                    ty.inc(last)

            @blk.vector
            def _(v):
                v.wait_ge(ty.sem, ty0 + ND)
                ins = None
                for ch in range(NC):
                    v.tensor_scalar_mul(rowa[0:1, ch * CH:(ch + 1) * CH],
                                        PSB[ch][0:1, :], 1.0 / D)
                    ins = v.tensor_scalar_mul(rowb[0:1, ch * CH:(ch + 1) * CH],
                                              PSB[4 + ch][0:1, :], 1.0 / D)
                P.vmark(ins)
                P.vjoin(v)
                P.vmark(v.tensor_mul(rowc[0:1, :], rowa[0:1, :], rowa[0:1, :]))
                P.vjoin(v)
                vy.inc(v.tensor_sub(rowc[0:1, :], rowb[0:1, :], rowc[0:1, :]))

            @blk.scalar
            def _(s):
                s.wait_ge(vy.sem, vy0 + ND + 1)
                P.smark(s.activation(rowb[0:1, :], rowc[0:1, :], AF.Ln,
                                     bias=bufs["EPSC"][0:1, 0:1]))
                P.sjoin(s)
                sy.inc(s.activation(rowb[0:1, :], rowb[0:1, :], AF.Exp,
                                    scale=-0.5))

            @blk.vector
            def _(v):
                v.wait_ge(sy.sem, sy0 + 1)
                vy.inc(v.tensor_mul(rowa[0:1, :], rowa[0:1, :], rowb[0:1, :]))

            @blk.tensor
            def _(t):
                t.wait_ge(vy.sem, vy0 + ND + 2)
                last = None
                for ch in range(NC):
                    t.matmul(PSB[ch][:], bufs["ONESROW"][:],
                             rowb[0:1, ch * CH:(ch + 1) * CH], start=True,
                             stop=True)
                    last = t.matmul(PSB[4 + ch][:], bufs["ONESROW"][:],
                                    rowa[0:1, ch * CH:(ch + 1) * CH],
                                    start=True, stop=True)
                ty.inc(last)

            @blk.vector
            def _(v):
                v.wait_ge(ty.sem, ty0 + ND + 1)
                for dt in range(ND):
                    if dt >= 2:
                        v.wait_ge(sy.sem, sy0 + 1 + dt - 1)
                    ins = None
                    for ch in range(NC):
                        cs = slice(ch * CH, (ch + 1) * CH)
                        ins = v.tensor_tensor(T1[dt % 2][:, cs], ACT2[dt][:, cs],
                                              PSB[ch][:], ALU.mult)
                    P.vmark(ins)
                    P.vjoin(v)
                    ins = None
                    for ch in range(NC):
                        cs = slice(ch * CH, (ch + 1) * CH)
                        ins = v.tensor_tensor(T1[dt % 2][:, cs],
                                              T1[dt % 2][:, cs],
                                              PSB[4 + ch][:], ALU.subtract)
                    vy.inc(ins)

            @blk.scalar
            def _(s):
                for dt in range(ND):
                    s.wait_ge(vy.sem, vy0 + ND + 2 + dt + 1)
                    sy.inc(s.activation(ACT2[dt][:], T1[dt % 2][:], AF.Identity,
                                        bias=bufs["BBET"][:, dt:dt + 1],
                                        scale=bufs["BGAM"][:, dt:dt + 1]))

        tg2, vg2, sg2 = P.c("t_bg"), P.c("v_bg"), P.c("s_bg")
        gq2, tg20, vg20, sg20, gq2b = gq.v, tg2.v, vg2.v, sg2.v, gqb.v
        with (nc.sbuf_tensor("bgt0", [128, T], BF16) as gt0,
              nc.sbuf_tensor("bgt1", [128, T], BF16) as gt1,
              nc.Block("bg") as blk):
            GTt = [gt0, gt1]

            @blk.gpsimd
            def _(g):
                for j in range(ND):
                    if j >= 2:
                        g.wait_ge(tg2.sem, tg20 + (j - 1) * NC)
                    (gq, gqb)[j % 2].inc(g.dma_start(
                        WB[:].rearrange("p (t o) -> p t o", t=ND)[
                            :, :, j * 128:(j + 1) * 128],
                        bufs["bWgT"][:, j * 128:(j + 1) * 128].rearrange(
                            "(t p) o -> p t o", p=128)), 16)

            @blk.tensor
            def _(t):
                ng = 0
                for dt2 in range(ND):
                    t.wait_ge((gq, gqb)[dt2 % 2].sem,
                              (gq2, gq2b)[dt2 % 2] + 16 * (dt2 // 2 + 1))
                    for ch in range(NC):
                        ng += 1
                        if ng > 3:
                            t.wait_ge(sg2.sem, sg20 + ng - 3)
                        bank = (ng - 1) % 3
                        last = None
                        for dt in range(ND):
                            last = t.matmul(
                                PSB[bank][:],
                                WB[:, dt * D + dt2 * 128:dt * D + (dt2 + 1) * 128],
                                XLN[dt][:, ch * CH:(ch + 1) * CH],
                                start=(dt == 0), stop=(dt == ND - 1))
                        tg2.inc(last)

            @blk.scalar
            def _(s):
                ng = 0
                for dt2 in range(ND):
                    if dt2 >= 2:
                        s.wait_ge(vg2.sem, vg20 + dt2 - 1)
                    for ch in range(NC):
                        ng += 1
                        s.wait_ge(tg2.sem, tg20 + ng)
                        bank = (ng - 1) % 3
                        sg2.inc(s.activation(GTt[dt2 % 2][:, ch * CH:(ch + 1) * CH],
                                             PSB[bank][:], AF.Silu))

            @blk.vector
            def _(v):
                for dt2 in range(ND):
                    v.wait_ge(sg2.sem, sg20 + NC * (dt2 + 1))
                    vg2.inc(v.tensor_mul(ACT2[dt2][:], ACT2[dt2][:],
                                         GTt[dt2 % 2][:]))



def build_program(nsub=4):
    nc = bass.Bass("TRN2", target_bir_lowering=False, debug=False)
    P = Prog(nc)
    bufs = {}

    def din(name, shape, dtype=F32):
        bufs[name] = nc.dram_tensor(name, shape, dtype, kind="ExternalInput")

    din("xT", [D, T])
    for w in ("aWqT", "aWkT", "aWvT", "aWoT", "bWqT", "bWvT", "bWgT", "bWoT",
              "fWqT", "fWkT", "fWvT", "fWoT"):
        din(w, [D, D])
    for x in "qkvo":
        din(f"fV{x}", [D, 128])
        din(f"fU{x}", [128, D])
        din(f"fS{x}", [1, 128])
    din("fbq_c", [128, 8]); din("fbk_c", [128, 8]); din("fbo_c", [128, 8])
    din("fbv_r", [1, D])
    din("gw1T", [D, 256]); din("gb1c", [128, 2]); din("gw2T", [256, 8])
    din("gb2r", [1, 8])
    din("bbeta", [1, 16]); din("bgam", [128, 8]); din("bbet", [128, 8])
    din("IDENT", [128, 128]); din("IDENTB", [128, 128], BF16)
    din("ONESB", [128, 1], BF16)
    din("ONESROW", [1, 128]); din("MASKS", [128, 4 * 1024], BF16)
    din("SELH", [16, D]); din("SEL8", [8, 128])

    bufs["y"] = nc.dram_tensor("y", [D, T], F32, kind="ExternalOutput")
    bufs["Xs"] = nc.dram_tensor("Xs", [D, T], F32, kind="Internal")

    for cn, shape, dt in (("IDENT", [128, 128], F32),
                          ("IDENTB", [128, 128], BF16),
                          ("ONESB", [128, 1], BF16),
                          ("ONESROW", [1, 128], F32),
                          ("MASKS", [128, 4 * 1024], BF16),
                          ("SELH", [16, D], F32), ("SEL8", [8, 128], F32)):
        sb = nc.alloc_sbuf_tensor("c_" + cn, shape, dt)
        bufs[cn + "_dram"] = bufs[cn]
        bufs[cn] = sb
    bufs["XLN"] = [nc.alloc_sbuf_tensor(f"xln{i}", [128, T], BF16)
                   for i in range(ND)]
    bufs["ACT2"] = [nc.alloc_sbuf_tensor(f"act2{i}", [128, T], BF16)
                    for i in range(ND)]
    bufs["GR"] = nc.alloc_sbuf_tensor("grow", [1, 64], F32)
    bufs["NEGC"] = nc.alloc_sbuf_tensor("negcb", [128, NP], F32)
    bufs["RECIP"] = nc.alloc_sbuf_tensor("recip", [128, 256], F32)
    bufs["RECIPH"] = nc.alloc_sbuf_tensor("reciph", [128, 256], BF16)
    bufs["RECIPL"] = nc.alloc_sbuf_tensor("recipl", [128, 256], BF16)
    psd = [nc.alloc_psum_tensor(f"psd{i}", [128, 1024], F32)
           for i in range(2)]
    bufs["PSD"] = [h[:] for h in psd]
    bufs["PSB"] = [psd[0][:, 0:512], psd[0][:, 512:1024],
                   psd[1][:, 0:512], psd[1][:, 512:1024]] + [
        nc.alloc_psum_tensor(f"psb{i}", [128, 512], F32)[:] for i in range(4, 8)]
    bufs["GW2"] = nc.alloc_sbuf_tensor("gw2", [128, 16], F32)
    bufs["GB1"] = nc.alloc_sbuf_tensor("gb1", [128, 2], F32)
    bufs["GB2"] = nc.alloc_sbuf_tensor("gb2", [1, 8], F32)
    bufs["POOL"] = nc.alloc_sbuf_tensor("pool", [128, 8], F32)
    bufs["HSB"] = nc.alloc_sbuf_tensor("hsb", [128, 2], F32)
    bufs["GCOL"] = nc.alloc_sbuf_tensor("gcol", [8, 1], F32)
    bufs["FBQ"] = nc.alloc_sbuf_tensor("fbq", [128, 8], F32)
    bufs["FBK"] = nc.alloc_sbuf_tensor("fbk", [128, 8], F32)
    bufs["FBO"] = nc.alloc_sbuf_tensor("fbo", [128, 8], F32)
    bufs["BGAM"] = nc.alloc_sbuf_tensor("bgam_sb", [128, 8], F32)
    bufs["BBET"] = nc.alloc_sbuf_tensor("bbet_sb", [128, 8], F32)
    bufs["EPSC"] = nc.alloc_sbuf_tensor("epsc", [1, 1], F32)

    sq = P.c("sq")
    with nc.Block("init") as blk:
        @blk.sync
        def _(sync):
            for cn in ("IDENT", "IDENTB", "ONESB", "ONESROW", "MASKS", "SELH",
                       "SEL8"):
                sync.dma_start(bufs[cn][:], bufs[cn + "_dram"][:]
                               ).then_inc(sq.sem, 16)
            sync.dma_start(bufs["FBQ"][:], bufs["fbq_c"][:]).then_inc(sq.sem, 16)
            sync.dma_start(bufs["FBK"][:], bufs["fbk_c"][:]).then_inc(sq.sem, 16)
            sync.dma_start(bufs["FBO"][:], bufs["fbo_c"][:]).then_inc(sq.sem, 16)
            sq.v += 10 * 16
            sync.wait_ge(sq.sem, sq.v)

        @blk.vector
        def _(v):
            v.memset(bufs["EPSC"][:], EPS)

    rs_c = _ring_slots(True)
    rs_f = _ring_slots(False)

    def attention_scope(es, tag):
        bufs["VP"] = [es.enter_context(
            nc.sbuf_tensor(f"vpr{tag}{i}", [128, 16 * 65], BF16))
            for i in range(NT)]
        bufs["RING"] = [es.enter_context(
            nc.sbuf_tensor(f"ring{tag}{i}", [128, 1024], BF16))
            for i in range(NRING)]
        bufs["QP"] = [es.enter_context(
            nc.sbuf_tensor(f"qpair{tag}{i}", [128, T], BF16)) for i in range(2)]
        bufs["KP"] = [es.enter_context(
            nc.sbuf_tensor(f"kpair{tag}{i}", [128, T], BF16)) for i in range(2)]
        bufs["QSL"] = [es.enter_context(
            nc.sbuf_tensor(f"qsl{tag}{i}", [128, ND * 128], BF16))
            for i in range(2)]
        bufs["KSL"] = [es.enter_context(
            nc.sbuf_tensor(f"ksl{tag}{i}", [128, ND * 128], BF16))
            for i in range(2)]
        bufs["SQQ"] = es.enter_context(
            nc.sbuf_tensor(f"sqq{tag}", [128, T], BF16))
        bufs["SQK"] = bufs["SQQ"]
        bufs["DENS"] = es.enter_context(
            nc.sbuf_tensor(f"dens{tag}", [1, 4 * CH], F32))

    def ffn_scope(es, tag):
        bufs["VW"] = {x: es.enter_context(
            nc.sbuf_tensor(f"vw{tag}_{x}", [128, ND * 128], BF16))
            for x in "qkvo"}
        bufs["UP"] = {"o": es.enter_context(
            nc.sbuf_tensor(f"up{tag}_o", [128, D], BF16))}
        bufs["XVS"] = {"o": es.enter_context(
            nc.sbuf_tensor(f"xvs{tag}_o", [128, T], BF16))}

    def ffn_scope2(es, tag):
        for x in "qkv":
            bufs["UP"][x] = es.enter_context(
                nc.sbuf_tensor(f"up{tag}_{x}", [128, D], BF16))
            bufs["XVS"][x] = es.enter_context(
                nc.sbuf_tensor(f"xvs{tag}_{x}", [128, T], BF16))

    emit_ln(P, bufs, bufs["xT"], "a1")
    with ExitStack() as es:
        attention_scope(es, "a1")
        emit_vprime(P, bufs, "a1", "aWvT", ffn=False)
        emit_attention(P, bufs, "a1", "aWqT", "aWkT", causal=True, ffn=False,
                       ring_slot=rs_c)
    if nsub == 1:
        emit_oproj(P, bufs, "a1", "aWoT", "xT", "y", ffn=False, do_cn=True)
    else:
        emit_oproj(P, bufs, "a1", "aWoT", "xT", "Xs", ffn=False, do_cn=True,
                   ln_src=bufs["Xs"], ln_tag="f1")

    if nsub >= 2:
        with ExitStack() as es:
            ffn_scope(es, "f1")
            with ExitStack() as es2:
                ffn_scope2(es2, "f1")
                emit_gate(P, bufs, "f1")
                attention_scope(es2, "f1")
                emit_vprime(P, bufs, "f1", "fWvT", ffn=True)
                emit_attention(P, bufs, "f1", "fWqT", "fWkT", causal=False,
                               ffn=True, ring_slot=rs_f)
            if nsub == 2:
                emit_oproj(P, bufs, "f1", "fWoT", "Xs", "y", ffn=True,
                           do_cn=True)
            else:
                emit_oproj(P, bufs, "f1", "fWoT", "Xs", "Xs", ffn=True,
                           do_cn=True, ln_src=bufs["Xs"], ln_tag="b1")

    if nsub >= 3:
        emit_attnb(P, bufs)
        if nsub == 3:
            emit_oproj(P, bufs, "b1", "bWoT", "Xs", "y", ffn=False)
        else:
            emit_oproj(P, bufs, "b1", "bWoT", "Xs", "Xs", ffn=False,
                       ln_src=bufs["Xs"], ln_tag="f2")

    if nsub >= 4:
        with ExitStack() as es:
            ffn_scope(es, "f2")
            with ExitStack() as es2:
                ffn_scope2(es2, "f2")
                emit_gate(P, bufs, "f2")
                attention_scope(es2, "f2")
                emit_vprime(P, bufs, "f2", "fWvT", ffn=True)
                emit_attention(P, bufs, "f2", "fWqT", "fWkT", causal=False,
                               ffn=True, ring_slot=rs_f)
            emit_oproj(P, bufs, "f2", "fWoT", "Xs", "y", ffn=True, do_cn=True)

    with nc.Block("fin") as blk:
        @blk.sync
        def _(sync):
            for nm in ("sto0", "sto1"):
                if nm in P.ctr:
                    sync.wait_ge(P.ctr[nm].sem, P.ctr[nm].v)

    return nc


def host_inputs(inp):
    c = np.ascontiguousarray
    d = {k: np.asarray(v) for k, v in _mk_consts().items()}
    d["aWqT"] = c(np.asarray(inp["L0_attn_a_q_proj_weight"]).T)
    d["aWkT"] = c(np.asarray(inp["L0_attn_a_k_proj_weight"]).T)
    d["aWvT"] = c(np.asarray(inp["L0_attn_a_v_proj_weight"]).T)
    d["aWoT"] = c(np.asarray(inp["L0_attn_a_o_proj_weight"]).T)
    d["bWqT"] = c(np.asarray(inp["L0_attn_b_Wq"]).T)
    d["bWvT"] = c(np.asarray(inp["L0_attn_b_Wv"]).T)
    d["bWgT"] = c(np.asarray(inp["L0_attn_b_Wg"]).T)
    d["bWoT"] = c(np.asarray(inp["L0_attn_b_Wo"]).T)
    d["bbeta"] = c(np.asarray(inp["L0_attn_b_beta"]).reshape(1, 16))
    d["bgam"] = c(np.asarray(inp["L0_attn_b_ln_gamma"]).reshape(8, 128).T)
    d["bbet"] = c(np.asarray(inp["L0_attn_b_ln_beta"]).reshape(8, 128).T)
    for x in "qkvo":
        d[f"fW{x}T"] = c(np.asarray(inp[f"L0_ffn_{x}Wm"]).T)
        d[f"fV{x}"] = c(np.asarray(inp[f"L0_ffn_{x}V"]
                                   ).transpose(2, 0, 1).reshape(D, 128))
        d[f"fU{x}"] = c(np.asarray(inp[f"L0_ffn_{x}U"]
                                   ).transpose(0, 2, 1).reshape(128, D))
        d[f"fS{x}"] = c(np.asarray(inp[f"L0_ffn_{x}S"]).reshape(1, 128))
    d["fbq_c"] = c(np.asarray(inp["L0_ffn_qb"]).reshape(8, 128).T)
    d["fbk_c"] = c(np.asarray(inp["L0_ffn_kb"]).reshape(8, 128).T)
    d["fbo_c"] = c(np.asarray(inp["L0_ffn_ob"]).reshape(8, 128).T)
    d["fbv_r"] = c(np.asarray(inp["L0_ffn_vb"]).reshape(1, D))
    d["gw1T"] = c(np.asarray(inp["L0_ffn_gw1"]).T)
    d["gb1c"] = c(np.asarray(inp["L0_ffn_gb1"]).reshape(2, 128).T)
    d["gw2T"] = c(np.asarray(inp["L0_ffn_gw2"]).T)
    d["gb2r"] = c(np.asarray(inp["L0_ffn_gb2"]).reshape(1, 8))
    return d


def kernel(**inp):
    shared = host_inputs(inp)
    x = np.asarray(inp["L0_attn_a_x"])
    nc = build_program(nsub=4)
    in_maps = []
    for b in range(8):
        m = dict(shared)
        m["xT"] = np.ascontiguousarray(x[b].T)
        in_maps.append(m)
    res = run_bass_kernel_spmd(nc, in_maps, core_ids=list(range(8)))
    out = np.stack([np.asarray(r["y"]).T for r in res.results], axis=0)
    return np.ascontiguousarray(out.astype(np.float32))


# version 15
# speedup vs baseline: 1.0539x; 1.0187x over previous
from contextlib import ExitStack

import numpy as np
import ml_dtypes
import concourse.bass as bass
import concourse.mybir as mybir
from concourse.bass_utils import run_bass_kernel_spmd

F32 = mybir.dt.float32
BF16 = mybir.dt.bfloat16
AF = mybir.ActivationFunctionType
ALU = mybir.AluOpType
AX = mybir.AxisListType

D = 1024
T = 2048
ND = 8
NT = 16
NH = 16
DH = 64
NP = 8
CH = 512
NC = 4
EPS = 1e-5
ASCALE = float(DH) ** -0.5
NRING = 6


class Ctr:
    def __init__(self, nc, name):
        self.sem = nc.alloc_semaphore(name)
        self.v = 0

    def inc(self, inst, n=1):
        self.v += n
        inst.then_inc(self.sem, n)
        return inst


class Prog:
    def __init__(self, nc):
        self.nc = nc
        self.ctr = {}
        self.dg = nc.alloc_sbuf_tensor("dgap", [1, 8], mybir.dt.float32)
        self._gv = 0
        self._gs = 0

    def c(self, name):
        if name not in self.ctr:
            self.ctr[name] = Ctr(self.nc, name)
        return self.ctr[name]

    def vmark(self, inst):
        return self.c("ve").inc(inst)

    def vjoin(self, v):
        ve = self.c("ve")
        v.wait_ge(ve.sem, ve.v)

    def smark(self, inst):
        return self.c("se").inc(inst)

    def sjoin(self, s):
        se = self.c("se")
        s.wait_ge(se.sem, se.v)


def _mk_consts():
    ident = np.eye(128, dtype=np.float32)
    onesb = np.ones((128, 1), dtype=ml_dtypes.bfloat16)
    onesrow = np.ones((1, 128), dtype=np.float32)
    masks = np.zeros((128, 4 * 1024), dtype=np.float32)
    p = np.arange(128)[:, None]
    f = np.arange(CH)[None, :]
    for j in range(4):
        m = (f >= p + 128 * j)
        masks[:, j * 1024:j * 1024 + CH] = m
        masks[:, j * 1024 + CH:(j + 1) * 1024] = m
    masks = masks.astype(ml_dtypes.bfloat16)
    selh = np.zeros((16, 1024), dtype=np.float32)
    for dt in range(8):
        for pp in range(128):
            selh[2 * dt + pp // 64, dt * 128 + pp] = 1.0
    sel8 = np.zeros((8, 128), dtype=np.float32)
    for q in range(128):
        sel8[q // 16, q] = 1.0
    return dict(IDENT=ident, IDENTB=ident.astype(ml_dtypes.bfloat16),
                ONESB=onesb, ONESROW=onesrow, MASKS=masks,
                SELH=selh, SEL8=sel8)


def _ring_slots(causal):
    slots = {}
    g = 0
    ntk = (lambda c: 4 * c + 4) if causal else (lambda c: NT)
    for p in range(NP):
        for c in range(NC):
            for tk in range(ntk(c)):
                slots[(p, c, tk)] = g % NRING
                g += 1
    return slots



class LN:
    CSB = (2, 3, 6, 7)

    def __init__(self, P, bufs, scr, fused, target=None, copy=True):
        self.P, self.bufs, self.fused = P, bufs, fused
        self.XT = target if target is not None else bufs["XLN"]
        self.copy = copy
        (self.XSQ, self.t1, self.s2, self.rowa, self.rowb, self.rowc) = scr
        self.lns, self.vcp = P.c("ln_sc"), P.c("ln_vc")
        self.tc, self.sr = P.c("ln_t"), P.c("ln_sr")
        self.vr, self.tb, self.vn = P.c("ln_vr"), P.c("ln_tb"), P.c("ln_vn")
        self.lns0, self.vcp0 = self.lns.v, self.vcp.v
        self.tc0, self.tb0, self.vn0 = self.tc.v, self.tb.v, self.vn.v
        self.s_order = [("mu", 0), ("m2", 0), ("mu", 1), ("m2", 1), ("ln", 0),
                        ("ex", 0), ("mu", 2), ("m2", 2), ("ln", 1), ("ex", 1),
                        ("mu", 3), ("m2", 3), ("ln", 2), ("ex", 2), ("ln", 3),
                        ("ex", 3)]
        self.srv, vv = {}, self.sr.v
        for kc in self.s_order:
            if kc[0] != "ln":
                vv += 1
                self.srv[kc] = vv
        self.vrv = {("var", ch): self.vr.v + ch + 1 for ch in range(NC)}
        self.vrv.update({("murs", ch): self.vr.v + NC + ch + 1
                         for ch in range(NC)})

    def cast_tile(self, s_, dt, src_ap):
        nb = len(self.XSQ)
        if dt >= nb:
            s_.wait_ge(self.tc.sem, self.tc0 + dt - nb + 1)
        if self.fused:
            self.lns.inc(s_.activation(self.bufs["XLN"][dt][:], src_ap[:],
                                       AF.Identity))
        self.lns.inc(s_.activation(self.XSQ[dt % nb][:], src_ap[:],
                                   AF.Square))

    def copy_tile(self, v, dt, src_ap):
        self.vcp.inc(v.tensor_copy(self.bufs["XLN"][dt][:], src_ap[:]))

    def colsums(self, t, dt):
        bufs = self.bufs
        per = 2 if self.fused else 1
        t.wait_ge(self.lns.sem, self.lns0 + per * (dt + 1))
        if self.copy and not self.fused:
            t.wait_ge(self.vcp.sem, self.vcp0 + dt + 1)
        st, sp = dt == 0, dt == ND - 1
        nb = len(self.XSQ)
        for ch in range(NC):
            cs = slice(ch * CH, (ch + 1) * CH)
            t.matmul(bufs["PSB"][self.CSB[ch]][0:1, :], bufs["ONESB"][:],
                     self.XT[dt][:, cs], start=st, stop=sp)
        last = None
        for ch in range(NC):
            cs = slice(ch * CH, (ch + 1) * CH)
            last = t.matmul(bufs["PSB"][self.CSB[ch]][32:33, :],
                            bufs["ONESB"][:], self.XSQ[dt % nb][:, cs],
                            start=st, stop=sp)
        self.tc.inc(last)

    def scalar_rows(self, s_):
        P, bufs = self.P, self.bufs
        s_.wait_ge(self.tc.sem, self.tc0 + ND)
        for kind, ch in self.s_order:
            cs = slice(ch * CH, (ch + 1) * CH)
            b = bufs["PSB"][self.CSB[ch]]
            if kind == "mu":
                ins = s_.activation(self.rowa[0:1, cs], b[0:1, :],
                                    AF.Identity, scale=1.0 / D)
            elif kind == "m2":
                ins = s_.activation(self.rowb[0:1, cs], b[32:33, :],
                                    AF.Identity, scale=1.0 / D)
            elif kind == "ln":
                s_.wait_ge(self.vr.sem, self.vrv[("var", ch)])
                P.smark(s_.activation(self.rowb[0:1, cs],
                                      self.rowc[0:1, cs], AF.Ln,
                                      bias=bufs["EPSC"][0:1, 0:1]))
                continue
            else:
                P.sjoin(s_)
                ins = s_.activation(self.rowb[0:1, cs], self.rowb[0:1, cs],
                                    AF.Exp, scale=-0.5)
            self.sr.inc(ins)
            assert self.sr.v == self.srv[(kind, ch)]

    def vector_rows_norm(self, v):
        P, bufs, XLN = self.P, self.bufs, self.XT
        for ch in range(NC):
            cs = slice(ch * CH, (ch + 1) * CH)
            v.wait_ge(self.sr.sem, self.srv[("m2", ch)])
            P.vmark(v.tensor_mul(self.rowc[0:1, cs], self.rowa[0:1, cs],
                                 self.rowa[0:1, cs]))
            P.vjoin(v)
            self.vr.inc(v.tensor_sub(self.rowc[0:1, cs], self.rowb[0:1, cs],
                                     self.rowc[0:1, cs]))
            assert self.vr.v == self.vrv[("var", ch)]
        for ch in range(NC):
            cs = slice(ch * CH, (ch + 1) * CH)
            v.wait_ge(self.sr.sem, self.srv[("ex", ch)])
            self.vr.inc(v.tensor_mul(self.rowa[0:1, cs], self.rowa[0:1, cs],
                                     self.rowb[0:1, cs]))
            assert self.vr.v == self.vrv[("murs", ch)]
            v.wait_ge(self.tb.sem, self.tb0 + ch + 1)
            if ch > 0:
                P.vjoin(v)
            last = None
            for dt in range(ND):
                sl = (self.t1 if dt < 4 else self.s2)[
                    :, (dt % 4) * CH:(dt % 4 + 1) * CH]
                last = v.tensor_tensor(sl, XLN[dt][:, cs],
                                       bufs["PSB"][ch][:], ALU.mult)
            P.vmark(last)
            P.vjoin(v)
            for dt in range(ND):
                sl = (self.t1 if dt < 4 else self.s2)[
                    :, (dt % 4) * CH:(dt % 4 + 1) * CH]
                last = v.tensor_tensor(XLN[dt][:, cs], sl,
                                       bufs["PSB"][4 + ch][:], ALU.subtract)
            P.vmark(last)
            self.vn.inc(v.engine_nop())

    def tensor_bcasts(self, t, pre=None):
        bufs = self.bufs
        for ch in range(NC):
            cs = slice(ch * CH, (ch + 1) * CH)
            if ch == 0 and pre is not None:
                pre(t)
            t.wait_ge(self.sr.sem, self.srv[("ex", ch)])
            t.wait_ge(self.vr.sem, self.vrv[("murs", ch)])
            t.matmul(bufs["PSB"][ch][:], bufs["ONESROW"][:],
                     self.rowb[0:1, cs], start=True, stop=True)
            self.tb.inc(t.matmul(bufs["PSB"][4 + ch][:], bufs["ONESROW"][:],
                                 self.rowa[0:1, cs], start=True, stop=True))


def emit_ln(P, bufs, src, tag):
    nc = P.nc
    with (nc.sbuf_tensor(f"lnst{tag}0", [128, T], F32) as xst0,
          nc.sbuf_tensor(f"lnst{tag}1", [128, T], F32) as xst1,
          nc.sbuf_tensor(f"lnsq{tag}0", [128, T], BF16) as xsq0,
          nc.sbuf_tensor(f"lnsq{tag}1", [128, T], BF16) as xsq1,
          nc.sbuf_tensor(f"lnt1{tag}", [128, T], F32) as t1,
          nc.sbuf_tensor(f"lnra{tag}", [1, T], F32) as rowa,
          nc.sbuf_tensor(f"lnrb{tag}", [1, T], F32) as rowb,
          nc.sbuf_tensor(f"lnrc{tag}", [1, T], F32) as rowc,
          nc.Block(f"ln_{tag}") as blk):
        XST = [xst0, xst1]
        ln = LN(P, bufs, ([xsq0, xsq1], t1, xst0, rowa, rowb, rowc),
                fused=False)
        lq = [P.c("lnq0"), P.c("lnq1")]
        lq0 = [lq[0].v, lq[1].v]

        @blk.sync
        def _(sync):
            for dt in range(ND):
                if dt >= 2:
                    sync.wait_ge(ln.lns.sem, ln.lns0 + dt - 1)
                    sync.wait_ge(ln.vcp.sem, ln.vcp0 + dt - 1)
                lq[dt % 2].inc(
                    sync.dma_start(XST[dt % 2][:],
                                   src[dt * 128:(dt + 1) * 128, :]), 16)

        @blk.scalar
        def _(s_):
            for dt in range(ND):
                s_.wait_ge(lq[dt % 2].sem, lq0[dt % 2] + 16 * (dt // 2 + 1))
                ln.cast_tile(s_, dt, XST[dt % 2])
            ln.scalar_rows(s_)

        @blk.vector
        def _(v):
            for dt in range(ND):
                v.wait_ge(lq[dt % 2].sem, lq0[dt % 2] + 16 * (dt // 2 + 1))
                ln.copy_tile(v, dt, XST[dt % 2])
            ln.vector_rows_norm(v)

        @blk.tensor
        def _(t):
            for dt in range(ND):
                ln.colsums(t, dt)
            ln.tensor_bcasts(t)


def emit_gate(P, bufs, tag):
    nc = P.nc
    XLN, PSB = bufs["XLN"], bufs["PSB"]
    tg, vg, sg, gq, sq = P.c("t_g"), P.c("v_g"), P.c("s_g"), P.c("gq"), P.c("sq")
    tg0, vg0, sg0, gq0, sq0 = tg.v, vg.v, sg.v, gq.v, sq.v
    R = bufs["GR"]

    with (nc.sbuf_tensor(f"gvtq{tag}", [128, ND * 128], BF16) as vtq,
          nc.sbuf_tensor(f"gvtk{tag}", [128, ND * 128], BF16) as vtk,
          nc.sbuf_tensor(f"gvtv{tag}", [128, ND * 128], BF16) as vtv,
          nc.sbuf_tensor(f"gvto{tag}", [128, ND * 128], BF16) as vto,
          nc.sbuf_tensor(f"gw1{tag}", [128, ND * 256], F32) as gw1,
          ExitStack() as ges,
          nc.Block(f"gate_{tag}") as blk):
        VT = dict(q=vtq, k=vtk, v=vtv, o=vto)
        bufs["SROW"] = {x: ges.enter_context(
            nc.sbuf_tensor(f"srow{tag}_{x}", [1, 128], F32)) for x in "qkvo"}
        bufs["WROW"] = {x: ges.enter_context(
            nc.sbuf_tensor(f"wrow{tag}_{x}", [1, 128], F32)) for x in "qkvo"}

        @blk.gpsimd
        def _(g):
            for x in "qkvo":
                g.dma_start(
                    VT[x][:].rearrange("p (t e) -> p t e", t=ND),
                    bufs[f"fV{x}"][:].rearrange("(t p) e -> p t e", p=128)
                ).then_inc(gq.sem, 16)
                g.dma_start(bufs["UP"][x][:], bufs[f"fU{x}"][:]
                            ).then_inc(gq.sem, 16)
            gq.v += 8 * 16

        @blk.sync
        def _(sync):
            sync.dma_start(gw1[:].rearrange("p (t g) -> p t g", t=ND),
                           bufs["gw1T"][:].rearrange("(t p) g -> p t g", p=128)
                           ).then_inc(sq.sem, 16)
            sync.dma_start(bufs["GW2"][:].rearrange("p (t e) -> p t e", t=2),
                           bufs["gw2T"][:].rearrange("(t p) e -> p t e", p=128)
                           ).then_inc(sq.sem, 16)
            sync.dma_start(bufs["GB1"][:], bufs["gb1c"][:]).then_inc(sq.sem, 16)
            sync.dma_start(bufs["GB2"][:], bufs["gb2r"][:]).then_inc(sq.sem, 16)
            for x in "qkvo":
                sync.dma_start(bufs["SROW"][x][:], bufs[f"fS{x}"][:]
                               ).then_inc(sq.sem, 16)
            sq.v += 8 * 16

        @blk.vector
        def _(v):
            for dt in range(ND):
                vg.inc(v.reduce_sum(bufs["POOL"][:, dt:dt + 1], XLN[dt][:], AX.X))

        @blk.tensor
        def _(t):
            t.wait_ge(vg.sem, vg0 + ND)
            t.wait_ge(sq.sem, sq0 + 8 * 16)
            for i in range(2):
                last = None
                for dt in range(ND):
                    last = t.matmul(PSB[0][:, i:i + 1],
                                    gw1[:, dt * 256 + i * 128:
                                        dt * 256 + (i + 1) * 128],
                                    bufs["POOL"][:, dt:dt + 1],
                                    start=(dt == 0), stop=(dt == ND - 1))
                tg.inc(last)

        @blk.scalar
        def _(s):
            s.wait_ge(sq.sem, sq0 + 8 * 16)
            s.wait_ge(tg.sem, tg0 + 2)
            for i in range(2):
                sg.inc(s.activation(bufs["HSB"][:, i:i + 1], PSB[0][:, i:i + 1],
                                    AF.Relu, bias=bufs["GB1"][:, i:i + 1],
                                    scale=1.0 / T))

        @blk.tensor
        def _(t):
            t.wait_ge(sg.sem, sg0 + 2)
            last = None
            for i in range(2):
                last = t.matmul(PSB[1][0:1, 0:8], bufs["HSB"][:, i:i + 1],
                                bufs["GW2"][:, i * 8:(i + 1) * 8],
                                start=(i == 0), stop=(i == 1))
            tg.inc(last)

        @blk.vector
        def _(v):
            v.wait_ge(tg.sem, tg0 + 3)
            v.wait_ge(sq.sem, sq0 + 8 * 16)
            LG = R[0:1, 0:8]
            P.vmark(v.tensor_add(LG, PSB[1][0:1, 0:8], bufs["GB2"][0:1, :]))
            P.vjoin(v)
            P.vmark(v.reduce_max(R[0:1, 8:9], LG, AX.X))
            P.vjoin(v)
            P.vmark(v.tensor_scalar(R[0:1, 16:24], LG, R[0:1, 8:9], None,
                                    ALU.subtract))
            P.vjoin(v)
            P.vmark(v.tensor_scalar(R[0:1, 24:32], R[0:1, 16:24], 0.0, None,
                                    ALU.is_ge))
            P.vjoin(v)
            P.vmark(v.scalar_tensor_tensor(R[0:1, 32:40], R[0:1, 24:32], -1e30,
                                           LG, ALU.mult, ALU.add))
            P.vjoin(v)
            P.vmark(v.reduce_max(R[0:1, 9:10], R[0:1, 32:40], AX.X))
            P.vjoin(v)
            vg.inc(v.tensor_scalar(R[0:1, 40:48], LG, R[0:1, 9:10], None,
                                   ALU.is_ge))

        @blk.scalar
        def _(s):
            s.wait_ge(vg.sem, vg0 + 9)
            sg.inc(s.activation(R[0:1, 48:56], R[0:1, 16:24], AF.Exp))

        @blk.vector
        def _(v):
            v.wait_ge(sg.sem, sg0 + 3)
            P.vmark(v.tensor_mul(R[0:1, 56:64], R[0:1, 48:56], R[0:1, 40:48]))
            P.vjoin(v)
            P.vmark(v.reduce_sum(R[0:1, 10:11], R[0:1, 56:64], AX.X))
            P.vjoin(v)
            P.vmark(v.reciprocal(R[0:1, 11:12], R[0:1, 10:11]))
            P.vjoin(v)
            vg.inc(v.tensor_scalar(R[0:1, 0:8], R[0:1, 56:64], R[0:1, 11:12],
                                   None, ALU.mult))

        @blk.tensor
        def _(t):
            t.wait_ge(vg.sem, vg0 + 10)
            tg.inc(t.transpose(PSB[2][0:8, 0:1], R[0:1, 0:8],
                               bufs["IDENT"][0:1, 0:1]))

        @blk.vector
        def _(v):
            v.wait_ge(tg.sem, tg0 + 4)
            vg.inc(v.tensor_copy(bufs["GCOL"][:], PSB[2][0:8, 0:1]))

        @blk.tensor
        def _(t):
            t.wait_ge(vg.sem, vg0 + 11)
            tg.inc(t.matmul(PSB[2][0:1, 0:128], bufs["GCOL"][:], bufs["SEL8"][:],
                            start=True, stop=True))

        @blk.vector
        def _(v):
            v.wait_ge(tg.sem, tg0 + 5)
            v.wait_ge(sq.sem, sq0 + 8 * 16)
            for x in "qkvo":
                vg.inc(v.tensor_mul(bufs["WROW"][x][:], PSB[2][0:1, 0:128],
                                    bufs["SROW"][x][:]))

        @blk.tensor
        def _(t):
            for i, x in enumerate("qkvo"):
                t.wait_ge(vg.sem, vg0 + 12 + i)
                if i >= 2:
                    t.wait_ge(vg.sem, vg0 + 16 + (i - 2))
                tg.inc(t.matmul(PSB[3 + (i % 2)][:, 0:128], bufs["ONESROW"][:],
                                bufs["WROW"][x][:], start=True, stop=True))

        @blk.vector
        def _(v):
            v.wait_ge(gq.sem, gq0 + 8 * 16)
            for i, x in enumerate("qkvo"):
                v.wait_ge(tg.sem, tg0 + 6 + i)
                ins = None
                for dt in range(ND):
                    ins = v.tensor_tensor(
                        bufs["VW"][x][:, dt * 128:(dt + 1) * 128],
                        VT[x][:, dt * 128:(dt + 1) * 128],
                        PSB[3 + (i % 2)][:, 0:128], ALU.mult)
                vg.inc(ins)

        @blk.tensor
        def _(t):
            n = 0
            for i, x in enumerate("qkv"):
                t.wait_ge(vg.sem, vg0 + 16 + i)
                for ch in range(NC):
                    n += 1
                    if n > 2:
                        t.wait_ge(vg.sem, vg0 + 19 + n - 2)
                    bank = 5 + (ch % 2)
                    last = None
                    for dt in range(ND):
                        last = t.matmul(PSB[bank][:],
                                        bufs["VW"][x][:, dt * 128:(dt + 1) * 128],
                                        XLN[dt][:, ch * CH:(ch + 1) * CH],
                                        start=(dt == 0), stop=(dt == ND - 1))
                    tg.inc(last)

        @blk.vector
        def _(v):
            n = 0
            for i, x in enumerate("qkv"):
                for ch in range(NC):
                    n += 1
                    v.wait_ge(tg.sem, tg0 + 9 + n)
                    bank = 5 + (ch % 2)
                    vg.inc(v.tensor_copy(bufs["XVS"][x][:, ch * CH:(ch + 1) * CH],
                                         PSB[bank][:]))


def emit_vprime(P, bufs, tag, wname, ffn):
    nc = P.nc
    XLN, VP, PSB = bufs["XLN"], bufs["VP"], bufs["PSB"]
    gq, tvp, vvp, sq = P.c("gq"), P.c("t_vp"), P.c("v_vp"), P.c("sq")
    gq0, tvp0, vvp0, sq0 = gq.v, tvp.v, vvp.v, sq.v

    with (nc.sbuf_tensor(f"wvb{tag}", [128, ND * 512], BF16) as WVB,
          nc.sbuf_tensor(f"vbb{tag}", [128, D], F32) as VBB,
          nc.Block(f"vp_{tag}") as blk):
        vbrow = bufs["DENS"][0:1, 0:D]

        @blk.gpsimd
        def _(g):
            for j in range(2):
                if j == 1:
                    g.wait_ge(tvp.sem, tvp0 + (1 if ffn else 0) + NT)
                g.dma_start(
                    WVB[:].rearrange("p (t o) -> p t o", t=ND),
                    bufs[wname][:, j * 512:(j + 1) * 512].rearrange(
                        "(t p) o -> p t o", p=128)
                ).then_inc(gq.sem, 16)
            gq.v += 32

        if ffn:
            @blk.sync
            def _(sync):
                sync.dma_start(vbrow, bufs["fbv_r"][:]
                               ).then_inc(sq.sem, 16)
                sq.v += 16

            @blk.tensor
            def _(t):
                t.wait_ge(sq.sem, sq0 + 16)
                t.matmul(PSB[6][:], bufs["ONESROW"][:],
                         vbrow[0:1, 0:512], start=True, stop=True)
                tvp.inc(t.matmul(PSB[7][:], bufs["ONESROW"][:],
                                 vbrow[0:1, 512:1024],
                                 start=True, stop=True))

        @blk.vector
        def _(v):
            if ffn:
                v.wait_ge(tvp.sem, tvp0 + 1)
                v.tensor_copy(VBB[:, 0:512], PSB[6][:])
                v.tensor_copy(VBB[:, 512:1024], PSB[7][:])
            for m in range(NT):
                ins = v.memset(
                    VP[m][:].rearrange("p (h c) -> p h c", c=65)[:, :, 64:65],
                    1.0)
            vvp.inc(ins)

        tbase = tvp.v

        @blk.tensor
        def _(t):
            t.wait_ge(vvp.sem, vvp0 + 1)
            n = 0
            for j in range(2):
                t.wait_ge(gq.sem, gq0 + 16 * (j + 1))
                for m in range(NT):
                    n += 1
                    if n > 2:
                        t.wait_ge(vvp.sem, vvp0 + 1 + n - 2)
                    bank = (n - 1) % 2
                    last = None
                    for dt in range(ND):
                        last = t.matmul(
                            PSB[bank][:],
                            XLN[dt][:, m * 128:(m + 1) * 128],
                            WVB[:, dt * 512:(dt + 1) * 512],
                            start=(dt == 0), stop=((not ffn) and dt == ND - 1))
                    if ffn:
                        last = t.matmul(PSB[bank][:],
                                        bufs["XVS"]["v"][:, m * 128:(m + 1) * 128],
                                        bufs["UP"]["v"][:, j * 512:(j + 1) * 512],
                                        start=False, stop=True)
                    tvp.inc(last)

        @blk.vector
        def _(v):
            n = 0
            for j in range(2):
                for m in range(NT):
                    n += 1
                    v.wait_ge(tvp.sem, tbase + n)
                    bank = (n - 1) % 2
                    dst = VP[m][:].rearrange("p (h c) -> p h c", c=65)[
                        :, j * 8:(j + 1) * 8, 0:64]
                    src3 = PSB[bank][:].rearrange("p (h c) -> p h c", c=64)
                    if ffn:
                        vbb3 = VBB[:, j * 512:(j + 1) * 512].rearrange(
                            "p (h c) -> p h c", c=64)
                        vvp.inc(v.tensor_add(dst, src3, vbb3))
                    else:
                        vvp.inc(v.tensor_copy(dst, src3))


def emit_attention(P, bufs, tag, wq_name, wk_name, causal, ffn, ring_slot):
    nc = P.nc
    XLN, VP, PSB, PSD = bufs["XLN"], bufs["VP"], bufs["PSB"], bufs["PSD"]
    RING, QP, KP, QSL, KSL = (bufs["RING"], bufs["QP"], bufs["KP"],
                              bufs["QSL"], bufs["KSL"])
    SQQ, SQK, DENS = bufs["SQQ"], bufs["SQK"], bufs["DENS"]
    ACT2, NEGC, R = bufs["ACT2"], bufs["NEGC"], bufs["GR"]
    DEN = PSB[6]
    softmax_scale = ASCALE if causal else 1.0
    ntk = (lambda c: 4 * c + 4) if causal else (lambda c: NT)

    tqk, vev, snc = P.c("t_qk"), P.c("v_ev"), P.c("s_nc")
    aq = [P.c("aq0"), P.c("aq1")]
    ts, ex, vm, pv = P.c("t_s"), P.c("s_ex"), P.c("v_m"), P.c("t_pv")
    vce, td = P.c("v_ce"), P.c("t_d")
    B = dict(tqk=tqk.v, vev=vev.v, snc=snc.v, ts=ts.v, ex=ex.v,
             vm=vm.v, pv=pv.v, vce=vce.v, td=td.v)
    aq0 = [aq[0].v, aq[1].v]

    ts_done, ex_done, vm_done = {}, {}, {}
    pv_done, exp_ring_wait, scores_bank_wait = {}, {}, {}
    last_scores = {}
    tsv, exv, vmv, pvv = B["ts"], B["ex"], B["vm"], B["pv"]
    slot_last_pv = {}
    hist = []
    for p in range(NP):
        for c in range(NC):
            n = ntk(c)
            for tk in range(n):
                hist.append((p, c, tk))
                g = len(hist) - 1
                scores_bank_wait[(p, c, tk)] = (
                    ex_done[hist[g - 2]] if g >= 2 else B["ex"])
                tsv += 1
                ts_done[(p, c, tk)] = tsv
                slot = ring_slot[(p, c, tk)]
                exp_ring_wait[(p, c, tk)] = slot_last_pv.get(slot, B["pv"])
                exv += 1
                ex_done[(p, c, tk)] = exv
                vmv += 1
                vm_done[(p, c, tk)] = vmv
                if (tk + 1) % 4 == 0 or tk == n - 1:
                    pvv += 1
                slot_last_pv[slot] = pvv
            pv_done[(p, c)] = pvv
        last_scores[p] = tsv

    with nc.Block(f"attn_{tag}") as blk:
        @blk.gpsimd
        def _(g):
            for p in range(NP):
                if p >= 2:
                    g.wait_ge(tqk.sem, B["tqk"] + 11 * (p - 1))
                for wn, bb in ((wq_name, QSL), (wk_name, KSL)):
                    aq[p % 2].inc(g.dma_start(
                        bb[p % 2][:].rearrange("p (t c) -> p t c", t=ND),
                        bufs[wn][:, p * 128:(p + 1) * 128].rearrange(
                            "(t p) c -> p t c", p=128)), 16)

        qk_state = {}

        def t_qk_group(t, p, g):
            if g == 0:
                t.wait_ge(aq[p % 2].sem, aq0[p % 2] + 32 * (p // 2 + 1))
            bank = 7
            if (p, g) != (0, 0):
                t.wait_ge(vev.sem, qk_state["free"])
            if g < 8:
                proj = "q" if g < 4 else "k"
                sl = QSL if g < 4 else KSL
                ch = g % 4
                last = None
                for dt in range(ND):
                    last = t.matmul(
                        PSB[bank][:],
                        sl[p % 2][:, dt * 128:(dt + 1) * 128],
                        XLN[dt][:, ch * CH:(ch + 1) * CH],
                        start=(dt == 0), stop=((not ffn) and dt == ND - 1))
                if ffn:
                    last = t.matmul(
                        PSB[bank][:],
                        bufs["UP"][proj][:, p * 128:(p + 1) * 128],
                        bufs["XVS"][proj][:, ch * CH:(ch + 1) * CH],
                        start=False, stop=True)
            elif g < 10:
                sq_i = SQQ if g == 8 else SQK
                t.wait_ge(vev.sem, B["vev"] + 13 * p + 9 + (g - 8))
                last = None
                for ch in range(NC):
                    last = t.matmul(PSB[bank][0:1, :], bufs["ONESB"][:],
                                    sq_i[:, ch * CH:(ch + 1) * CH],
                                    start=(ch == 0), stop=(ch == NC - 1))
            else:
                t.wait_ge(snc.sem, B["snc"] + p + 1)
                gb = 16 + 24 * (p % 2)
                last = t.matmul(PSB[bank][:, 0:1], bufs["ONESROW"][:],
                                R[0:1, gb + 3:gb + 4], start=True, stop=True)
            tqk.inc(last)
            consumed = {8: 11, 9: 12, 10: 13}.get(g, g + 1)
            qk_state["free"] = B["vev"] + 13 * p + consumed

        cad = 3 if causal else 4

        @blk.tensor
        def _(t):
            def emit_colify(p, c):
                t.wait_ge(vce.sem, B["vce"] + p * NC + c + 1)
                half = (p * NC + c) % 2
                last = None
                for h01 in range(2):
                    h = 2 * p + h01
                    for sub in range(4):
                        col = (h * NC + c) * 4 + sub
                        last = t.matmul(
                            DEN[:, col:col + 1],
                            DENS[0:1, half * 2 * CH + h01 * CH + sub * 128:
                                 half * 2 * CH + h01 * CH + (sub + 1) * 128],
                            bufs["IDENT"][0:1, 0:1], start=True, stop=True)
                td.inc(last)

            for g in range(11):
                t_qk_group(t, 0, g)
            col_pend = []
            TS_MAP = {0: 0, 1: 1, 2: 2, 3: 3, 4: 4, 5: 5, 6: 6, 7: 7,
                      8: 8, 11: 9, 13: 10}
            for p in range(NP):
                qkq = list(range(11)) if p + 1 < NP else []
                ci = [0]
                for c in range(NC):
                    n = ntk(c)
                    hA, hB = 2 * p, 2 * p + 1

                    def emit_pv(tkpv, first):
                        t.wait_ge(ex.sem, ex_done[(p, c, tkpv)])
                        if causal:
                            t.wait_ge(vm.sem, vm_done[(p, c, tkpv)])
                        if first and (p, c) != (0, 0):
                            t.wait_ge(vce.sem, B["vce"] + p * NC + c)
                        slot = ring_slot[(p, c, tkpv)]
                        t.matmul(PSB[4][0:65, :],
                                 VP[tkpv][:, hA * 65:hA * 65 + 65],
                                 RING[slot][:, 0:512], start=(tkpv == 0),
                                 stop=(tkpv == n - 1))
                        last = t.matmul(PSB[5][0:65, :],
                                        VP[tkpv][:, hB * 65:hB * 65 + 65],
                                        RING[slot][:, 512:1024],
                                        start=(tkpv == 0), stop=(tkpv == n - 1))
                        if (tkpv + 1) % 4 == 0 or tkpv == n - 1:
                            pv.inc(last)

                    pend = []
                    for tk in range(n):
                        t.wait_ge(ex.sem, scores_bank_wait[(p, c, tk)])
                        ks = slice(tk * 128, (tk + 1) * 128)
                        cs = slice(c * CH, (c + 1) * CH)
                        pd = PSD[(ts_done[(p, c, tk)] - 1 - B["ts"]) % 2]
                        t.matmul(pd[:, 0:512], KP[p % 2][0:64, ks],
                                 QP[p % 2][0:64, cs], start=True,
                                 stop=True, tile_position=(0, 0))
                        ts.inc(t.matmul(pd[:, 512:1024], KP[p % 2][64:128, ks],
                                        QP[p % 2][64:128, cs], start=True,
                                        stop=True, tile_position=(64, 0)))
                        pend.append(tk)
                        if len(pend) > 2:
                            emit_pv(pend.pop(0), first=(tk == 2))
                        if tk == 0 and col_pend:
                            emit_colify(*col_pend.pop(0))
                        if tk % cad == cad - 1 and p + 1 < NP:
                            g = TS_MAP.get(ci[0])
                            ci[0] += 1
                            if g is not None and g in qkq:
                                qkq.remove(g)
                                t_qk_group(t, p + 1, g)
                    for tkpv in pend:
                        emit_pv(tkpv, first=(tkpv == 0))
                    if c == NC - 1:
                        while qkq:
                            t_qk_group(t, p + 1, qkq.pop(0))
                    col_pend.append((p, c))
            while col_pend:
                emit_colify(*col_pend.pop(0))

        @blk.scalar
        def _(s):
            def negc_math(pp):
                gb = 16 + 24 * (pp % 2)
                s.wait_ge(vev.sem, B["vev"] + 13 * pp + 12)
                if pp >= 2:
                    s.wait_ge(tqk.sem, B["tqk"] + 11 * (pp - 2) + 11)
                P.smark(s.activation(R[0:1, gb + 1:gb + 2], R[0:1, gb:gb + 1],
                                     AF.Ln))
                P.sjoin(s)
                P.smark(s.activation(R[0:1, gb + 2:gb + 3],
                                     R[0:1, gb + 1:gb + 2], AF.Exp, scale=0.5))
                P.sjoin(s)
                snc.inc(s.mul(R[0:1, gb + 3:gb + 4], R[0:1, gb + 2:gb + 3],
                              -softmax_scale))

            negc_math(0)
            for p in range(NP):
                si = [0]
                done_negc = [p + 1 >= NP]
                s.wait_ge(vev.sem, B["vev"] + 13 * (p + 1))
                for c in range(NC):
                    n = ntk(c)
                    for tk in range(n):
                        s.wait_ge(ts.sem, ts_done[(p, c, tk)])
                        if exp_ring_wait[(p, c, tk)] > B["pv"]:
                            s.wait_ge(pv.sem, exp_ring_wait[(p, c, tk)])
                        slot = ring_slot[(p, c, tk)]
                        pd = PSD[(ts_done[(p, c, tk)] - 1 - B["ts"]) % 2]
                        ex.inc(s.activation(RING[slot][:], pd[:], AF.Exp,
                                            bias=NEGC[:, p:p + 1],
                                            scale=softmax_scale))
                        if tk % cad == cad - 1:
                            if si[0] == 12 and not done_negc[0]:
                                negc_math(p + 1)
                                done_negc[0] = True
                            si[0] += 1
                if not done_negc[0]:
                    negc_math(p + 1)

        ev_state = {}

        def v_evgroup(v, p, g):
            gbp = 16 + 24 * (p % 2)
            if g < 8:
                v.wait_ge(tqk.sem, B["tqk"] + 11 * p + g + 1)
                if p >= 2 and g == 0:
                    v.wait_ge(ts.sem, last_scores[p - 2])
                proj = "q" if g < 4 else "k"
                ch = g % 4
                dstb = QP if proj == "q" else KP
                cs = slice(ch * CH, (ch + 1) * CH)
                if ffn and proj == "q":
                    ins = v.tensor_scalar(dstb[p % 2][:, cs], PSB[7][:],
                                          bufs["FBQ"][:, p:p + 1], ASCALE,
                                          ALU.add, ALU.mult)
                elif ffn:
                    ins = v.tensor_scalar(dstb[p % 2][:, cs], PSB[7][:],
                                          bufs["FBK"][:, p:p + 1], None,
                                          ALU.add)
                else:
                    ins = v.tensor_copy(dstb[p % 2][:, cs], PSB[7][:])
                vev.inc(ins)
            elif g == 8:
                if p >= 1:
                    v.wait_ge(tqk.sem, B["tqk"] + 11 * (p - 1) + 10)
                v.wait_ge(vev.sem, vev.v)
                vev.inc(v.tensor_mul(SQQ[:], QP[p % 2][:], QP[p % 2][:]))
            elif g == 9:
                v.wait_ge(tqk.sem, B["tqk"] + 11 * p + 9)
                vev.inc(v.tensor_mul(SQK[:], KP[p % 2][:], KP[p % 2][:]))
            elif g == 10:
                v.wait_ge(tqk.sem, B["tqk"] + 11 * p + 9)
                if p >= 2:
                    v.wait_ge(snc.sem, B["snc"] + p - 1)
                vev.inc(v.reduce_max(R[0:1, gbp + 4:gbp + 5], PSB[7][0:1, :],
                                     AX.X))
            elif g == 11:
                v.wait_ge(tqk.sem, B["tqk"] + 11 * p + 10)
                P.vmark(v.reduce_max(R[0:1, gbp + 5:gbp + 6], PSB[7][0:1, :],
                                     AX.X))
                P.vjoin(v)
                vev.inc(v.tensor_mul(R[0:1, gbp:gbp + 1],
                                     R[0:1, gbp + 4:gbp + 5],
                                     R[0:1, gbp + 5:gbp + 6]))
            elif g == 12:
                v.wait_ge(tqk.sem, B["tqk"] + 11 * p + 11)
                vev.inc(v.tensor_copy(NEGC[:, p:p + 1], PSB[7][:, 0:1]))

        @blk.vector
        def _(v):
            for g in range(13):
                v_evgroup(v, 0, g)
            VS_MAP = {0: 0, 1: 1, 2: 2, 3: 3, 4: 4, 5: 5, 6: 6, 7: 7,
                      8: 8, 9: 9, 10: 10, 11: 11, 14: 12}
            for p in range(NP):
                evq = list(range(13)) if p + 1 < NP else []
                vi = [0]
                for c in range(NC):
                    n = ntk(c)
                    hA, hB = 2 * p, 2 * p + 1
                    for tk in range(n):
                        if causal:
                            v.wait_ge(ex.sem, ex_done[(p, c, tk)])
                            j = tk - 4 * c
                            slot = ring_slot[(p, c, tk)]
                            if 0 <= j <= 3:
                                ms = bufs["MASKS"][:, j * 1024:(j + 1) * 1024]
                                vm.inc(v.tensor_mul(RING[slot][:], RING[slot][:],
                                                    ms))
                            else:
                                vm.inc(v.engine_nop())
                        if tk % cad == cad - 1 and p + 1 < NP:
                            g = VS_MAP.get(vi[0])
                            vi[0] += 1
                            if g is not None and g in evq:
                                evq.remove(g)
                                v_evgroup(v, p + 1, g)
                    v.wait_ge(pv.sem, pv_done[(p, c)])
                    cc = p * NC + c
                    if cc >= 2:
                        v.wait_ge(td.sem, B["td"] + cc - 1)
                    half = cc % 2
                    v.tensor_copy(DENS[0:1, half * 2 * CH:half * 2 * CH + CH],
                                  PSB[4][64:65, :])
                    v.tensor_copy(DENS[0:1, half * 2 * CH + CH:
                                       half * 2 * CH + 2 * CH],
                                  PSB[5][64:65, :])
                    v.tensor_copy(ACT2[hA // 2][(hA % 2) * 64:(hA % 2) * 64 + 64,
                                                c * CH:(c + 1) * CH],
                                  PSB[4][0:64, :])
                    vce.inc(v.tensor_copy(
                        ACT2[hB // 2][(hB % 2) * 64:(hB % 2) * 64 + 64,
                                      c * CH:(c + 1) * CH], PSB[5][0:64, :]))
                    if c == NC - 1:
                        while evq:
                            v_evgroup(v, p + 1, evq.pop(0))



def emit_ctxnorm(P, bufs, tag):
    nc = P.nc
    PSB, ACT2 = bufs["PSB"], bufs["ACT2"]
    tn, vn = P.c("t_nb"), P.c("v_nm")
    tn0, vn0 = tn.v, vn.v

    with nc.Block(f"cn_{tag}") as blk:
        @blk.vector
        def _(v):
            vn.inc(v.reciprocal(bufs["RECIP"][:], PSB[6][:, 0:256]))

        @blk.tensor
        def _(t):
            t.wait_ge(vn.sem, vn0 + 1)
            i = 0
            for h in range(NH):
                for c in range(NC):
                    bank = i % 2
                    if i >= 2:
                        t.wait_ge(vn.sem, vn0 + 1 + i - 1)
                    last = None
                    for sub in range(4):
                        col = (h * NC + c) * 4 + sub
                        last = t.matmul(
                            PSB[bank][0:64, sub * 128:(sub + 1) * 128],
                            bufs["RECIP"][:, col:col + 1].broadcast_to([128, 64]),
                            bufs["IDENT"][:], start=True, stop=True)
                    tn.inc(last)
                    i += 1

        @blk.vector
        def _(v):
            i = 0
            for h in range(NH):
                for c in range(NC):
                    v.wait_ge(tn.sem, tn0 + i + 1)
                    bank = i % 2
                    sl = ACT2[h // 2][(h % 2) * 64:(h % 2) * 64 + 64,
                                      c * CH:(c + 1) * CH]
                    vn.inc(v.tensor_tensor(sl, sl, PSB[bank][0:64, :], ALU.mult))
                    i += 1


def emit_oproj(P, bufs, tag, wo_name, src_res_name, dst_name, ffn,
               do_cn=False, ln_src=None, ln_tag=None):
    nc = P.nc
    ACT2, PSB = bufs["ACT2"], bufs["PSB"]
    src_res = bufs[src_res_name]
    dst = bufs[dst_name]
    to, vo, gq, gqb = P.c("t_o"), P.c("v_o"), P.c("gq"), P.c("gqb")
    oq = [P.c("oq0"), P.c("oq1")]
    so = [P.c(f"sto{i}") for i in range(4)]
    to0, vo0, gq0, gqb0 = to.v, vo.v, gq.v, gqb.v
    oq0 = [oq[0].v, oq[1].v]
    so0 = [s.v for s in so]

    with (nc.sbuf_tensor(f"wob{tag}", [128, ND * D], BF16)) as WOB, (
          nc.sbuf_tensor(f"xo{tag}0", [128, T], F32)) as xo0, (
          nc.sbuf_tensor(f"xo{tag}1", [128, T], F32)) as xo1, (
          nc.sbuf_tensor(f"xn{tag}0", [128, T], F32)) as xn0, (
          nc.sbuf_tensor(f"xn{tag}1", [128, T], F32)) as xn1, (
          nc.sbuf_tensor(f"xn{tag}2", [128, T], F32)) as xn2, (
          nc.sbuf_tensor(f"xn{tag}3", [128, T], F32)) as xn3, (
          ExitStack()) as les, (
          nc.Block(f"o_{tag}")) as blk:
        XOLD = [xo0, xo1]
        XNEW = [xn0, xn1, xn2, xn3]
        ln = None
        if ln_src is not None:
            lt = ln_tag
            ln_scr = (
                [les.enter_context(nc.sbuf_tensor(f"lnsq{lt}{i}", [128, T],
                                                  BF16)) for i in range(4)],
                xo0, xo1,
                les.enter_context(nc.sbuf_tensor(f"lnra{lt}", [1, T], F32)),
                les.enter_context(nc.sbuf_tensor(f"lnrb{lt}", [1, T], F32)),
                les.enter_context(nc.sbuf_tensor(f"lnrc{lt}", [1, T], F32)),
            )
            ln = LN(P, bufs, ln_scr, fused=True)

        nxvso_holder = [NC if ffn else 0]
        tn, vn = P.c("t_nb"), P.c("v_nm")
        tn0, vn0 = tn.v, vn.v

        if do_cn:
            @blk.vector
            def _(v):
                P.vmark(v.reciprocal(bufs["RECIP"][:], PSB[6][:, 0:256]))
                P.vjoin(v)
                P.vmark(v.tensor_copy(bufs["RECIPH"][:], bufs["RECIP"][:]))
                P.vjoin(v)
                vn.inc(v.tensor_sub(bufs["RECIPL"][:], bufs["RECIP"][:],
                                    bufs["RECIPH"][:]))

            @blk.tensor
            def _(t):
                t.wait_ge(vn.sem, vn0 + 1)
                i = 0
                for h in range(NH):
                    for c in range(NC):
                        bank = i % 2
                        if i >= 2:
                            t.wait_ge(vn.sem, vn0 + 1 + i - 1)
                        last = None
                        for sub in range(4):
                            col = (h * NC + c) * 4 + sub
                            for rp in (bufs["RECIPH"], bufs["RECIPL"]):
                                last = t.matmul(
                                    PSB[bank][0:64, sub * 128:(sub + 1) * 128],
                                    rp[:, col:col + 1].broadcast_to([128, 64]),
                                    bufs["IDENTB"][:],
                                    start=(rp is bufs["RECIPH"]),
                                    stop=(rp is bufs["RECIPL"]))
                        tn.inc(last)
                        i += 1

            @blk.vector
            def _(v):
                i = 0
                for h in range(NH):
                    for c in range(NC):
                        v.wait_ge(tn.sem, tn0 + i + 1)
                        bank = i % 2
                        sl = ACT2[h // 2][(h % 2) * 64:(h % 2) * 64 + 64,
                                          c * CH:(c + 1) * CH]
                        vn.inc(v.tensor_tensor(sl, sl, PSB[bank][0:64, :],
                                               ALU.mult))
                        i += 1
        ncn = vn.v - vn0

        @blk.gpsimd
        def _(g):
            for j, ctr in enumerate((gq, gqb)):
                ctr.inc(g.dma_start(
                    WOB[:].rearrange("p (t o) -> p t o", t=ND)[
                        :, :, j * 512:(j + 1) * 512],
                    bufs[wo_name][:, j * 512:(j + 1) * 512].rearrange(
                        "(t p) o -> p t o", p=128)), 16)

        def _store(sync, dt2):
            sync.wait_ge(vo.sem, vo0 + nxvso_holder[0] + NC * (dt2 + 1))
            if dt2 >= 4:
                sync.wait_ge(so[dt2 % 4].sem, so0[dt2 % 4] + 16 * (dt2 // 4))
            so[dt2 % 4].inc(
                sync.dma_start(dst[dt2 * 128:(dt2 + 1) * 128, :],
                               XNEW[dt2 % 4][:]), 16)

        @blk.sync
        def _(sync):
            for s in so:
                sync.wait_ge(s.sem, s.v)
            for dt2 in range(ND):
                if dt2 >= 2:
                    sync.wait_ge(vo.sem, vo0 + nxvso_holder[0] +
                                 NC * (dt2 - 1))
                oq[dt2 % 2].inc(
                    sync.dma_start(XOLD[dt2 % 2][:],
                                   src_res[dt2 * 128:(dt2 + 1) * 128, :]), 16)
                if dt2 >= 1:
                    _store(sync, dt2 - 1)
            _store(sync, ND - 1)
            for i in range(4):
                sync.wait_ge(so[i].sem, so0[i] + 16 * (ND // 4))


        nxvso = 0
        if ffn:
            @blk.tensor
            def _(t):
                t.wait_ge(vn.sem, vn0 + ncn)
                for ch in range(NC):
                    bank = 4 + (ch % 2)
                    if ch >= 2:
                        t.wait_ge(vo.sem, vo0 + ch - 1)
                    last = None
                    for dt in range(ND):
                        last = t.matmul(
                            PSB[bank][:],
                            bufs["VW"]["o"][:, dt * 128:(dt + 1) * 128],
                            ACT2[dt][:, ch * CH:(ch + 1) * CH],
                            start=(dt == 0), stop=(dt == ND - 1))
                    to.inc(last)

            @blk.vector
            def _(v):
                for ch in range(NC):
                    v.wait_ge(to.sem, to0 + ch + 1)
                    bank = 4 + (ch % 2)
                    vo.inc(v.tensor_copy(
                        bufs["XVS"]["o"][:, ch * CH:(ch + 1) * CH], PSB[bank][:]))
            nxvso = NC

        @blk.tensor
        def _(t):
            t.wait_ge(gq.sem, gq0 + 16)
            t.wait_ge(vn.sem, vn0 + ncn)
            if ffn:
                t.wait_ge(vo.sem, vo0 + NC)
            ng = 0
            for dt2 in range(ND):
                if dt2 == 4:
                    t.wait_ge(gqb.sem, gqb0 + 16)
                if ln is not None and dt2 >= 2:
                    ln.colsums(t, dt2 - 2)
                for ch in range(NC):
                    ng += 1
                    if ng > 2:
                        t.wait_ge(vo.sem, vo0 + nxvso + ng - 2)
                    bank = (ng - 1) % 2
                    last = None
                    for dt in range(ND):
                        last = t.matmul(
                            PSB[bank][:],
                            WOB[:, dt * D + dt2 * 128:dt * D + (dt2 + 1) * 128],
                            ACT2[dt][:, ch * CH:(ch + 1) * CH],
                            start=(dt == 0), stop=((not ffn) and dt == ND - 1))
                    if ffn:
                        last = t.matmul(
                            PSB[bank][:],
                            bufs["UP"]["o"][:, dt2 * 128:(dt2 + 1) * 128],
                            bufs["XVS"]["o"][:, ch * CH:(ch + 1) * CH],
                            start=False, stop=True)
                    to.inc(last)
            if ln is not None:
                ln.colsums(t, ND - 2)
                ln.colsums(t, ND - 1)

        @blk.vector
        def _(v):
            ng = 0
            for dt2 in range(ND):
                v.wait_ge(oq[dt2 % 2].sem, oq0[dt2 % 2] + 16 * (dt2 // 2 + 1))
                if dt2 >= 4:
                    v.wait_ge(so[dt2 % 4].sem, so0[dt2 % 4] + 16 * (dt2 // 4))
                    if ln is not None:
                        v.wait_ge(ln.lns.sem, ln.lns0 + 2 * (dt2 - 3))
                for ch in range(NC):
                    ng += 1
                    v.wait_ge(to.sem, to0 + nxvso + ng)
                    bank = (ng - 1) % 2
                    cs = slice(ch * CH, (ch + 1) * CH)
                    if ffn:
                        ins = v.scalar_tensor_tensor(
                            XNEW[dt2 % 4][:, cs], PSB[bank][:],
                            bufs["FBO"][:, dt2:dt2 + 1], XOLD[dt2 % 2][:, cs],
                            ALU.add, ALU.add)
                    else:
                        ins = v.tensor_add(XNEW[dt2 % 4][:, cs], PSB[bank][:],
                                           XOLD[dt2 % 2][:, cs])
                    vo.inc(ins)




        if ln is not None:
            @blk.scalar
            def _(s_):
                for dt in range(ND):
                    s_.wait_ge(vo.sem, vo0 + nxvso_holder[0] + NC * (dt + 1))
                    ln.cast_tile(s_, dt, XNEW[dt % 4])
                ln.scalar_rows(s_)

            @blk.vector
            def _(v):
                ln.vector_rows_norm(v)

            @blk.tensor
            def _(t):
                ln.tensor_bcasts(
                    t, pre=lambda t_: t_.wait_ge(
                        vo.sem, vo0 + nxvso_holder[0] + NC * ND))


def emit_attnb(P, bufs):
    nc = P.nc
    XLN, PSB, ACT2 = bufs["XLN"], bufs["PSB"], bufs["ACT2"]
    gq, gqb, sq = P.c("gq"), P.c("gqb"), P.c("sq")

    with ExitStack() as es:
        QT = [es.enter_context(nc.sbuf_tensor(f"bq{i}", [128, T], BF16))
              for i in range(ND)]
        WB = es.enter_context(nc.sbuf_tensor("bwb", [128, ND * D], BF16))
        LAMC = es.enter_context(nc.sbuf_tensor("lamc", [128, 8], F32))

        tp, vp2, sp = P.c("t_bp"), P.c("v_bp"), P.c("s_bp")
        gq0, sq0, tp0, vp0, sp0 = gq.v, sq.v, tp.v, vp2.v, sp.v
        gq0b = gqb.v
        with (nc.sbuf_tensor("lamr", [1, 16], F32) as LAMR,
              nc.sbuf_tensor("lamt", [16, 1], F32) as LAMT,
              nc.Block("bq") as blk):

            @blk.gpsimd
            def _(g):
                for j, ctr in enumerate((gq, gqb)):
                    ctr.inc(g.dma_start(
                        WB[:].rearrange("p (t o) -> p t o", t=ND)[
                            :, :, j * 512:(j + 1) * 512],
                        bufs["bWqT"][:, j * 512:(j + 1) * 512].rearrange(
                            "(t p) o -> p t o", p=128)), 16)

            @blk.sync
            def _(sync):
                sync.dma_start(LAMR[:], bufs["bbeta"][:]).then_inc(sq.sem, 16)
                sync.dma_start(bufs["BGAM"][:], bufs["bgam"][:]
                               ).then_inc(sq.sem, 16)
                sync.dma_start(bufs["BBET"][:], bufs["bbet"][:]
                               ).then_inc(sq.sem, 16)
                sq.v += 3 * 16

            @blk.tensor
            def _(t):
                t.wait_ge(gq.sem, gq0 + 16)
                ng = 0
                for dt2 in range(ND):
                    if dt2 == 4:
                        t.wait_ge(gqb.sem, gq0b + 16)
                    for ch in range(NC):
                        ng += 1
                        if ng > 3:
                            t.wait_ge(vp2.sem, vp0 + ng - 3)
                        bank = (ng - 1) % 3
                        last = None
                        for dt in range(ND):
                            last = t.matmul(
                                PSB[bank][:],
                                WB[:, dt * D + dt2 * 128:dt * D + (dt2 + 1) * 128],
                                XLN[dt][:, ch * CH:(ch + 1) * CH],
                                start=(dt == 0), stop=(dt == ND - 1))
                        tp.inc(last)

            @blk.vector
            def _(v):
                ng = 0
                for dt2 in range(ND):
                    for ch in range(NC):
                        ng += 1
                        v.wait_ge(tp.sem, tp0 + ng)
                        bank = (ng - 1) % 3
                        vp2.inc(v.tensor_copy(
                            QT[dt2][:, ch * CH:(ch + 1) * CH], PSB[bank][:]))

            @blk.scalar
            def _(s):
                s.wait_ge(sq.sem, sq0 + 3 * 16)
                sp.inc(s.activation(LAMR[:], LAMR[:], AF.Sigmoid))

            @blk.tensor
            def _(t):
                t.wait_ge(sp.sem, sp0 + 1)
                tp.inc(t.transpose(PSB[3][0:16, 0:1], LAMR[:],
                                   bufs["IDENT"][0:1, 0:1]))

            @blk.vector
            def _(v):
                v.wait_ge(tp.sem, tp0 + 33)
                vp2.inc(v.tensor_copy(LAMT[:], PSB[3][0:16, 0:1]))

            @blk.tensor
            def _(t):
                t.wait_ge(vp2.sem, vp0 + 33)
                last = None
                for dt in range(ND):
                    last = t.matmul(PSB[4][:, dt:dt + 1],
                                    bufs["SELH"][:, dt * 128:(dt + 1) * 128],
                                    LAMT[:], start=True, stop=True)
                tp.inc(last)

            @blk.vector
            def _(v):
                v.wait_ge(tp.sem, tp0 + 34)
                vp2.inc(v.tensor_copy(LAMC[:], PSB[4][:, 0:8]))

        tv, vv = P.c("t_bv"), P.c("v_bv")
        gq1, tv0, vv0, gq1b = gq.v, tv.v, vv.v, gqb.v
        with (nc.sbuf_tensor("bvt", [128, T], BF16) as VB,
              nc.sbuf_tensor("bscn", [128, T], F32) as SCN,
              nc.Block("bv") as blk):

            @blk.gpsimd
            def _(g):
                for j, ctr in enumerate((gq, gqb)):
                    ctr.inc(g.dma_start(
                        WB[:].rearrange("p (t o) -> p t o", t=ND)[
                            :, :, j * 512:(j + 1) * 512],
                        bufs["bWvT"][:, j * 512:(j + 1) * 512].rearrange(
                            "(t p) o -> p t o", p=128)), 16)

            @blk.tensor
            def _(t):
                t.wait_ge(gq.sem, gq1 + 16)
                ng = 0
                for dt2 in range(ND):
                    if dt2 == 4:
                        t.wait_ge(gqb.sem, gq1b + 16)
                    for ch in range(NC):
                        ng += 1
                        if ng > 3:
                            t.wait_ge(vv.sem, vv0 + (ng - 3) + (ng - 3) // 4 * 2)
                        bank = (ng - 1) % 3
                        last = None
                        for dt in range(ND):
                            last = t.matmul(
                                PSB[bank][:],
                                WB[:, dt * D + dt2 * 128:dt * D + (dt2 + 1) * 128],
                                XLN[dt][:, ch * CH:(ch + 1) * CH],
                                start=(dt == 0), stop=(dt == ND - 1))
                        tv.inc(last)

            @blk.vector
            def _(v):
                ng = 0
                for dt2 in range(ND):
                    v.wait_ge(vv.sem, vv.v)
                    for ch in range(NC):
                        ng += 1
                        v.wait_ge(tv.sem, tv0 + ng)
                        bank = (ng - 1) % 3
                        vv.inc(v.tensor_copy(VB[:, ch * CH:(ch + 1) * CH],
                                             PSB[bank][:]))
                    v.wait_ge(vv.sem, vv.v)
                    vv.inc(v.tensor_tensor_scan(
                        SCN[:], LAMC[:, dt2:dt2 + 1].broadcast_to([128, T]),
                        VB[:], 0.0, ALU.mult, ALU.add))
                    v.wait_ge(vv.sem, vv.v)
                    vv.inc(v.tensor_tensor(ACT2[dt2][:], QT[dt2][:], SCN[:],
                                           ALU.mult))

        sy = P.c("s_y")
        with (nc.sbuf_tensor("bysq0", [128, T], BF16) as ysq0,
              nc.sbuf_tensor("bysq1", [128, T], BF16) as ysq1,
              nc.sbuf_tensor("byt10", [128, T], F32) as t1a,
              nc.sbuf_tensor("byt11", [128, T], F32) as t1b,
              nc.sbuf_tensor("byra", [1, T], F32) as rowa,
              nc.sbuf_tensor("byrb", [1, T], F32) as rowb,
              nc.sbuf_tensor("byrc", [1, T], F32) as rowc,
              nc.Block("byln") as blk):
            yln = LN(P, bufs, ([ysq0, ysq1], t1a, t1b, rowa, rowb, rowc),
                     fused=False, target=ACT2, copy=False)
            sy0 = sy.v

            @blk.scalar
            def _(s_):
                for dt in range(ND):
                    yln.cast_tile(s_, dt, ACT2[dt])
                yln.scalar_rows(s_)
                for ch in range(NC):
                    s_.wait_ge(yln.vn.sem, yln.vn0 + ch + 1)
                    cs = slice(ch * CH, (ch + 1) * CH)
                    for dt in range(ND):
                        sy.inc(s_.activation(ACT2[dt][:, cs], ACT2[dt][:, cs],
                                             AF.Identity,
                                             bias=bufs["BBET"][:, dt:dt + 1],
                                             scale=bufs["BGAM"][:, dt:dt + 1]))

            @blk.vector
            def _(v):
                yln.vector_rows_norm(v)

            @blk.tensor
            def _(t):
                for dt in range(ND):
                    yln.colsums(t, dt)
                yln.tensor_bcasts(t)

        tg2, vg2, sg2 = P.c("t_bg"), P.c("v_bg"), P.c("s_bg")
        gq2, tg20, vg20, sg20, gq2b = gq.v, tg2.v, vg2.v, sg2.v, gqb.v
        with (nc.sbuf_tensor("bgt0", [128, T], BF16) as gt0,
              nc.sbuf_tensor("bgt1", [128, T], BF16) as gt1,
              nc.Block("bg") as blk):
            GTt = [gt0, gt1]

            @blk.gpsimd
            def _(g):
                for j, ctr in enumerate((gq, gqb)):
                    ctr.inc(g.dma_start(
                        WB[:].rearrange("p (t o) -> p t o", t=ND)[
                            :, :, j * 512:(j + 1) * 512],
                        bufs["bWgT"][:, j * 512:(j + 1) * 512].rearrange(
                            "(t p) o -> p t o", p=128)), 16)

            @blk.tensor
            def _(t):
                t.wait_ge(gq.sem, gq2 + 16)
                ng = 0
                for dt2 in range(ND):
                    if dt2 == 4:
                        t.wait_ge(gqb.sem, gq2b + 16)
                    for ch in range(NC):
                        ng += 1
                        if ng > 3:
                            t.wait_ge(sg2.sem, sg20 + ng - 3)
                        bank = (ng - 1) % 3
                        last = None
                        for dt in range(ND):
                            last = t.matmul(
                                PSB[bank][:],
                                WB[:, dt * D + dt2 * 128:dt * D + (dt2 + 1) * 128],
                                XLN[dt][:, ch * CH:(ch + 1) * CH],
                                start=(dt == 0), stop=(dt == ND - 1))
                        tg2.inc(last)

            @blk.scalar
            def _(s):
                ng = 0
                for dt2 in range(ND):
                    if dt2 >= 2:
                        s.wait_ge(vg2.sem, vg20 + dt2 - 1)
                    for ch in range(NC):
                        ng += 1
                        s.wait_ge(tg2.sem, tg20 + ng)
                        bank = (ng - 1) % 3
                        sg2.inc(s.activation(GTt[dt2 % 2][:, ch * CH:(ch + 1) * CH],
                                             PSB[bank][:], AF.Silu))

            @blk.vector
            def _(v):
                for dt2 in range(ND):
                    v.wait_ge(sg2.sem, sg20 + NC * (dt2 + 1))
                    vg2.inc(v.tensor_mul(ACT2[dt2][:], ACT2[dt2][:],
                                         GTt[dt2 % 2][:]))



def build_program(nsub=4):
    nc = bass.Bass("TRN2", target_bir_lowering=False, debug=False)
    P = Prog(nc)
    bufs = {}

    def din(name, shape, dtype=F32):
        bufs[name] = nc.dram_tensor(name, shape, dtype, kind="ExternalInput")

    din("xT", [D, T])
    for w in ("aWqT", "aWkT", "aWvT", "aWoT", "bWqT", "bWvT", "bWgT", "bWoT",
              "fWqT", "fWkT", "fWvT", "fWoT"):
        din(w, [D, D])
    for x in "qkvo":
        din(f"fV{x}", [D, 128])
        din(f"fU{x}", [128, D])
        din(f"fS{x}", [1, 128])
    din("fbq_c", [128, 8]); din("fbk_c", [128, 8]); din("fbo_c", [128, 8])
    din("fbv_r", [1, D])
    din("gw1T", [D, 256]); din("gb1c", [128, 2]); din("gw2T", [256, 8])
    din("gb2r", [1, 8])
    din("bbeta", [1, 16]); din("bgam", [128, 8]); din("bbet", [128, 8])
    din("IDENT", [128, 128]); din("IDENTB", [128, 128], BF16)
    din("ONESB", [128, 1], BF16)
    din("ONESROW", [1, 128]); din("MASKS", [128, 4 * 1024], BF16)
    din("SELH", [16, D]); din("SEL8", [8, 128])

    bufs["y"] = nc.dram_tensor("y", [D, T], F32, kind="ExternalOutput")
    bufs["Xs"] = nc.dram_tensor("Xs", [D, T], F32, kind="Internal")

    for cn, shape, dt in (("IDENT", [128, 128], F32),
                          ("IDENTB", [128, 128], BF16),
                          ("ONESB", [128, 1], BF16),
                          ("ONESROW", [1, 128], F32),
                          ("MASKS", [128, 4 * 1024], BF16),
                          ("SELH", [16, D], F32), ("SEL8", [8, 128], F32)):
        sb = nc.alloc_sbuf_tensor("c_" + cn, shape, dt)
        bufs[cn + "_dram"] = bufs[cn]
        bufs[cn] = sb
    bufs["XLN"] = [nc.alloc_sbuf_tensor(f"xln{i}", [128, T], BF16)
                   for i in range(ND)]
    bufs["ACT2"] = [nc.alloc_sbuf_tensor(f"act2{i}", [128, T], BF16)
                    for i in range(ND)]
    bufs["GR"] = nc.alloc_sbuf_tensor("grow", [1, 64], F32)
    bufs["NEGC"] = nc.alloc_sbuf_tensor("negcb", [128, NP], F32)
    bufs["RECIP"] = nc.alloc_sbuf_tensor("recip", [128, 256], F32)
    bufs["RECIPH"] = nc.alloc_sbuf_tensor("reciph", [128, 256], BF16)
    bufs["RECIPL"] = nc.alloc_sbuf_tensor("recipl", [128, 256], BF16)
    psd = [nc.alloc_psum_tensor(f"psd{i}", [128, 1024], F32)
           for i in range(2)]
    bufs["PSD"] = [h[:] for h in psd]
    bufs["PSB"] = [psd[0][:, 0:512], psd[0][:, 512:1024],
                   psd[1][:, 0:512], psd[1][:, 512:1024]] + [
        nc.alloc_psum_tensor(f"psb{i}", [128, 512], F32)[:] for i in range(4, 8)]
    bufs["GW2"] = nc.alloc_sbuf_tensor("gw2", [128, 16], F32)
    bufs["GB1"] = nc.alloc_sbuf_tensor("gb1", [128, 2], F32)
    bufs["GB2"] = nc.alloc_sbuf_tensor("gb2", [1, 8], F32)
    bufs["POOL"] = nc.alloc_sbuf_tensor("pool", [128, 8], F32)
    bufs["HSB"] = nc.alloc_sbuf_tensor("hsb", [128, 2], F32)
    bufs["GCOL"] = nc.alloc_sbuf_tensor("gcol", [8, 1], F32)
    bufs["FBQ"] = nc.alloc_sbuf_tensor("fbq", [128, 8], F32)
    bufs["FBK"] = nc.alloc_sbuf_tensor("fbk", [128, 8], F32)
    bufs["FBO"] = nc.alloc_sbuf_tensor("fbo", [128, 8], F32)
    bufs["BGAM"] = nc.alloc_sbuf_tensor("bgam_sb", [128, 8], F32)
    bufs["BBET"] = nc.alloc_sbuf_tensor("bbet_sb", [128, 8], F32)
    bufs["EPSC"] = nc.alloc_sbuf_tensor("epsc", [1, 1], F32)

    sq = P.c("sq")
    with nc.Block("init") as blk:
        @blk.sync
        def _(sync):
            for cn in ("IDENT", "IDENTB", "ONESB", "ONESROW", "MASKS", "SELH",
                       "SEL8"):
                sync.dma_start(bufs[cn][:], bufs[cn + "_dram"][:]
                               ).then_inc(sq.sem, 16)
            sync.dma_start(bufs["FBQ"][:], bufs["fbq_c"][:]).then_inc(sq.sem, 16)
            sync.dma_start(bufs["FBK"][:], bufs["fbk_c"][:]).then_inc(sq.sem, 16)
            sync.dma_start(bufs["FBO"][:], bufs["fbo_c"][:]).then_inc(sq.sem, 16)
            sq.v += 10 * 16
            sync.wait_ge(sq.sem, sq.v)

        @blk.vector
        def _(v):
            v.memset(bufs["EPSC"][:], EPS)

    rs_c = _ring_slots(True)
    rs_f = _ring_slots(False)

    def attention_scope(es, tag):
        bufs["VP"] = [es.enter_context(
            nc.sbuf_tensor(f"vpr{tag}{i}", [128, 16 * 65], BF16))
            for i in range(NT)]
        bufs["RING"] = [es.enter_context(
            nc.sbuf_tensor(f"ring{tag}{i}", [128, 1024], BF16))
            for i in range(NRING)]
        bufs["QP"] = [es.enter_context(
            nc.sbuf_tensor(f"qpair{tag}{i}", [128, T], BF16)) for i in range(2)]
        bufs["KP"] = [es.enter_context(
            nc.sbuf_tensor(f"kpair{tag}{i}", [128, T], BF16)) for i in range(2)]
        bufs["QSL"] = [es.enter_context(
            nc.sbuf_tensor(f"qsl{tag}{i}", [128, ND * 128], BF16))
            for i in range(2)]
        bufs["KSL"] = [es.enter_context(
            nc.sbuf_tensor(f"ksl{tag}{i}", [128, ND * 128], BF16))
            for i in range(2)]
        bufs["SQQ"] = es.enter_context(
            nc.sbuf_tensor(f"sqq{tag}", [128, T], BF16))
        bufs["SQK"] = bufs["SQQ"]
        bufs["DENS"] = es.enter_context(
            nc.sbuf_tensor(f"dens{tag}", [1, 4 * CH], F32))

    def ffn_scope(es, tag):
        bufs["VW"] = {x: es.enter_context(
            nc.sbuf_tensor(f"vw{tag}_{x}", [128, ND * 128], BF16))
            for x in "qkvo"}
        bufs["UP"] = {"o": es.enter_context(
            nc.sbuf_tensor(f"up{tag}_o", [128, D], BF16))}
        bufs["XVS"] = {"o": es.enter_context(
            nc.sbuf_tensor(f"xvs{tag}_o", [128, T], BF16))}

    def ffn_scope2(es, tag):
        for x in "qkv":
            bufs["UP"][x] = es.enter_context(
                nc.sbuf_tensor(f"up{tag}_{x}", [128, D], BF16))
            bufs["XVS"][x] = es.enter_context(
                nc.sbuf_tensor(f"xvs{tag}_{x}", [128, T], BF16))

    emit_ln(P, bufs, bufs["xT"], "a1")
    with ExitStack() as es:
        attention_scope(es, "a1")
        emit_vprime(P, bufs, "a1", "aWvT", ffn=False)
        emit_attention(P, bufs, "a1", "aWqT", "aWkT", causal=True, ffn=False,
                       ring_slot=rs_c)
    if nsub == 1:
        emit_oproj(P, bufs, "a1", "aWoT", "xT", "y", ffn=False, do_cn=True)
    else:
        emit_oproj(P, bufs, "a1", "aWoT", "xT", "Xs", ffn=False, do_cn=True,
                   ln_src=bufs["Xs"], ln_tag="f1")

    if nsub >= 2:
        with ExitStack() as es:
            ffn_scope(es, "f1")
            with ExitStack() as es2:
                ffn_scope2(es2, "f1")
                emit_gate(P, bufs, "f1")
                attention_scope(es2, "f1")
                emit_vprime(P, bufs, "f1", "fWvT", ffn=True)
                emit_attention(P, bufs, "f1", "fWqT", "fWkT", causal=False,
                               ffn=True, ring_slot=rs_f)
            if nsub == 2:
                emit_oproj(P, bufs, "f1", "fWoT", "Xs", "y", ffn=True,
                           do_cn=True)
            else:
                emit_oproj(P, bufs, "f1", "fWoT", "Xs", "Xs", ffn=True,
                           do_cn=True, ln_src=bufs["Xs"], ln_tag="b1")

    if nsub >= 3:
        emit_attnb(P, bufs)
        if nsub == 3:
            emit_oproj(P, bufs, "b1", "bWoT", "Xs", "y", ffn=False)
        else:
            emit_oproj(P, bufs, "b1", "bWoT", "Xs", "Xs", ffn=False,
                       ln_src=bufs["Xs"], ln_tag="f2")

    if nsub >= 4:
        with ExitStack() as es:
            ffn_scope(es, "f2")
            with ExitStack() as es2:
                ffn_scope2(es2, "f2")
                emit_gate(P, bufs, "f2")
                attention_scope(es2, "f2")
                emit_vprime(P, bufs, "f2", "fWvT", ffn=True)
                emit_attention(P, bufs, "f2", "fWqT", "fWkT", causal=False,
                               ffn=True, ring_slot=rs_f)
            emit_oproj(P, bufs, "f2", "fWoT", "Xs", "y", ffn=True, do_cn=True)

    with nc.Block("fin") as blk:
        @blk.sync
        def _(sync):
            for nm in ("sto0", "sto1", "sto2", "sto3"):
                if nm in P.ctr:
                    sync.wait_ge(P.ctr[nm].sem, P.ctr[nm].v)

    return nc


def host_inputs(inp):
    c = np.ascontiguousarray
    d = {k: np.asarray(v) for k, v in _mk_consts().items()}
    d["aWqT"] = c(np.asarray(inp["L0_attn_a_q_proj_weight"]).T)
    d["aWkT"] = c(np.asarray(inp["L0_attn_a_k_proj_weight"]).T)
    d["aWvT"] = c(np.asarray(inp["L0_attn_a_v_proj_weight"]).T)
    d["aWoT"] = c(np.asarray(inp["L0_attn_a_o_proj_weight"]).T)
    d["bWqT"] = c(np.asarray(inp["L0_attn_b_Wq"]).T)
    d["bWvT"] = c(np.asarray(inp["L0_attn_b_Wv"]).T)
    d["bWgT"] = c(np.asarray(inp["L0_attn_b_Wg"]).T)
    d["bWoT"] = c(np.asarray(inp["L0_attn_b_Wo"]).T)
    d["bbeta"] = c(np.asarray(inp["L0_attn_b_beta"]).reshape(1, 16))
    d["bgam"] = c(np.asarray(inp["L0_attn_b_ln_gamma"]).reshape(8, 128).T)
    d["bbet"] = c(np.asarray(inp["L0_attn_b_ln_beta"]).reshape(8, 128).T)
    for x in "qkvo":
        d[f"fW{x}T"] = c(np.asarray(inp[f"L0_ffn_{x}Wm"]).T)
        d[f"fV{x}"] = c(np.asarray(inp[f"L0_ffn_{x}V"]
                                   ).transpose(2, 0, 1).reshape(D, 128))
        d[f"fU{x}"] = c(np.asarray(inp[f"L0_ffn_{x}U"]
                                   ).transpose(0, 2, 1).reshape(128, D))
        d[f"fS{x}"] = c(np.asarray(inp[f"L0_ffn_{x}S"]).reshape(1, 128))
    d["fbq_c"] = c(np.asarray(inp["L0_ffn_qb"]).reshape(8, 128).T)
    d["fbk_c"] = c(np.asarray(inp["L0_ffn_kb"]).reshape(8, 128).T)
    d["fbo_c"] = c(np.asarray(inp["L0_ffn_ob"]).reshape(8, 128).T)
    d["fbv_r"] = c(np.asarray(inp["L0_ffn_vb"]).reshape(1, D))
    d["gw1T"] = c(np.asarray(inp["L0_ffn_gw1"]).T)
    d["gb1c"] = c(np.asarray(inp["L0_ffn_gb1"]).reshape(2, 128).T)
    d["gw2T"] = c(np.asarray(inp["L0_ffn_gw2"]).T)
    d["gb2r"] = c(np.asarray(inp["L0_ffn_gb2"]).reshape(1, 8))
    return d


def kernel(**inp):
    shared = host_inputs(inp)
    x = np.asarray(inp["L0_attn_a_x"])
    nc = build_program(nsub=4)
    in_maps = []
    for b in range(8):
        m = dict(shared)
        m["xT"] = np.ascontiguousarray(x[b].T)
        in_maps.append(m)
    res = run_bass_kernel_spmd(nc, in_maps, core_ids=list(range(8)))
    out = np.stack([np.asarray(r["y"]).T for r in res.results], axis=0)
    return np.ascontiguousarray(out.astype(np.float32))
